# version 19
# baseline (speedup 1.0000x reference)
import heapq
import math
import numpy as np
import ml_dtypes
import concourse.bass as bass
import concourse.mybir as mybir
from concourse.bass_utils import run_bass_kernel_spmd
from concourse.aot_env import allow_ici_all_to_all
from contextlib import ExitStack

F32 = mybir.dt.float32
BF16 = mybir.dt.bfloat16
I32 = mybir.dt.int32
I16 = mybir.dt.int16
AF = mybir.ActivationFunctionType
ALU = mybir.AluOpType
AX = mybir.AxisListType.X

NC = 8
B, S, E, H, D = 4, 4096, 1024, 16, 64
TOK = B * S
OWN = TOK // NC
HR = OWN + 8
NTIL = 17
HRP = NTIL * 128
EPS_LN = 1e-5
EPS_RMS = 1e-6
SCALING = float(E) ** -0.5

DEBUG_STAGE = 0



def _decay(h):
    return math.log(1.0 - 2.0 ** (-5.0 - h))


def _host_tables(core):
    t = {}
    d0, d1 = _decay(2 * core), _decay(2 * core + 1)

    ang = 1.0 / 10000.0 ** np.linspace(0.0, 1.0, D // 2)
    angle = np.repeat(ang, 2)
    idx = np.arange(S, dtype=np.float64)
    sinT = np.sin(idx[:, None] * angle[None, :]).T
    cosT = np.cos(idx[:, None] * angle[None, :]).T

    kk = np.arange(S, dtype=np.float64)
    kloc = kk - 128.0 * np.floor(kk / 128.0)
    pat0 = np.exp(-d0 * kloc)
    pat1 = np.exp(-d1 * kloc)
    t["tk_cos"] = (np.concatenate([cosT * pat0, cosT * pat1], 0) * SCALING).astype(ml_dtypes.bfloat16)
    t["tk_sin"] = (np.concatenate([sinT * pat0, sinT * pat1], 0) * SCALING).astype(ml_dtypes.bfloat16)
    t["tq_cos"] = np.concatenate([cosT, cosT], 0).astype(ml_dtypes.bfloat16)
    t["tq_sin"] = np.concatenate([sinT, sinT], 0).astype(ml_dtypes.bfloat16)

    bt = np.zeros((2, 8, 32), dtype=np.float64)
    for hh, dd in ((0, d0), (1, d1)):
        for qt in range(8):
            for kt in range(4 * qt + 4):
                bt[hh, qt, kt] = np.exp(dd * (512.0 * qt - 128.0 * kt))
    t["btab"] = np.broadcast_to(bt.reshape(1, 512), (128, 512)).astype(np.float32).copy()

    q = np.arange(S, dtype=np.float64)
    qt0 = 512.0 * np.floor(q / 512.0)
    grow = np.zeros((1, 2 * S), dtype=np.float64)
    for hh, dd in ((0, d0), (1, d1)):
        r = np.exp(dd)
        rowsum = (1.0 - r ** (q + 1.0)) / (1.0 - r)
        val = (np.exp(dd * (q - qt0)) / np.sqrt(rowsum)).reshape(8, 512)
        for qt in range(8):
            grow[0, 1024 * qt + 512 * hh:1024 * qt + 512 * hh + 512] = val[qt]
    t["grow"] = grow.astype(np.float32)

    tril = np.zeros((128, 4, 512), dtype=np.float32)
    kp = np.arange(128)[:, None]
    qf = np.arange(512)[None, :]
    for ri, rel in enumerate((0, 128, 256, 384)):
        tril[:, ri, :] = (kp + rel <= qf).astype(np.float32)
    t["trilt"] = np.ascontiguousarray(tril.reshape(128, 2048))

    t["maskL"] = np.full((128, 1), 0.0 if core % 2 == 0 else 1.0, np.float32)
    t["maskR"] = np.full((128, 1), 0.0 if core % 2 == 1 else 1.0, np.float32)
    return t


def _stage_inputs(core, xf, Wq, Wk, Wv, Wg, w1, conv_w, conv_b, w2):
    m = dict(_host_tables(core))
    lo = OWN * core
    rows = np.clip(np.arange(-1, HRP - 1) + lo, 0, TOK - 1)
    m["xh"] = np.ascontiguousarray(xf[rows])
    cols = slice(128 * core, 128 * core + 128)
    m["wq"] = np.ascontiguousarray(Wq[:, cols])
    m["wk"] = np.ascontiguousarray(Wk[:, cols])
    m["wvg"] = np.ascontiguousarray(np.concatenate([Wv[:, cols], Wg[:, cols]], axis=1))
    m["w1f"] = w1
    m["w2f"] = w2
    m["convw"] = np.ascontiguousarray(conv_w.reshape(32, 128, 3).transpose(1, 0, 2).reshape(128, 96))
    m["convb"] = np.ascontiguousarray(conv_b.reshape(32, 128).T)
    return m



class Sem:
    def __init__(self, ctx, nc, name):
        self.h = ctx.enter_context(nc.semaphore(name))
        self.n = 0
        self.name = name


class Prog:
    ENGINES = ("sync", "gpsimd", "tensor", "vector", "scalar")

    def __init__(self, markers=None):
        self.steps = {e: [] for e in self.ENGINES}
        self.M = dict(markers or {})

    def on(self, eng, fn):
        self.steps[eng].append(fn)

    def wait(self, eng, sem, key_or_val):
        M = self.M
        if isinstance(key_or_val, int):
            v = key_or_val
            self.on(eng, lambda e: e.wait_ge(sem.h, v))
        else:
            k = key_or_val
            self.on(eng, lambda e: e.wait_ge(sem.h, M[k]))

    def mark(self, key, sem):
        self.M[key] = sem.n

    def emit(self, block):
        for eng in self.ENGINES:
            steps = self.steps[eng]
            if not steps:
                continue

            def body(e, steps=steps):
                for fn in steps:
                    fn(e)

            getattr(block, eng)(body)


def _sl3(t, k, a, b):
    ap_ = t[:, k, a:b]
    if len(ap_.shape) == 3:
        ap_ = ap_.squeeze(1)
    return ap_


def _tile_end2(qt):
    return 4 * qt * qt + 12 * qt + 7


def _tile_end(h, qt):
    n = 0
    for hh in range(2):
        for qq in range(8):
            cnt = 4 * qq + 4
            if hh == h and qq == qt:
                return n + cnt - 1
            n += cnt
    raise AssertionError


def build_program():
    import os as _osx
    SKIP_AG = bool(_osx.environ.get("SKIP_AG")) and DEBUG_STAGE == 19
    SKIP_WQ = bool(_osx.environ.get("SKIP_WQ")) and DEBUG_STAGE == 19
    LN_TILES = int(_osx.environ.get("LN_TILES", "16")) if DEBUG_STAGE == 19 else 16
    SKIP_TR = bool(_osx.environ.get("SKIP_TR")) and DEBUG_STAGE == 19
    nc = bass.Bass("TRN2", target_bir_lowering=False)

    ap = lambda name, shape, dt, kind: nc.dram_tensor(name, shape, dt, kind=kind).ap()
    xh = ap("xh", [HRP, E], F32, "ExternalInput")
    wq = ap("wq", [E, 128], F32, "ExternalInput")
    wk = ap("wk", [E, 128], F32, "ExternalInput")
    wvg = ap("wvg", [E, 256], F32, "ExternalInput")
    w1f = ap("w1f", [4096, E], F32, "ExternalInput")
    w2f = ap("w2f", [E, 4096], F32, "ExternalInput")
    convw = ap("convw", [128, 96], F32, "ExternalInput")
    convb = ap("convb", [128, 32], F32, "ExternalInput")
    tq_cos = ap("tq_cos", [128, S], BF16, "ExternalInput")
    tq_sin = ap("tq_sin", [128, S], BF16, "ExternalInput")
    tk_cos = ap("tk_cos", [128, S], BF16, "ExternalInput")
    tk_sin = ap("tk_sin", [128, S], BF16, "ExternalInput")
    btab = ap("btab", [128, 512], F32, "ExternalInput")
    grow = ap("grow", [1, 2 * S], F32, "ExternalInput")
    trilt = ap("trilt", [128, 2048], F32, "ExternalInput")
    maskL = ap("maskL", [128, 1], F32, "ExternalInput")
    maskR = ap("maskR", [128, 1], F32, "ExternalInput")
    out = ap("o", [OWN, E], F32, "ExternalOutput")
    dbg = None
    if DEBUG_STAGE == 1:
        dbg = ap("dbg", [NC * E, OWN], BF16, "ExternalOutput")
    elif DEBUG_STAGE == 2:
        dbg = ap("dbg", [NC * HR, 128], BF16, "ExternalOutput")
    elif DEBUG_STAGE in (19, 20, 21, 22, 23):
        dbg = ap("dbg", [128, 8192], BF16, "ExternalOutput")
    elif DEBUG_STAGE == 24:
        dbg = ap("dbg", [128, 8704], F32, "ExternalOutput")

    ag1in = nc.dram_tensor("ag1in", [E, OWN], BF16).ap()
    ag1out = nc.dram_tensor("ag1out", [NC * E, OWN], BF16, addr_space="Shared").ap()
    a2ain = nc.dram_tensor("a2ain", [NC * HR, 128], BF16).ap()
    a2aout = nc.dram_tensor("a2aout", [NC * HR, 128], BF16).ap()
    w1qT_d = nc.dram_tensor("w1qT_d", [E, 4096], BF16).ap()
    w2qT_d = nc.dram_tensor("w2qT_d", [4096, E], BF16).ap()
    x1_d = nc.dram_tensor("x1_d", [HRP, E], F32).ap()

    ident_bf = nc.inline_tensor(np.eye(128, dtype=ml_dtypes.bfloat16), "ident_bf").ap()
    ident_f = nc.inline_tensor(np.eye(128, dtype=np.float32), "ident_f").ap()
    r2m = np.zeros((128, 128), dtype=np.float64)
    for j in range(64):
        r2m[2 * j, 2 * j + 1] = -1.0
        r2m[2 * j + 1, 2 * j] = 1.0
    r2t = nc.inline_tensor(np.ascontiguousarray(r2m.T).astype(ml_dtypes.bfloat16), "r2t").ap()
    ones_bf = nc.inline_tensor(np.ones((128, 1), dtype=ml_dtypes.bfloat16), "ones_bf").ap()
    onesrow_f = nc.inline_tensor(np.ones((1, 128), dtype=np.float32), "onesrow_f").ap()
    allones_f = nc.inline_tensor(np.ones((128, 128), dtype=np.float32), "allones_f").ap()
    onescell_f = nc.inline_tensor(np.ones((1, 1), dtype=np.float32), "onescell_f").ap()
    CPOOL_VALS = (EPS_LN, -EPS_LN, EPS_RMS)
    cpool_f = nc.inline_tensor(
        np.broadcast_to(np.array(CPOOL_VALS, np.float32), (128, 3)).copy(), "cpool_f").ap()

    CTX = ExitStack()
    E_ = CTX.enter_context
    with CTX:
        sSY = Sem(CTX, nc, "sSY")
        sGP = Sem(CTX, nc, "sGP")
        sCC = Sem(CTX, nc, "sCC")
        sPE = Sem(CTX, nc, "sPE")
        sDV = Sem(CTX, nc, "sDV")
        sSC = Sem(CTX, nc, "sSC")
        sX0 = Sem(CTX, nc, "sX0")
        sX1 = Sem(CTX, nc, "sX1")
        sW0 = Sem(CTX, nc, "sW0")
        sW1 = Sem(CTX, nc, "sW1")
        sS0 = Sem(CTX, nc, "sS0")
        sS1 = Sem(CTX, nc, "sS1")
        sO0 = Sem(CTX, nc, "sO0")
        sO1 = Sem(CTX, nc, "sO1")
        sGC = Sem(CTX, nc, "sGC")
        XSL = (sX0, sX1)
        WSL = (sW0, sW1)
        SSL = (sS0, sS1)
        OSL = (sO0, sO1)

        identb_t = E_(nc.sbuf_tensor("identb_t", [128, 128], BF16))
        identf_t = E_(nc.sbuf_tensor("identf_t", [128, 128], F32))
        r2t_t = E_(nc.sbuf_tensor("r2t_t", [128, 128], BF16))
        onesb_t = E_(nc.sbuf_tensor("onesb_t", [128, 1], BF16))
        onesr_t = E_(nc.sbuf_tensor("onesr_t", [1, 128], F32))
        allones_t = E_(nc.sbuf_tensor("allones_t", [128, 128], F32))
        onescell_t = E_(nc.sbuf_tensor("onescell_t", [1, 1], F32))
        cA = E_(nc.sbuf_tensor("cA", [128, 32], F32))
        cB = E_(nc.sbuf_tensor("cB", [128, 32], F32))
        m2w1c = E_(nc.sbuf_tensor("m2w1c", [128, 1], F32))
        m2w2c = E_(nc.sbuf_tensor("m2w2c", [128, 1], F32))
        convw_t = E_(nc.sbuf_tensor("convw_t", [128, 96], F32))
        convb_t = E_(nc.sbuf_tensor("convb_t", [128, 32], F32))
        tapsp = E_(nc.sbuf_tensor("tapsp", [128, 96], F32))
        maskL_t = E_(nc.sbuf_tensor("maskL_t", [128, 1], F32))
        maskR_t = E_(nc.sbuf_tensor("maskR_t", [128, 1], F32))
        cpool_t = E_(nc.sbuf_tensor("cpool_t", [128, 3], F32))
        for vi, vv in enumerate(CPOOL_VALS):
            nc.const_aps.register(F32, float(vv), cpool_t[:, vi:vi + 1])

        with ExitStack() as c1:
            e1 = c1.enter_context
            xa = e1(nc.sbuf_tensor("xa", [128, 2, E], F32))
            xc_t = e1(nc.sbuf_tensor("xc_t", [128, 2, E], F32))
            sqsc = e1(nc.sbuf_tensor("sqsc", [128, E], F32))
            xlnbf = e1(nc.sbuf_tensor("xlnbf", [128, 2, E], BF16))
            xlnTo = e1(nc.sbuf_tensor("xlnTo", [128, 8, OWN], BF16))
            wtile = e1(nc.sbuf_tensor("wtile", [128, 2, 4096], F32))
            wsq = e1(nc.sbuf_tensor("wsq", [128, 4096], F32))
            wint = e1(nc.sbuf_tensor("wint", [128, 4096], F32))
            wqb = e1(nc.sbuf_tensor("wqb", [128, 4096], BF16))
            wtT = e1(nc.sbuf_tensor("wtT", [128, 4096], BF16))
            wabs = e1(nc.sbuf_tensor("wabs", [128, 40], F32))
            ptrA = e1(nc.psum_tensor("ptrA", [128, 1024], BF16))
            ptrB = e1(nc.psum_tensor("ptrB", [128, 1024], BF16))
            pred = e1(nc.psum_tensor("pred", [128, 2], F32))
            PTR = (ptrA, ptrB)

            P = Prog()

            def cdma(dst, src):
                def f(e):
                    e.dma_start(out=dst, in_=src).then_inc(sSY.h, 16)
                return f

            for dst, src in ((identb_t[:], ident_bf[:]), (identf_t[:], ident_f[:]),
                             (r2t_t[:], r2t[:]), (onesb_t[:], ones_bf[:]),
                             (onesr_t[:], onesrow_f[:]), (allones_t[:], allones_f[:]),
                             (onescell_t[:], onescell_f[:]),
                             (convw_t[:], convw[:]), (convb_t[:], convb[:]),
                             (maskL_t[:], maskL[:]), (maskR_t[:], maskR[:]),
                             (cpool_t[:], cpool_f[:])):
                P.on("sync", cdma(dst, src))
                sSY.n += 16
            P.mark("const", sSY)

            for t in range(LN_TILES):
                sl = XSL[t % 2]
                if t >= 2:
                    P.wait("sync", sDV, ("lnxc", t - 2))

                def xload(e, t=t):
                    e.dma_start(out=_sl3(xa, t % 2, 0, E),
                                in_=xh[1 + 128 * t:129 + 128 * t, :]).then_inc(XSL[t % 2].h, 16)
                P.on("sync", xload)
                sl.n += 16

                P.wait("vector", sl, sl.n)
                co = 8 * (t % 2)

                def v1(e, t=t, co=co):
                    e.reduce_sum(cB[:, co:co + 1], _sl3(xa, t % 2, 0, E), axis=AX).then_inc(sDV.h, 1)
                P.on("vector", v1)
                sDV.n += 1
                P.mark(("lnsum", t), sDV)

                if t == 0:
                    P.wait("scalar", sSY, "const")
                P.wait("scalar", sDV, ("lnsum", t))

                def s1(e, co=co):
                    e.mul(out=cA[:, co:co + 1], in_=cB[:, co:co + 1], mul=1.0 / E).then_inc(sSC.h, 1)
                P.on("scalar", s1)
                sSC.n += 1
                P.mark(("lnm", t), sSC)

                P.wait("vector", sSC, ("lnm", t))
                if t >= 2:
                    P.wait("vector", sSC, ("lnout", t - 2))

                def v2(e, t=t, co=co):
                    e.tensor_scalar(out=_sl3(xc_t, t % 2, 0, E), in0=_sl3(xa, t % 2, 0, E),
                                    scalar1=cA[:, co:co + 1], scalar2=None,
                                    op0=ALU.subtract).then_inc(sDV.h, 1)
                P.on("vector", v2)
                sDV.n += 1
                P.mark(("lnxc", t), sDV)

                P.wait("scalar", sDV, ("lnxc", t))

                acc_n = sSC.n + 1

                def s2(e, t=t, co=co, acc_n=acc_n):
                    e.activation(out=sqsc[:], in_=_sl3(xc_t, t % 2, 0, E), func=AF.Square,
                                 accum_out=cA[:, co + 1:co + 2]).then_inc(sSC.h, 1)
                    e.wait_ge(sSC.h, acc_n)
                    e.activation(out=cA[:, co + 2:co + 3], in_=cA[:, co + 1:co + 2], func=AF.Sqrt,
                                 scale=1.0 / E, bias=EPS_LN).then_inc(sSC.h, 1)
                P.on("scalar", s2)
                sSC.n += 2
                P.mark(("lnsrt", t), sSC)

                P.wait("vector", sSC, ("lnsrt", t))

                def v3(e, co=co):
                    e.reciprocal(cB[:, co + 2:co + 3], cA[:, co + 2:co + 3]).then_inc(sDV.h, 1)
                P.on("vector", v3)
                sDV.n += 1
                P.mark(("lnrstd", t), sDV)

                P.wait("scalar", sDV, ("lnrstd", t))
                if t >= 2:
                    P.wait("scalar", sPE, ("lntr", t - 2))

                def s3(e, t=t, co=co):
                    e.activation(out=_sl3(xlnbf, t % 2, 0, E), in_=_sl3(xc_t, t % 2, 0, E),
                                 func=AF.Copy, scale=cB[:, co + 2:co + 3]).then_inc(sSC.h, 1)
                P.on("scalar", s3)
                sSC.n += 1
                P.mark(("lnout", t), sSC)

                if SKIP_TR:
                    P.M[("lntr", t)] = sPE.n
                    P.M[("lnband", t)] = sDV.n
                    continue
                P.wait("tensor", sSC, ("lnout", t))
                P.wait("tensor", sSY, "const")
                if t >= 2:
                    P.wait("tensor", sDV, ("lnband", t - 2))

                def te1(e, t=t):
                    pb = PTR[t % 2]
                    for ec in range(8):
                        d = e.transpose(pb[:, 128 * ec:128 * (ec + 1)],
                                        xlnbf[:, t % 2, 128 * ec:128 * (ec + 1)], identb_t[:])
                    d.then_inc(sPE.h, 1)
                P.on("tensor", te1)
                sPE.n += 1
                P.mark(("lntr", t), sPE)

                P.wait("vector", sPE, ("lntr", t))

                def v4(e, t=t):
                    pb = PTR[t % 2]
                    e.tensor_copy(xlnTo[:, :, 128 * t:128 * (t + 1)],
                                  pb[:].rearrange("p (ec c) -> p ec c", c=128)).then_inc(sDV.h, 1)
                P.on("vector", v4)
                sDV.n += 1
                P.mark(("lnband", t), sDV)

            if LN_TILES > 0:
                P.wait("sync", sDV, ("lnband", LN_TILES - 1))

            def agw(e):
                for ec in range(8):
                    e.dma_start(out=ag1in[128 * ec:128 * (ec + 1), :],
                                in_=xlnTo[:, ec, :]).then_inc(sSY.h, 16)
            if not SKIP_AG:
                P.on("sync", agw)
                sSY.n += 8 * 16
            P.mark("ag1w", sSY)

            if not SKIP_AG:
                P.wait("gpsimd", sSY, "ag1w")

            def ag1(e):
                e.collective_compute("AllGather", ALU.bypass,
                                     replica_groups=[list(range(NC))],
                                     ins=[ag1in[:].opt()], outs=[ag1out[:].opt()]).then_inc(sCC.h, 1)
            if not SKIP_AG:
                P.on("gpsimd", ag1)
                sCC.n += 1
            P.mark("ag1", sCC)

            for i in ([] if SKIP_WQ else range(40)):
                sl = WSL[i % 2]
                if i >= 2:
                    P.wait("sync", sDV, ("wabs", i - 2))
                wd = 1024 if i < 32 else 4096

                def wload(e, i=i, wd=wd):
                    src = w1f[128 * i:128 * (i + 1), :] if i < 32 else w2f[128 * (i - 32):128 * (i - 31), :]
                    e.dma_start(out=_sl3(wtile, i % 2, 0, wd), in_=src).then_inc(WSL[i % 2].h, 16)
                P.on("sync", wload)
                sl.n += 16
                P.wait("vector", sl, sl.n)

                def wabs_f(e, i=i, wd=wd):
                    e.reduce_sum(wabs[:, i:i + 1], _sl3(wtile, i % 2, 0, wd), axis=AX,
                                 apply_absolute_value=True).then_inc(sDV.h, 1)
                P.on("vector", wabs_f)
                sDV.n += 1
                P.mark(("wabs", i), sDV)

            if not SKIP_WQ:
                P.wait("vector", sDV, ("wabs", 39))

                def wsums(e):
                    e.reduce_sum(cB[:, 16:17], wabs[:, 0:32], axis=AX)
                    e.reduce_sum(cB[:, 17:18], wabs[:, 32:40], axis=AX).then_inc(sDV.h, 1)
                P.on("vector", wsums)
                sDV.n += 1
                P.mark("wsum", sDV)

                P.wait("tensor", sDV, "wsum")

                def wtot(e):
                    e.matmul(pred[:], allones_t[:], cB[:, 16:18], start=True, stop=True).then_inc(sPE.h, 1)
                P.on("tensor", wtot)
                sPE.n += 1
                P.mark("wtot", sPE)

                P.wait("scalar", sPE, "wtot")

                def wm1(e):
                    e.mul(out=cA[:, 16:17], in_=pred[:, 0:1], mul=1.0 / (4096.0 * 1024.0))
                    e.mul(out=cA[:, 17:18], in_=pred[:, 1:2], mul=1.0 / (1024.0 * 4096.0)).then_inc(sSC.h, 1)
                P.on("scalar", wm1)
                sSC.n += 1
                P.mark("wm1", sSC)

                P.wait("vector", sSC, "wm1")

                def wm2(e):
                    e.tensor_scalar_max(m2w1c[:], cA[:, 16:17], 1e-5)
                    e.tensor_scalar_max(m2w2c[:], cA[:, 17:18], 1e-5).then_inc(sDV.h, 1)
                P.on("vector", wm2)
                sDV.n += 1
                P.mark("wm2", sDV)

                P.wait("scalar", sDV, "wm2")

                def wm2s(e):
                    e.copy(out=cA[:, 18:19], in_=m2w1c[:])
                    e.copy(out=cA[:, 19:20], in_=m2w2c[:]).then_inc(sSC.h, 1)
                P.on("scalar", wm2s)
                sSC.n += 1
                P.mark("wm2s", sSC)

                P.wait("vector", sSC, "wm2s")

                def wrec(e):
                    e.reciprocal(cB[:, 18:19], cA[:, 18:19])
                    e.reciprocal(cB[:, 19:20], cA[:, 19:20]).then_inc(sDV.h, 1)
                P.on("vector", wrec)
                sDV.n += 1
                P.mark("wrec", sDV)

                P.wait("scalar", sDV, "wm2")
                P.wait("scalar", sSY, "const")

                def tap(e):
                    e.mul(out=tapsp[:], in_=convw_t[:], mul=m2w1c[:]).then_inc(sSC.h, 1)
                P.on("scalar", tap)
                sSC.n += 1
                P.mark("taps", sSC)

                band = 0
                for i in range(40):
                    sl = WSL[i % 2]
                    wd = 1024 if i < 32 else 4096
                    nb = wd // 1024
                    if i >= 2:
                        P.wait("sync", sSC, ("wsf", i - 2))
                    else:
                        P.wait("sync", sDV, ("wabs", 38 + i))

                    def wload2(e, i=i, wd=wd):
                        src = w1f[128 * i:128 * (i + 1), :] if i < 32 else w2f[128 * (i - 32):128 * (i - 31), :]
                        e.dma_start(out=_sl3(wtile, i % 2, 0, wd), in_=src).then_inc(WSL[i % 2].h, 16)
                    P.on("sync", wload2)
                    sl.n += 16

                    P.wait("scalar", sl, sl.n)
                    P.wait("scalar", sDV, "wrec")
                    if i >= 1:
                        P.wait("scalar", sDV, ("wint", i - 1))

                    def wsf(e, i=i, wd=wd):
                        e.activation(out=wsq[:, 0:wd], in_=_sl3(wtile, i % 2, 0, wd), func=AF.Copy,
                                     scale=cB[:, 18:19] if i < 32 else cB[:, 19:20]).then_inc(sSC.h, 1)
                    P.on("scalar", wsf)
                    sSC.n += 1
                    P.mark(("wsf", i), sSC)

                    P.wait("vector", sSC, ("wsf", i))

                    acc_n = sDV.n + 1

                    def wcv(e, wd=wd, acc_n=acc_n):
                        e.tensor_scalar(out=wint[:, 0:wd], in0=wsq[:, 0:wd],
                                        scalar1=0.5, scalar2=1.0,
                                        op0=ALU.is_gt, op1=ALU.subtract).then_inc(sDV.h, 1)
                        e.wait_ge(sDV.h, acc_n)
                        e.scalar_tensor_tensor(out=wqb[:, 0:wd], in0=wsq[:, 0:wd],
                                               scalar=-0.5, in1=wint[:, 0:wd],
                                               op0=ALU.is_ge, op1=ALU.add).then_inc(sDV.h, 1)
                    P.on("vector", wcv)
                    sDV.n += 2
                    P.mark(("wint", i), sDV)

                    for bb in range(nb):
                        P.wait("tensor", sDV, ("wint", i))
                        if band < 2:
                            P.wait("tensor", sDV, ("lnband", 15))
                        else:
                            P.wait("tensor", sDV, ("wtb", band - 2))

                        def wtr(e, bb=bb, band=band):
                            pb = PTR[band % 2]
                            for k in range(8):
                                d = e.transpose(pb[:, 128 * k:128 * (k + 1)],
                                                wqb[:, 1024 * bb + 128 * k:1024 * bb + 128 * (k + 1)],
                                                identb_t[:])
                            d.then_inc(sPE.h, 1)
                        P.on("tensor", wtr)
                        sPE.n += 1
                        P.mark(("wtr", band), sPE)

                        P.wait("vector", sPE, ("wtr", band))
                        if i >= 1 and bb == 0:
                            P.wait("vector", sSY, ("wst", i - 1))

                        def wtb(e, bb=bb, band=band):
                            pb = PTR[band % 2]
                            e.tensor_copy(wtT[:, 1024 * bb:1024 * (bb + 1)],
                                          pb[:].rearrange("p (k c) -> p k c", c=128)).then_inc(sDV.h, 1)
                        P.on("vector", wtb)
                        sDV.n += 1
                        P.mark(("wtb", band), sDV)
                        band += 1

                    P.wait("sync", sDV, ("wtb", band - 1))

                    def wst(e, i=i):
                        if i < 32:
                            d = e.dma_start(
                                out=w1qT_d[:, 128 * i:128 * (i + 1)].rearrange("(ec p) c -> p ec c", p=128),
                                in_=wtT[:, 0:1024].rearrange("p (ec c) -> p ec c", c=128))
                        else:
                            jj = i - 32
                            d = e.dma_start(
                                out=w2qT_d[:, 128 * jj:128 * (jj + 1)].rearrange("(ch p) c -> p ch c", p=128),
                                in_=wtT[:, 0:4096].rearrange("p (ch c) -> p ch c", c=128))
                        d.then_inc(sSY.h, 16)
                    P.on("sync", wst)
                    sSY.n += 16
                    P.mark(("wst", i), sSY)

            if DEBUG_STAGE == 24:
                P.wait("sync", sDV, "wrec")
                P.wait("sync", sSC, ("wsf", 39))
                P.wait("sync", sDV, ("wint", 39))
                base24 = sSY.n

                def dump24(e, base24=base24):
                    e.dma_start(out=dbg[:, 0:40], in_=wabs[:]).then_inc(sSY.h, 16)
                    e.dma_start(out=dbg[:, 56:60], in_=cA[:, 16:20]).then_inc(sSY.h, 16)
                    e.dma_start(out=dbg[:, 88:92], in_=cB[:, 16:20]).then_inc(sSY.h, 16)
                    e.dma_start(out=dbg[:, 104:4200], in_=wsq[:]).then_inc(sSY.h, 16)
                    e.wait_ge(sSY.h, base24 + 64)
                P.on("sync", dump24)
                sSY.n += 64

                P.wait("gpsimd", sSC, ("wsf", 39))
                P.wait("gpsimd", sDV, ("wint", 39))
                base24g = sGP.n

                def dump24g(e, base24g=base24g):
                    e.dma_start(out=dbg[:, 4200:8296], in_=wqb[:]).then_inc(sGP.h, 16)
                    e.wait_ge(sGP.h, base24g + 16)
                P.on("gpsimd", dump24g)
                sGP.n += 16

            if DEBUG_STAGE == 1:
                P.wait("sync", sCC, "ag1")
                base0 = sSY.n

                def dump1(e, base0=base0):
                    n = base0
                    for r in range(NC):
                        e.dma_start(out=xlnTo[:],
                                    in_=ag1out[E * r:E * (r + 1), :].rearrange(
                                        "(ec p) c -> p ec c", p=128)).then_inc(sSY.h, 16)
                        n += 16
                        e.wait_ge(sSY.h, n)
                        e.dma_start(out=dbg[E * r:E * (r + 1), :].rearrange(
                                        "(ec p) c -> p ec c", p=128),
                                    in_=xlnTo[:]).then_inc(sSY.h, 16)
                        n += 16
                        e.wait_ge(sSY.h, n)
                P.on("sync", dump1)
                sSY.n += 2 * NC * 16

            for s_ in (sSY, sGP, sCC, sPE, sDV, sSC, sX0, sX1, sW0, sW1, sS0, sS1, sO0, sO1):
                P.mark(("b1", s_.name), s_)
            blk1 = e1(nc.Block("blk1"))
            P.emit(blk1)
            M1 = dict(P.M)

        if DEBUG_STAGE in (1, 24):
            return nc

        with ExitStack() as c2:
            e2 = c2.enter_context
            import os as _os2
            if DEBUG_STAGE == 19 and _os2.environ.get("SKIP_B2ALLOC"):
                P = Prog(M1)
                if sCC.n:
                    P.wait("sync", sCC, "ag1")

                n019 = sSY.n

                def dump19(e, n019=n019):
                    e.dma_start(out=dbg[:, 0:128], in_=identb_t[:]).then_inc(sSY.h, 16)
                    e.wait_ge(sSY.h, n019 + 16)
                P.on("sync", dump19)
                sSY.n += 16
                blk2y = e2(nc.Block("blk2"))
                P.emit(blk2y)
                return nc
            xlnT = e2(nc.sbuf_tensor("xlnT", [128, 8, S], BF16))
            tqc = e2(nc.sbuf_tensor("tqc", [128, S], BF16))
            tqs = e2(nc.sbuf_tensor("tqs", [128, S], BF16))
            tkc = e2(nc.sbuf_tensor("tkc", [128, S], BF16))
            tks = e2(nc.sbuf_tensor("tks", [128, S], BF16))
            btab_t = e2(nc.sbuf_tensor("btab_t", [128, 512], F32))
            grow_t = e2(nc.sbuf_tensor("grow_t", [1, 2 * S], BF16))
            tril_t = e2(nc.sbuf_tensor("tril_t", [128, 2048], BF16))
            wq_sb = e2(nc.sbuf_tensor("wq_sb", [128, 8, 128], BF16))
            wk_sb = e2(nc.sbuf_tensor("wk_sb", [128, 8, 128], BF16))
            wvg_sb = e2(nc.sbuf_tensor("wvg_sb", [128, 8, 256], BF16))
            qrT = e2(nc.sbuf_tensor("qrT", [128, S], BF16))
            krT = e2(nc.sbuf_tensor("krT", [128, S], BF16))
            vr = e2(nc.sbuf_tensor("vr", [128, 32, 128], BF16))
            gsil = e2(nc.sbuf_tensor("gsil", [128, 32, 128], BF16))
            tsinb = e2(nc.sbuf_tensor("tsinb", [128, 2, 512], BF16))
            tcosb = e2(nc.sbuf_tensor("tcosb", [128, 2, 512], BF16))
            ptb = e2(nc.sbuf_tensor("ptb", [128, 6, 512], BF16))
            pab = e2(nc.sbuf_tensor("pab", [128, 6, 512], BF16))
            pvsb0 = e2(nc.sbuf_tensor("pvsb0", [64, 512], BF16))
            pvsb1 = e2(nc.sbuf_tensor("pvsb1", [64, 512], BF16))
            densb = e2(nc.sbuf_tensor("densb", [1, 1024], F32))
            rsrow = e2(nc.sbuf_tensor("rsrow", [1, 1024], F32))
            o1b = e2(nc.sbuf_tensor("o1b", [128, 512], F32))
            sqep = e2(nc.sbuf_tensor("sqep", [128, 512], F32))
            o2b = e2(nc.sbuf_tensor("o2b", [128, 512], BF16))
            attno = e2(nc.sbuf_tensor("attno", [128, 32, 128], BF16))
            pQK0 = e2(nc.psum_tensor("pQK0", [128, 512], F32))
            pQK1 = e2(nc.psum_tensor("pQK1", [128, 512], F32))
            pPV0 = e2(nc.psum_tensor("pPV0", [64, 512], F32))
            pPV1 = e2(nc.psum_tensor("pPV1", [64, 512], F32))
            pDEN0 = e2(nc.psum_tensor("pDEN0", [1, 512], F32))
            pDEN1 = e2(nc.psum_tensor("pDEN1", [1, 512], F32))
            pROT = e2(nc.psum_tensor("pROT", [128, 512], F32))
            pEPI = e2(nc.psum_tensor("pEPI", [128, 512], BF16))
            PQK = (pQK0, pQK1)
            pPRJ0, pPRJ1 = pQK0, pQK1
            PPRJ = (pPRJ0, pPRJ1)
            PPVH = (pPV0, pPV1)
            PDENH = (pDEN0, pDEN1)

            P = Prog(M1)
            import os as _os
            if not _os.environ.get("SKIP_BARRIER"):
                for eng_ in Prog.ENGINES:
                    for s_ in (sSY, sGP, sPE, sDV, sSC, sX0, sX1, sW0, sW1, sS0, sS1, sO0, sO1):
                        P.wait(eng_, s_, ("b1", s_.name))

            def _attn_epilogue_stages(b, qt):
                e_ = 8 * b + qt
                tiles_g = ("pvden", 288 * b + _tile_end2(qt))

                def st_epev():
                    P.wait("scalar", sPE, tiles_g)
                    if e_ >= 1:
                        P.wait("scalar", sDV, ("eprs", e_ - 1))

                    def epev(e):
                        e.copy(out=pvsb0[:], in_=pPV0[:])
                        e.copy(out=pvsb1[:], in_=pPV1[:])
                        e.copy(out=densb[0:1, 0:512], in_=pDEN0[:])
                        e.copy(out=densb[0:1, 512:1024], in_=pDEN1[:]).then_inc(sSC.h, 1)
                    P.on("scalar", epev)
                    sSC.n += 1
                    P.mark(("epev", e_), sSC)

                def st_eprs():
                    P.wait("vector", sSC, ("epev", e_))
                    if e_ >= 1:
                        P.wait("vector", sPE, ("eptr", e_ - 1))

                    base_n = sDV.n

                    def eprs(e, qt=qt, base_n=base_n):
                        g0 = 1024 * qt
                        e.tensor_tensor(rsrow[:], densb[:], grow_t[0:1, g0:g0 + 1024],
                                        op=ALU.mult).then_inc(sDV.h, 1)
                        e.wait_ge(sDV.h, base_n + 1)
                        e.tensor_scalar(out=rsrow[:], in0=rsrow[:], scalar1=1.0, scalar2=50000.0,
                                        op0=ALU.max, op1=ALU.min).then_inc(sDV.h, 1)
                        e.wait_ge(sDV.h, base_n + 2)
                        e.reciprocal(rsrow[:], rsrow[:]).then_inc(sDV.h, 1)
                        e.wait_ge(sDV.h, base_n + 3)
                        e.tensor_tensor(rsrow[:], rsrow[:], grow_t[0:1, g0:g0 + 1024],
                                        op=ALU.mult).then_inc(sDV.h, 1)
                    P.on("vector", eprs)
                    sDV.n += 4
                    P.mark(("eprs", e_), sDV)

                def st_eptr():
                    P.wait("tensor", sDV, ("eprs", e_))
                    P.wait("tensor", sSC, ("epev", e_))
                    if e_ >= 1:
                        P.wait("tensor", sSC, ("epo1", e_ - 1))
                        P.wait("tensor", sDV, ("eprsc", e_ - 1))

                    def eptr(e):
                        for c in range(8):
                            qc, h = c // 2, c % 2
                            pvh = (pvsb0, pvsb1)[h]
                            e.transpose(pEPI[:, 64 * c:64 * (c + 1)],
                                        pvh[:, 128 * qc:128 * (qc + 1)],
                                        identb_t[0:64, 0:64])
                        for c in range(8):
                            qc, h = c // 2, c % 2
                            d = e.matmul(pROT[:, c:c + 1],
                                         rsrow[0:1, 512 * h + 128 * qc:512 * h + 128 * qc + 128],
                                         onescell_t[:], start=True, stop=True)
                        d.then_inc(sPE.h, 1)
                    P.on("tensor", eptr)
                    sPE.n += 1
                    P.mark(("eptr", e_), sPE)

                def st_eprsc():
                    P.wait("vector", sPE, ("eptr", e_))
                    if e_ >= 1:
                        P.wait("vector", sSC, ("epo1", e_ - 1))

                    def eprsc(e):
                        e.tensor_copy(cB[:, 16:24], pROT[:, 0:8]).then_inc(sDV.h, 1)
                    P.on("vector", eprsc)
                    sDV.n += 1
                    P.mark(("eprsc", e_), sDV)

                def st_epo1():
                    P.wait("scalar", sDV, ("eprsc", e_))
                    if e_ >= 1:
                        P.wait("scalar", sDV, ("eprms", e_ - 1))

                    acc_n = sSC.n

                    def epo1(e, acc_n=acc_n):
                        for c in range(8):
                            d = e.activation(out=o1b[:, 64 * c:64 * (c + 1)],
                                             in_=pEPI[:, 64 * c:64 * (c + 1)],
                                             func=AF.Copy, scale=cB[:, 16 + c:17 + c])
                        d.then_inc(sSC.h, 1)
                        e.wait_ge(sSC.h, acc_n + 1)
                        for c in range(8):
                            d = e.activation(out=sqep[:, 64 * c:64 * (c + 1)],
                                             in_=o1b[:, 64 * c:64 * (c + 1)], func=AF.Square,
                                             accum_out=cA[:, 16 + c:17 + c])
                        d.then_inc(sSC.h, 1)
                        e.wait_ge(sSC.h, acc_n + 2)
                        e.activation(out=cA[:, 24:32], in_=cA[:, 16:24], func=AF.Sqrt,
                                     scale=1.0 / 64.0, bias=EPS_RMS).then_inc(sSC.h, 1)
                    P.on("scalar", epo1)
                    sSC.n += 3
                    P.mark(("epo1", e_), sSC)

                def st_eprms():
                    P.wait("vector", sSC, ("epo1", e_))

                    def eprms(e):
                        e.reciprocal(cB[:, 24:32], cA[:, 24:32]).then_inc(sDV.h, 1)
                    P.on("vector", eprms)
                    sDV.n += 1
                    P.mark(("eprms", e_), sDV)

                def st_epo2():
                    P.wait("scalar", sDV, ("eprms", e_))
                    if e_ >= 1:
                        P.wait("scalar", sDV, ("epat", e_ - 1))

                    def epo2(e):
                        for c in range(8):
                            d = e.activation(out=o2b[:, 64 * c:64 * (c + 1)],
                                             in_=o1b[:, 64 * c:64 * (c + 1)],
                                             func=AF.Copy, scale=cB[:, 24 + c:25 + c])
                        d.then_inc(sSC.h, 1)
                    P.on("scalar", epo2)
                    sSC.n += 1
                    P.mark(("epo2", e_), sSC)

                def st_epat():
                    P.wait("vector", sSC, ("epo2", e_))
                    if b > 0 and e_ == 8 * b:
                        P.wait("vector", sGP, ("a2aw", b - 1))

                    def epat(e, qt=qt):
                        e.tensor_tensor(attno[:, 4 * qt:4 * qt + 4, :],
                                        o2b[:].rearrange("p (qc hd) -> p qc hd", hd=128),
                                        gsil[:, 4 * qt:4 * qt + 4, :],
                                        op=ALU.mult).then_inc(sDV.h, 1)
                    P.on("vector", epat)
                    sDV.n += 1
                    P.mark(("epat", e_), sDV)

                return [st_epev, st_eprs, st_eptr, st_eprsc,
                        st_epo1, st_eprms, st_epo2, st_epat]

            if DEBUG_STAGE == 19:
                if sCC.n:
                    P.wait("sync", sCC, "ag1")
                if not SKIP_WQ:
                    P.wait("sync", sSY, ("wst", 39))

                n019 = sSY.n

                def dump19(e, n019=n019):
                    e.dma_start(out=dbg[:, 0:128], in_=identb_t[:]).then_inc(sSY.h, 16)
                    e.wait_ge(sSY.h, n019 + 16)
                P.on("sync", dump19)
                sSY.n += 16
                for s_ in (sSY, sGP, sCC, sPE, sDV, sSC):
                    P.mark(("b2", s_.name), s_)
                blk2x = e2(nc.Block("blk2"))
                P.emit(blk2x)
                return nc

            def c2load(e):
                for dst, src in ((tqc[:], tq_cos[:]), (tqs[:], tq_sin[:]),
                                 (tkc[:], tk_cos[:]), (tks[:], tk_sin[:]),
                                 (btab_t[:], btab[:]), (grow_t[:], grow[:]),
                                 (tril_t[:], trilt[:])):
                    e.dma_start(out=dst, in_=src).then_inc(sGP.h, 16)
                e.dma_start(out=wq_sb[:], in_=wq[:].rearrange("(ec p) c -> p ec c", p=128)).then_inc(sGP.h, 16)
                e.dma_start(out=wk_sb[:], in_=wk[:].rearrange("(ec p) c -> p ec c", p=128)).then_inc(sGP.h, 16)
                e.dma_start(out=wvg_sb[:], in_=wvg[:].rearrange("(ec p) c -> p ec c", p=128)).then_inc(sGP.h, 16)
            P.on("gpsimd", c2load)
            sGP.n += 10 * 16
            P.mark("c2load", sGP)

            NB = 1 if DEBUG_STAGE in (19, 20, 21, 22, 23) else 4
            for b in range(NB):
                P.wait("sync", sCC, "ag1")
                if b > 0:
                    P.wait("sync", sPE, ("vgdone", b - 1))

                def xlnl(e, b=b):
                    for half in range(2):
                        for ec in range(8):
                            r0 = E * (2 * b + half) + 128 * ec
                            e.dma_start(out=xlnT[:, ec, OWN * half:OWN * (half + 1)],
                                        in_=ag1out[r0:r0 + 128, :]).then_inc(sSY.h, 16)
                P.on("sync", xlnl)
                sSY.n += 16 * 16
                P.mark(("xlnT", b), sSY)

                for part in (range(0) if DEBUG_STAGE == 20 else range(2)):
                    wsb = (wq_sb, wk_sb)[part]
                    tc = (tqc, tkc)[part]
                    tsn = (tqs, tks)[part]
                    dst = (qrT, krT)[part]
                    for ch in range(8):
                        gidx = 16 * b + 8 * part + ch
                        P.wait("tensor", sSY, ("xlnT", b))
                        P.wait("tensor", sGP, "c2load")
                        if gidx >= 2:
                            P.wait("tensor", sDV, ("rotcos", gidx - 2))
                        if b > 0 and gidx == 16 * b:
                            P.wait("tensor", sSC, ("gsl", 32 * b - 1))
                            P.wait("tensor", sSC, ("ptbm", 288 * b - 1))

                        def prj(e, ch=ch, wsb=wsb):
                            pp = PPRJ[ch % 2]
                            for ec in range(8):
                                d = e.matmul(pp[:], _sl3(wsb, ec, 0, 128),
                                             xlnT[:, ec, 512 * ch:512 * (ch + 1)],
                                             start=(ec == 0), stop=(ec == 7))
                            d.then_inc(sPE.h, 1)
                        P.on("tensor", prj)
                        sPE.n += 1
                        P.mark(("prj", gidx), sPE)

                        P.wait("vector", sPE, ("prj", gidx))
                        P.wait("vector", sGP, "c2load")
                        if gidx >= 2:
                            P.wait("vector", sPE, ("r2", gidx - 2))

                        def rot1(e, ch=ch, gidx=gidx, tc=tc, tsn=tsn):
                            pp = PPRJ[ch % 2]
                            e.tensor_tensor(_sl3(tsinb, gidx % 2, 0, 512), pp[:],
                                            tsn[:, 512 * ch:512 * (ch + 1)], op=ALU.mult)
                            e.tensor_tensor(_sl3(tcosb, gidx % 2, 0, 512), pp[:],
                                            tc[:, 512 * ch:512 * (ch + 1)],
                                            op=ALU.mult).then_inc(sDV.h, 1)
                        P.on("vector", rot1)
                        sDV.n += 1
                        P.mark(("rotcos", gidx), sDV)

                        P.wait("tensor", sDV, ("rotcos", gidx))
                        if gidx >= 1:
                            P.wait("tensor", sDV, ("rot", gidx - 1))

                        def r2mm(e, gidx=gidx):
                            e.matmul(pROT[:], r2t_t[:], _sl3(tsinb, gidx % 2, 0, 512),
                                     start=True, stop=True).then_inc(sPE.h, 1)
                        P.on("tensor", r2mm)
                        sPE.n += 1
                        P.mark(("r2", gidx), sPE)

                        P.wait("vector", sPE, ("r2", gidx))

                        def rot2(e, ch=ch, gidx=gidx, dst=dst):
                            e.tensor_tensor(dst[:, 512 * ch:512 * (ch + 1)], pROT[:],
                                            _sl3(tcosb, gidx % 2, 0, 512),
                                            op=ALU.add).then_inc(sDV.h, 1)
                        P.on("vector", rot2)
                        sDV.n += 1
                        P.mark(("rot", gidx), sDV)

                for tt in (range(0) if DEBUG_STAGE in (20, 21) else range(32)):
                    gv = 32 * b + tt
                    P.wait("tensor", sDV, ("rot", 16 * b + 15))
                    if tt >= 2:
                        P.wait("tensor", sDV, ("vrc", gv - 2))
                        P.wait("tensor", sSC, ("gsl", gv - 2))

                    def vgp(e, tt=tt):
                        pp = PPRJ[tt % 2]
                        for ec in range(8):
                            d = e.matmul(pp[:, 0:256], xlnT[:, ec, 128 * tt:128 * (tt + 1)],
                                         _sl3(wvg_sb, ec, 0, 256), start=(ec == 0), stop=(ec == 7))
                        d.then_inc(sPE.h, 1)
                    P.on("tensor", vgp)
                    sPE.n += 1
                    P.mark(("vg", gv), sPE)
                    if tt == 31:
                        P.mark(("vgdone", b), sPE)

                    P.wait("vector", sPE, ("vg", gv))

                    def vrc(e, tt=tt):
                        pp = PPRJ[tt % 2]
                        e.tensor_copy(vr[:, tt, :], pp[:, 0:128]).then_inc(sDV.h, 1)
                    P.on("vector", vrc)
                    sDV.n += 1
                    P.mark(("vrc", gv), sDV)

                    P.wait("scalar", sPE, ("vg", gv))
                    P.wait("scalar", sDV, ("vrc", gv))
                    if b > 0 and tt == 0:
                        P.wait("scalar", sDV, ("epat", 8 * b - 1))

                    def gsl(e, tt=tt):
                        pp = PPRJ[tt % 2]
                        e.activation(out=gsil[:, tt, :], in_=pp[:, 128:256],
                                     func=AF.Silu).then_inc(sSC.h, 1)
                    P.on("scalar", gsl)
                    sSC.n += 1
                    P.mark(("gsl", gv), sSC)

                if DEBUG_STAGE in (20, 21, 22):
                    continue
                tiles = []
                for qt in range(8):
                    for kt in range(4 * qt + 4):
                        for h in range(2):
                            tiles.append((h, qt, kt))
                G = len(tiles)
                assert G == 288

                def qk_step(e, h, qt, kt, par):
                    e.matmul(PQK[par][:], krT[64 * h:64 * (h + 1), 128 * kt:128 * (kt + 1)],
                             qrT[64 * h:64 * (h + 1), 512 * qt:512 * (qt + 1)],
                             start=True, stop=True).then_inc(sPE.h, 1)

                def pv_step(e, h, qt, kt, slot, first, last):
                    pt = _sl3(ptb, slot, 0, 512)
                    pa = _sl3(pab, slot, 0, 512)
                    e.matmul(PPVH[h][:], _sl3(vr, kt, 64 * h, 64 * h + 64), pt,
                             start=first, stop=last)
                    e.matmul(PDENH[h][:], onesb_t[:], pa,
                             start=first, stop=last).then_inc(sPE.h, 1)

                pend = []
                pend_seq = [0]

                def emit_pv(gi_prev, cur_gi, b=b):
                    hp, qp, kp_ = tiles[gi_prev]
                    g_prev = 288 * b + gi_prev
                    first = (kp_ == 0)
                    last = (kp_ == 4 * qp + 3)
                    P.wait("tensor", sDV, ("cast", g_prev))
                    if first and hp == 0:
                        egrp = 8 * b + qp
                        if egrp > 0:
                            P.wait("tensor", sSC, ("epev", egrp - 1))
                    P.on("tensor", lambda e, hp=hp, qp=qp, kp_=kp_, slot=g_prev % 6,
                         first=first, last=last: pv_step(e, hp, qp, kp_, slot, first, last))
                    sPE.n += 1
                    P.mark(("pvden", g_prev), sPE)
                    if last and hp == 1:
                        for k, st in enumerate(_attn_epilogue_stages(b, qp)):
                            heapq.heappush(pend, (cur_gi + k, pend_seq[0], st))
                            pend_seq[0] += 1

                def emit_qk(gi):
                    h, qt, kt = (tiles[gi][0], tiles[gi][1], tiles[gi][2])
                    g = 288 * b + gi
                    par = g % 2
                    P.wait("tensor", sDV, ("rot", 16 * b + 15))
                    if gi == 0:
                        P.wait("tensor", sDV, ("vrc", 32 * b + 31))
                        P.wait("tensor", sSC, ("gsl", 32 * b + 31))
                    if g >= 2:
                        P.wait("tensor", sSC, ("ptbm", g - 2))
                    P.on("tensor", lambda e, h=h, qt=qt, kt=kt, par=par: qk_step(e, h, qt, kt, par))
                    sPE.n += 1
                    P.mark(("qk", g), sPE)

                def emit_castchain(gi):
                    h, qt, kt = (tiles[gi][0], tiles[gi][1], tiles[gi][2])
                    g = 288 * b + gi
                    par = g % 2
                    slot = g % 6
                    diag = kt >= 4 * qt
                    if diag:
                        P.wait("vector", sPE, ("qk", g))

                        def dtril(e, qt=qt, kt=kt, par=par):
                            ri = kt - 4 * qt
                            e.tensor_tensor(PQK[par][:], PQK[par][:],
                                            tril_t[:, 512 * ri:512 * (ri + 1)],
                                            op=ALU.mult).then_inc(sDV.h, 1)
                        P.on("vector", dtril)
                        sDV.n += 1
                        P.mark(("diag", g), sDV)

                    P.wait("scalar", sPE, ("qk", g))
                    if diag:
                        P.wait("scalar", sDV, ("diag", g))
                    if g >= 6:
                        P.wait("scalar", sDV, ("cast", g - 6))
                    bi = 256 * h + 32 * qt + kt

                    def ptbm(e, par=par, slot=slot, bi=bi):
                        e.activation(out=_sl3(ptb, slot, 0, 512), in_=PQK[par][:],
                                     func=AF.Copy,
                                     scale=btab_t[:, bi:bi + 1]).then_inc(sSC.h, 1)
                    P.on("scalar", ptbm)
                    sSC.n += 1
                    P.mark(("ptbm", g), sSC)

                    P.wait("vector", sPE, ("qk", g))
                    P.wait("vector", sSC, ("ptbm", g))

                    def cast(e, slot=slot):
                        e.scalar_tensor_tensor(out=_sl3(pab, slot, 0, 512),
                                               in0=_sl3(ptb, slot, 0, 512), scalar=-1.0,
                                               in1=_sl3(ptb, slot, 0, 512),
                                               op0=ALU.mult, op1=ALU.max).then_inc(sDV.h, 1)
                    P.on("vector", cast)
                    sDV.n += 1
                    P.mark(("cast", g), sDV)

                for pi in range(G // 2):
                    emit_qk(2 * pi)
                    emit_qk(2 * pi + 1)
                    if pi > 0:
                        emit_pv(2 * pi - 2, 2 * pi + 1)
                        emit_pv(2 * pi - 1, 2 * pi + 1)
                    emit_castchain(2 * pi)
                    emit_castchain(2 * pi + 1)
                    gi = 2 * pi + 1
                    while pend and pend[0][0] <= gi:
                        heapq.heappop(pend)[2]()

                emit_pv(G - 2, G)
                emit_pv(G - 1, G)
                while pend:
                    heapq.heappop(pend)[2]()

                P.wait("gpsimd", sDV, ("epat", 8 * b + 7))

                def a2aw(e, b=b):
                    for jj in range(2):
                        j = 2 * b + jj
                        loc0 = jj * OWN
                        tt0 = loc0 // 128
                        e.dma_start(
                            out=a2ain[HR * j + 1:HR * j + 1 + OWN, :].rearrange("(tt p) c -> p tt c", p=128),
                            in_=attno[:, tt0:tt0 + 16, :]).then_inc(sGP.h, 16)
                        ta = max(loc0 - 1, 0)
                        e.dma_start(out=a2ain[HR * j:HR * j + 1, :],
                                    in_=attno[ta % 128:ta % 128 + 1, ta // 128, :]).then_inc(sGP.h, 16)
                        tb_ = min(loc0 + OWN, S - 1)
                        e.dma_start(out=a2ain[HR * j + 1 + OWN:HR * j + 2 + OWN, :],
                                    in_=attno[tb_ % 128:tb_ % 128 + 1, tb_ // 128, :]).then_inc(sGP.h, 16)
                        e.dma_start(out=a2ain[HR * j + 2 + OWN:HR * (j + 1), :],
                                    in_=attno[0:1, 0:6, :]).then_inc(sGP.h, 16)
                P.on("gpsimd", a2aw)
                sGP.n += 8 * 16
                P.mark(("a2aw", b), sGP)

            if DEBUG_STAGE == 20:
                P.wait("sync", sSY, ("xlnT", 0))
                P.wait("sync", sGP, "c2load")

                n0c = sSY.n

                def dump20(e, n0=n0c):
                    e.dma_start(out=dbg[:, 0:4096], in_=_sl3(xlnT, 0, 0, S)).then_inc(sSY.h, 16)
                    e.dma_start(out=dbg[:, 4096:8192], in_=_sl3(xlnT, 7, 0, S)).then_inc(sSY.h, 16)
                    e.wait_ge(sSY.h, n0 + 32)
                P.on("sync", dump20)
                sSY.n += 32
            if DEBUG_STAGE == 21:
                P.wait("sync", sDV, ("rot", 15))

                n0c = sSY.n

                def dump21(e, n0=n0c):
                    e.dma_start(out=dbg[:, 0:4096], in_=qrT[:]).then_inc(sSY.h, 16)
                    e.dma_start(out=dbg[:, 4096:8192], in_=krT[:]).then_inc(sSY.h, 16)
                    e.wait_ge(sSY.h, n0 + 32)
                P.on("sync", dump21)
                sSY.n += 32
            if DEBUG_STAGE == 22:
                P.wait("sync", sDV, ("vrc", 31))
                P.wait("sync", sSC, ("gsl", 31))

                n0c = sSY.n

                def dump22(e, n0=n0c):
                    e.dma_start(out=dbg[:, 0:4096], in_=vr[:].rearrange("p a c -> p (a c)")).then_inc(sSY.h, 16)
                    e.dma_start(out=dbg[:, 4096:8192], in_=gsil[:].rearrange("p a c -> p (a c)")).then_inc(sSY.h, 16)
                    e.wait_ge(sSY.h, n0 + 32)
                P.on("sync", dump22)
                sSY.n += 32
            if DEBUG_STAGE == 23:
                P.wait("sync", sDV, ("epat", 7))

                n0c = sSY.n

                def dump23(e, n0=n0c):
                    e.dma_start(out=dbg[:, 0:4096], in_=attno[:].rearrange("p a c -> p (a c)")).then_inc(sSY.h, 16)
                    e.wait_ge(sSY.h, n0 + 16)
                P.on("sync", dump23)
                sSY.n += 16
            if DEBUG_STAGE in (20, 21, 22, 23):
                for s_ in (sSY, sGP, sCC, sPE, sDV, sSC):
                    P.mark(("b2", s_.name), s_)
                blk2 = e2(nc.Block("blk2"))
                P.emit(blk2)
                return nc

            P.wait("gpsimd", sGP, ("a2aw", 3))

            def a2a(e):
                with allow_ici_all_to_all():
                    e.collective_compute("AllToAll", ALU.bypass,
                                         replica_groups=[list(range(NC))],
                                         ins=[a2ain[:].opt()], outs=[a2aout[:].opt()]).then_inc(sCC.h, 1)
            P.on("gpsimd", a2a)
            sCC.n += 1
            P.mark("a2a", sCC)

            if DEBUG_STAGE == 2:
                P.wait("sync", sCC, "a2a")
                base0 = sSY.n

                def dump2(e, base0=base0):
                    n = base0
                    bt_ = xlnT[:, 0, :].squeeze(1) if len(xlnT[:, 0, :].shape) == 3 else xlnT[:, 0, :]
                    bt16 = bt_[:, 0:2048].rearrange("p (tt c) -> p tt c", c=128)
                    bt8 = xlnT[0:8, 1, 0:128]
                    for r in range(NC):
                        e.dma_start(out=bt16,
                                    in_=a2aout[HR * r:HR * r + OWN, :].rearrange(
                                        "(tt p) c -> p tt c", p=128)).then_inc(sSY.h, 16)
                        e.dma_start(out=bt8,
                                    in_=a2aout[HR * r + OWN:HR * r + OWN + 8, :]).then_inc(sSY.h, 16)
                        n += 32
                        e.wait_ge(sSY.h, n)
                        e.dma_start(out=dbg[HR * r:HR * r + OWN, :].rearrange(
                                        "(tt p) c -> p tt c", p=128),
                                    in_=bt16).then_inc(sSY.h, 16)
                        e.dma_start(out=dbg[HR * r + OWN:HR * r + OWN + 8, :],
                                    in_=bt8).then_inc(sSY.h, 16)
                        n += 32
                        e.wait_ge(sSY.h, n)
                P.on("sync", dump2)
                sSY.n += 4 * NC * 16

            for s_ in (sSY, sGP, sCC, sPE, sDV, sSC, sX0, sX1, sW0, sW1, sS0, sS1, sO0, sO1):
                P.mark(("b2", s_.name), s_)
            blk2 = e2(nc.Block("blk2"))
            P.emit(blk2)
            M2 = dict(P.M)

        if DEBUG_STAGE == 2:
            return nc

        with ExitStack() as c3:
            e3 = c3.enter_context
            xrT = e3(nc.sbuf_tensor("xrT", [128, 8, HRP], BF16))
            invc = e3(nc.sbuf_tensor("invc", [128, 17], F32))
            invrow = e3(nc.sbuf_tensor("invrow", [1, HRP], F32))
            inv1b = e3(nc.sbuf_tensor("inv1b", [128, HRP], F32))
            c3a = ExitStack()
            e3a = c3a.enter_context
            attnf = e3a(nc.sbuf_tensor("attnf", [128, NTIL, E], BF16))
            xa3 = e3a(nc.sbuf_tensor("xa3", [128, 2, E], F32))
            x1b = e3a(nc.sbuf_tensor("x1b", [128, 2, E], F32))
            xc3 = e3a(nc.sbuf_tensor("xc3", [128, 2, E], F32))
            xln2 = e3a(nc.sbuf_tensor("xln2", [128, 2, E], F32))
            sq3 = e3a(nc.sbuf_tensor("sq3", [128, E], F32))
            xs3 = e3a(nc.sbuf_tensor("xs3", [128, E], F32))
            xi3 = e3a(nc.sbuf_tensor("xi3", [128, E], I32))
            xrb = e3a(nc.sbuf_tensor("xrb", [128, 2, E], BF16))
            pXT0 = e3a(nc.psum_tensor("pXT0", [128, 1024], BF16))
            pXT1 = e3a(nc.psum_tensor("pXT1", [128, 1024], BF16))
            pBCa = e3a(nc.psum_tensor("pBCa", [128, 512], F32))
            PXT = (pXT0, pXT1)

            P = Prog(M2)
            for eng_ in Prog.ENGINES:
                for s_ in (sSY, sGP, sPE, sDV, sSC, sX0, sX1, sW0, sW1, sS0, sS1, sO0, sO1):
                    P.wait(eng_, s_, ("b2", s_.name))

            def af16(e):
                e.memset(attnf[:, 16, :], 0.0).then_inc(sDV.h, 1)
            P.on("vector", af16)
            sDV.n += 1
            P.mark("af16", sDV)

            P.wait("gpsimd", sCC, "a2a")
            P.wait("gpsimd", sDV, "af16")

            def attl(e):
                for r in range(NC):
                    e.dma_start(
                        out=attnf[:, 0:16, 128 * r:128 * (r + 1)],
                        in_=a2aout[HR * r:HR * r + OWN, :].rearrange("(tt p) c -> p tt c", p=128)).then_inc(sGP.h, 16)
                    e.dma_start(out=attnf[0:8, 16, 128 * r:128 * (r + 1)],
                                in_=a2aout[HR * r + OWN:HR * r + OWN + 8, :]).then_inc(sGP.h, 16)
            P.on("gpsimd", attl)
            sGP.n += 16 * 16
            P.mark("attl", sGP)


            for t in range(NTIL):
                sl = XSL[t % 2]
                if t >= 2:
                    P.wait("sync", sDV, ("qmul", t - 2))

                def xl3(e, t=t):
                    e.dma_start(out=_sl3(xa3, t % 2, 0, E),
                                in_=xh[128 * t:128 * (t + 1), :]).then_inc(XSL[t % 2].h, 16)
                P.on("sync", xl3)
                sl.n += 16

                P.wait("vector", sl, sl.n)
                P.wait("vector", sGP, "attl")
                if t >= 2:
                    P.wait("vector", SSL[t % 2], ("x1st", t - 2))

                def x1a(e, t=t):
                    e.tensor_tensor(_sl3(x1b, t % 2, 0, E), _sl3(xa3, t % 2, 0, E),
                                    attnf[:, t, :], op=ALU.add).then_inc(sDV.h, 1)
                P.on("vector", x1a)
                sDV.n += 1
                P.mark(("x1", t), sDV)

                P.wait("sync", sDV, ("x1", t))

                def x1st(e, t=t):
                    e.dma_start(out=x1_d[128 * t:128 * (t + 1), :],
                                in_=_sl3(x1b, t % 2, 0, E)).then_inc(SSL[t % 2].h, 16)
                P.on("sync", x1st)
                SSL[t % 2].n += 16
                P.M[("x1st", t)] = SSL[t % 2].n

                co = 8 * (t % 2)
                P.wait("vector", sDV, ("x1", t))

                def l2a(e, t=t, co=co):
                    e.reduce_sum(cB[:, co:co + 1], _sl3(x1b, t % 2, 0, E), axis=AX).then_inc(sDV.h, 1)
                P.on("vector", l2a)
                sDV.n += 1
                P.mark(("l2sum", t), sDV)

                P.wait("scalar", sDV, ("l2sum", t))

                def l2b(e, co=co):
                    e.mul(out=cA[:, co:co + 1], in_=cB[:, co:co + 1], mul=1.0 / E).then_inc(sSC.h, 1)
                P.on("scalar", l2b)
                sSC.n += 1
                P.mark(("l2m", t), sSC)

                P.wait("vector", sSC, ("l2m", t))
                if t >= 2:
                    P.wait("vector", sSC, ("l2nrm", t - 2))

                def l2c(e, t=t, co=co):
                    e.tensor_scalar(out=_sl3(xc3, t % 2, 0, E), in0=_sl3(x1b, t % 2, 0, E),
                                    scalar1=cA[:, co:co + 1], scalar2=None,
                                    op0=ALU.subtract).then_inc(sDV.h, 1)
                P.on("vector", l2c)
                sDV.n += 1
                P.mark(("l2xc", t), sDV)

                P.wait("scalar", sDV, ("l2xc", t))

                acc_n = sSC.n + 1

                def l2d(e, t=t, co=co, acc_n=acc_n):
                    e.activation(out=sq3[:], in_=_sl3(xc3, t % 2, 0, E), func=AF.Square,
                                 accum_out=cA[:, co + 1:co + 2]).then_inc(sSC.h, 1)
                    e.wait_ge(sSC.h, acc_n)
                    e.activation(out=cA[:, co + 2:co + 3], in_=cA[:, co + 1:co + 2], func=AF.Sqrt,
                                 scale=1.0 / E, bias=EPS_LN).then_inc(sSC.h, 1)
                P.on("scalar", l2d)
                sSC.n += 2
                P.mark(("l2srt", t), sSC)

                P.wait("vector", sSC, ("l2srt", t))

                def l2e(e, co=co):
                    e.reciprocal(cB[:, co + 2:co + 3], cA[:, co + 2:co + 3]).then_inc(sDV.h, 1)
                P.on("vector", l2e)
                sDV.n += 1
                P.mark(("l2rstd", t), sDV)

                P.wait("scalar", sDV, ("l2rstd", t))
                if t >= 2:
                    P.wait("scalar", sDV, ("qmul", t - 2))

                def l2f(e, t=t, co=co):
                    e.activation(out=_sl3(xln2, t % 2, 0, E), in_=_sl3(xc3, t % 2, 0, E),
                                 func=AF.Copy, scale=cB[:, co + 2:co + 3]).then_inc(sSC.h, 1)
                P.on("scalar", l2f)
                sSC.n += 1
                P.mark(("l2nrm", t), sSC)

                P.wait("vector", sSC, ("l2nrm", t))

                def q1(e, t=t, co=co):
                    e.reduce_max(cB[:, co + 3:co + 4], _sl3(xln2, t % 2, 0, E), axis=AX,
                                 apply_absolute_value=True).then_inc(sDV.h, 1)
                P.on("vector", q1)
                sDV.n += 1
                P.mark(("qmx", t), sDV)

                P.wait("scalar", sDV, ("qmx", t))

                base_n = sSC.n

                def q2(e, t=t, co=co, base_n=base_n):
                    e.activation(out=cA[:, co + 3:co + 4], in_=cB[:, co + 3:co + 4], func=AF.Relu,
                                 bias=-1e-5).then_inc(sSC.h, 1)
                    e.wait_ge(sSC.h, base_n + 1)
                    e.add(out=cA[:, co + 3:co + 4], in_=cA[:, co + 3:co + 4],
                          add=1e-5).then_inc(sSC.h, 1)
                    e.wait_ge(sSC.h, base_n + 2)
                    e.mul(out=invc[:, t:t + 1], in_=cA[:, co + 3:co + 4],
                          mul=1.0 / 127.0).then_inc(sSC.h, 1)
                P.on("scalar", q2)
                sSC.n += 3
                P.mark(("qmc", t), sSC)

                P.wait("vector", sSC, ("qmc", t))

                def q3(e, co=co):
                    e.reciprocal(cB[:, co + 4:co + 5], cA[:, co + 3:co + 4]).then_inc(sDV.h, 1)
                P.on("vector", q3)
                sDV.n += 1
                P.mark(("qrec", t), sDV)

                P.wait("scalar", sDV, ("qrec", t))

                def q4(e, co=co):
                    e.mul(out=cA[:, co + 5:co + 6], in_=cB[:, co + 4:co + 5],
                          mul=127.0).then_inc(sSC.h, 1)
                P.on("scalar", q4)
                sSC.n += 1
                P.mark(("qs127", t), sSC)

                P.wait("vector", sSC, ("qs127", t))
                if t >= 2:
                    P.wait("vector", sPE, ("xtr", t - 2))

                acc_n = sDV.n

                def q5(e, t=t, co=co, acc_n=acc_n):
                    e.wait_ge(sDV.h, acc_n)
                    e.tensor_scalar(out=xs3[:], in0=_sl3(xln2, t % 2, 0, E),
                                    scalar1=cA[:, co + 5:co + 6], scalar2=None,
                                    op0=ALU.mult).then_inc(sDV.h, 1)
                    e.wait_ge(sDV.h, acc_n + 1)
                    e.tensor_scalar(out=xi3[:], in0=xs3[:], scalar1=-1.0, scalar2=None,
                                    op0=ALU.add).then_inc(sDV.h, 1)
                    e.wait_ge(sDV.h, acc_n + 2)
                    e.tensor_tensor(xs3[:], xs3[:], xi3[:], op=ALU.subtract).then_inc(sDV.h, 1)
                    e.wait_ge(sDV.h, acc_n + 3)
                    e.scalar_tensor_tensor(out=xi3[:], in0=xs3[:], scalar=1.5, in1=xi3[:],
                                           op0=ALU.is_gt, op1=ALU.add).then_inc(sDV.h, 1)
                    e.wait_ge(sDV.h, acc_n + 4)
                    e.scalar_tensor_tensor(out=_sl3(xrb, t % 2, 0, E), in0=xs3[:],
                                           scalar=0.5, in1=xi3[:],
                                           op0=ALU.is_ge, op1=ALU.add).then_inc(sDV.h, 1)
                P.on("vector", q5)
                sDV.n += 5
                P.mark(("qmul", t), sDV)

                P.wait("tensor", sDV, ("qmul", t))
                if t >= 2:
                    P.wait("tensor", sDV, ("xband", t - 2))

                def xtr(e, t=t):
                    pb = PXT[t % 2]
                    for ec in range(8):
                        d = e.transpose(pb[:, 128 * ec:128 * (ec + 1)],
                                        xrb[:, t % 2, 128 * ec:128 * (ec + 1)], identb_t[:])
                    d.then_inc(sPE.h, 1)
                P.on("tensor", xtr)
                sPE.n += 1
                P.mark(("xtr", t), sPE)

                P.wait("vector", sPE, ("xtr", t))

                def xband(e, t=t):
                    pb = PXT[t % 2]
                    e.tensor_copy(xrT[:, :, 128 * t:128 * (t + 1)],
                                  pb[:].rearrange("p (ec c) -> p ec c", c=128)).then_inc(sDV.h, 1)
                P.on("vector", xband)
                sDV.n += 1
                P.mark(("xband", t), sDV)

            grp = [(0, 4), (4, 8), (8, 12), (12, 16), (16, 17)]
            for gix, (a0, b0) in enumerate(grp):
                P.wait("tensor", sSC, ("qmc", b0 - 1))
                if gix >= 1:
                    P.wait("tensor", sDV, ("inve", gix - 1))

                def invtr(e, a0=a0, b0=b0):
                    for t in range(a0, b0):
                        d = e.transpose(pBCa[0:1, 128 * (t - a0):128 * (t - a0) + 128],
                                        invc[:, t:t + 1], identf_t[:])
                    d.then_inc(sPE.h, 1)
                P.on("tensor", invtr)
                sPE.n += 1
                P.mark(("invtr", gix), sPE)

                P.wait("vector", sPE, ("invtr", gix))

                def inve(e, a0=a0, b0=b0):
                    for t in range(a0, b0):
                        d = e.tensor_copy(invrow[0:1, 128 * t:128 * (t + 1)],
                                          pBCa[0:1, 128 * (t - a0):128 * (t - a0) + 128])
                    d.then_inc(sDV.h, 1)
                P.on("vector", inve)
                sDV.n += 1
                P.mark(("inve", gix), sDV)

            nch = (HRP + 511) // 512
            for cc in range(nch):
                w = min(512, HRP - 512 * cc)
                P.wait("tensor", sDV, ("inve", len(grp) - 1))
                if cc >= 1:
                    P.wait("tensor", sDV, ("bce", cc - 1))

                def bcm(e, cc=cc, w=w):
                    e.matmul(pBCa[:, 0:w], onesr_t[:], invrow[0:1, 512 * cc:512 * cc + w],
                             start=True, stop=True).then_inc(sPE.h, 1)
                P.on("tensor", bcm)
                sPE.n += 1
                P.mark(("bcm", cc), sPE)

                P.wait("vector", sPE, ("bcm", cc))

                def bce(e, cc=cc, w=w):
                    e.tensor_copy(inv1b[:, 512 * cc:512 * cc + w], pBCa[:, 0:w]).then_inc(sDV.h, 1)
                P.on("vector", bce)
                sDV.n += 1
                P.mark(("bce", cc), sDV)

            for s_ in (sSY, sGP, sCC, sPE, sDV, sSC, sX0, sX1, sW0, sW1, sS0, sS1, sO0, sO1):
                P.mark(("b3a", s_.name), s_)
            blk3a = e3a(nc.Block("blk3a"))
            P.emit(blk3a)
            M3a = dict(P.M)
            c3a.close()

            w1t = e3(nc.sbuf_tensor("w1t", [128, 2, 8, 128], BF16))
            hbuf = e3(nc.sbuf_tensor("hbuf", [128, 2, 514], F32))
            cv1 = e3(nc.sbuf_tensor("cv1", [128, 512], F32))
            cv2 = e3(nc.sbuf_tensor("cv2", [128, 512], F32))
            h2q = e3(nc.sbuf_tensor("h2q", [128, 32, 512], BF16))
            macc = e3(nc.sbuf_tensor("macc", [128, 512], F32))
            s2b = e3(nc.sbuf_tensor("s2b", [128, 512], F32))
            hs3 = e3(nc.sbuf_tensor("hs3", [128, 512], F32))
            hi3 = e3(nc.sbuf_tensor("hi3", [128, 512], I32))
            hs3g = e3(nc.sbuf_tensor("hs3g", [128, 512], F32))
            hi3g = e3(nc.sbuf_tensor("hi3g", [128, 512], I32))
            invc2s = e3(nc.sbuf_tensor("invc2s", [128, 4], F32))
            w2sb = e3(nc.sbuf_tensor("w2sb", [128, 32, E], BF16))
            x1rd = e3(nc.sbuf_tensor("x1rd", [128, 2, E], F32))
            obuf = e3(nc.sbuf_tensor("obuf", [128, 2, E], F32))
            pF10 = e3(nc.psum_tensor("pF10", [128, 512], F32))
            pF11 = e3(nc.psum_tensor("pF11", [128, 512], F32))
            pF20 = e3(nc.psum_tensor("pF20", [128, 512], F32))
            pF21 = e3(nc.psum_tensor("pF21", [128, 512], F32))
            pBC = e3(nc.psum_tensor("pBC", [128, 512], F32))
            PF1 = (pF10, pF11)
            PF2 = (pF20, pF21)

            P = Prog(M3a)
            for eng_ in Prog.ENGINES:
                for s_ in (sSY, sGP, sPE, sDV, sSC, sX0, sX1, sW0, sW1, sS0, sS1, sO0, sO1):
                    P.wait(eng_, s_, ("b3a", s_.name))

            def w2l(e):
                e.dma_start(out=w2sb[:],
                            in_=w2qT_d[:].rearrange("(ch p) c -> p ch c", p=128)).then_inc(sSY.h, 16)
            P.on("sync", w2l)
            sSY.n += 16
            P.mark("w2l", sSY)

            for q in range(4):
                hlo = 512 * q

                if q > 0:
                    P.wait("vector", sPE, ("mtr", 4 * q - 1))

                def mz(e):
                    e.memset(macc[:], 0.0).then_inc(sDV.h, 1)
                P.on("vector", mz)
                sDV.n += 1
                P.mark(("mz", q), sDV)

                for cht in range(32):
                    ci = 32 * q + cht
                    sl = WSL[ci % 2]
                    if ci >= 2:
                        P.wait("sync", sPE, ("f1", ci - 2))

                    def w1l(e, ci=ci, cht=cht):
                        e.dma_start(out=w1t[:, ci % 2, :, :],
                                    in_=w1qT_d[:, 128 * cht:128 * (cht + 1)].rearrange(
                                        "(ec p) c -> p ec c", p=128)).then_inc(WSL[ci % 2].h, 16)
                    P.on("sync", w1l)
                    sl.n += 16

                    P.wait("tensor", sl, sl.n)
                    P.wait("tensor", sDV, ("xband", NTIL - 1))
                    if ci == 0:
                        P.wait("tensor", sDV, ("bce", nch - 1))
                    elif cht == 0:
                        P.wait("tensor", sDV, ("s2e", 4 * q - 1))
                    if ci >= 2:
                        P.wait("tensor", sDV, ("f1e", ci - 2))
                    if ci >= 1:
                        P.wait("tensor", sDV, ("f1e", ci - 1))

                    def f1(e, ci=ci, hlo=hlo):
                        pp = PF1[ci % 2]
                        for ec in range(8):
                            d = e.matmul(pp[:], _sl3(w1t[:, ci % 2], ec, 0, 128),
                                         xrT[:, ec, hlo:hlo + 512],
                                         start=(ec == 0), stop=(ec == 7))
                        for ec in range(8):
                            d = e.matmul(pBC[:, 0:2], _sl3(w1t[:, ci % 2], ec, 0, 128),
                                         xrT[:, ec, hlo + 512:hlo + 514],
                                         start=(ec == 0), stop=(ec == 7))
                        d.then_inc(sPE.h, 1)
                    P.on("tensor", f1)
                    sPE.n += 1
                    P.mark(("f1", ci), sPE)

                    P.wait("vector", sPE, ("f1", ci))
                    P.wait("vector", sDV, ("mz", q))
                    if ci >= 2:
                        P.wait("vector", sSC, ("silu", ci - 2))

                    base_n = sDV.n
                    nmask = (1 if q == 0 else 0) + (1 if q == 3 else 0)

                    def f1e(e, q=q, ci=ci, hlo=hlo, base_n=base_n):
                        n = base_n
                        pp = PF1[ci % 2]
                        e.tensor_tensor(_sl3(hbuf, ci % 2, 0, 512), pp[:],
                                        inv1b[:, hlo:hlo + 512], op=ALU.mult).then_inc(sDV.h, 1)
                        e.tensor_tensor(_sl3(hbuf, ci % 2, 512, 514), pBC[:, 0:2],
                                        inv1b[:, hlo + 512:hlo + 514], op=ALU.mult).then_inc(sDV.h, 1)
                        n += 2
                        e.wait_ge(sDV.h, n)
                        if q == 0:
                            e.tensor_scalar(out=_sl3(hbuf, ci % 2, 0, 1),
                                            in0=_sl3(hbuf, ci % 2, 0, 1),
                                            scalar1=maskL_t[:], scalar2=None,
                                            op0=ALU.mult).then_inc(sDV.h, 1)
                            n += 1
                            e.wait_ge(sDV.h, n)
                        if q == 3:
                            e.tensor_scalar(out=_sl3(hbuf, ci % 2, 513, 514),
                                            in0=_sl3(hbuf, ci % 2, 513, 514),
                                            scalar1=maskR_t[:], scalar2=None,
                                            op0=ALU.mult).then_inc(sDV.h, 1)
                            n += 1
                            e.wait_ge(sDV.h, n)
                        ch = ci % 32
                        w0 = tapsp[:, 3 * ch:3 * ch + 1]
                        w1_ = tapsp[:, 3 * ch + 1:3 * ch + 2]
                        w2_ = tapsp[:, 3 * ch + 2:3 * ch + 3]
                        bcol = convb_t[:, ch:ch + 1]
                        e.scalar_tensor_tensor(out=cv1[:], in0=hbuf[:, ci % 2, 2:514], scalar=w2_,
                                               in1=bcol.broadcast_to([128, 512]),
                                               op0=ALU.mult, op1=ALU.add).then_inc(sDV.h, 1)
                        n += 1
                        e.wait_ge(sDV.h, n)
                        e.scalar_tensor_tensor(out=cv2[:], in0=hbuf[:, ci % 2, 1:513], scalar=w1_,
                                               in1=cv1[:], op0=ALU.mult, op1=ALU.add).then_inc(sDV.h, 1)
                        n += 1
                        e.wait_ge(sDV.h, n)
                        e.scalar_tensor_tensor(out=cv1[:], in0=hbuf[:, ci % 2, 0:512], scalar=w0,
                                               in1=cv2[:], op0=ALU.mult, op1=ALU.add).then_inc(sDV.h, 1)
                    P.on("vector", f1e)
                    sDV.n += 5 + nmask
                    P.mark(("f1e", ci), sDV)

                    P.wait("scalar", sDV, ("f1e", ci))
                    if q > 0 and cht == 0:
                        P.wait("scalar", sPE, ("f2", 4 * q - 1, 1))

                    def silu(e, cht=cht):
                        e.activation(out=h2q[:, cht, :], in_=cv1[:], func=AF.Silu).then_inc(sSC.h, 1)
                    P.on("scalar", silu)
                    sSC.n += 1
                    P.mark(("silu", ci), sSC)

                    P.wait("vector", sSC, ("silu", ci))

                    base_n = sDV.n

                    def mupd(e, cht=cht, base_n=base_n):
                        e.scalar_tensor_tensor(out=macc[:], in0=h2q[:, cht, :], scalar=1.0,
                                               in1=macc[:], op0=ALU.mult,
                                               op1=ALU.max).then_inc(sDV.h, 1)
                        e.wait_ge(sDV.h, base_n + 1)
                        e.scalar_tensor_tensor(out=macc[:], in0=h2q[:, cht, :], scalar=-1.0,
                                               in1=macc[:], op0=ALU.mult,
                                               op1=ALU.max).then_inc(sDV.h, 1)
                    P.on("vector", mupd)
                    sDV.n += 2
                    P.mark(("mupd", ci), sDV)

                for tt in range(4):
                    ot = 4 * q + tt
                    P.wait("tensor", sDV, ("mupd", 32 * q + 31))
                    P.wait("tensor", sDV, ("f1e", 32 * q + 31))
                    if ot >= 1:
                        P.wait("tensor", sDV, ("s2e", ot - 1))

                    def mtr(e, tt=tt):
                        e.transpose(pBC[:, 128:256], macc[:, 128 * tt:128 * (tt + 1)],
                                    identf_t[:]).then_inc(sPE.h, 1)
                    P.on("tensor", mtr)
                    sPE.n += 1
                    P.mark(("mtr", ot), sPE)

                    P.wait("vector", sPE, ("mtr", ot))

                    def mred(e):
                        e.reduce_max(cB[:, 20:21], pBC[:, 128:256], axis=AX).then_inc(sDV.h, 1)
                    P.on("vector", mred)
                    sDV.n += 1
                    P.mark(("mred", ot), sDV)

                    P.wait("scalar", sDV, ("mred", ot))
                    if q > 0 and tt == 0:
                        P.wait("scalar", sDV, ("oev", 4 * q - 1, 1))

                    base_n = sSC.n

                    def sc2(e, base_n=base_n):
                        e.activation(out=cA[:, 20:21], in_=cB[:, 20:21], func=AF.Relu,
                                     bias=-1e-5).then_inc(sSC.h, 1)
                        e.wait_ge(sSC.h, base_n + 1)
                        e.add(out=cA[:, 20:21], in_=cA[:, 20:21], add=1e-5).then_inc(sSC.h, 1)
                    P.on("scalar", sc2)
                    sSC.n += 2
                    P.mark(("mxc2", ot), sSC)

                    P.wait("vector", sSC, ("mxc2", ot))

                    def v2c(e):
                        e.tensor_scalar_mul(cB[:, 21:22], cA[:, 20:21], 1.0 / 127.0)
                        e.reciprocal(cB[:, 22:23], cA[:, 20:21]).then_inc(sDV.h, 1)
                    P.on("vector", v2c)
                    sDV.n += 1
                    P.mark(("v2c", ot), sDV)

                    P.wait("scalar", sDV, ("v2c", ot))
                    P.wait("scalar", sDV, "wm2")

                    def sc3(e, tt=tt):
                        e.activation(out=invc2s[:, tt:tt + 1], in_=m2w2c[:], func=AF.Copy,
                                     scale=cB[:, 21:22])
                        e.mul(out=cA[:, 21:22], in_=cB[:, 22:23], mul=127.0).then_inc(sSC.h, 1)
                    P.on("scalar", sc3)
                    sSC.n += 1
                    P.mark(("sc3", ot), sSC)

                    P.wait("tensor", sSC, ("sc3", ot))

                    def s2tr(e):
                        e.transpose(pBC[0:1, 256:384], cA[:, 21:22], identf_t[:]).then_inc(sPE.h, 1)
                    P.on("tensor", s2tr)
                    sPE.n += 1
                    P.mark(("s2tr", ot), sPE)

                    P.wait("vector", sPE, ("s2tr", ot))

                    def s2re(e):
                        e.tensor_copy(invrow[0:1, 0:128], pBC[0:1, 256:384]).then_inc(sDV.h, 1)
                    P.on("vector", s2re)
                    sDV.n += 1
                    P.mark(("s2r", ot), sDV)

                    P.wait("tensor", sDV, ("s2r", ot))

                    def s2bc(e):
                        e.matmul(pBC[:, 384:512], onesr_t[:], invrow[0:1, 0:128],
                                 start=True, stop=True).then_inc(sPE.h, 1)
                    P.on("tensor", s2bc)
                    sPE.n += 1
                    P.mark(("s2bc", ot), sPE)

                    P.wait("vector", sPE, ("s2bc", ot))

                    def s2e(e, tt=tt):
                        e.tensor_copy(s2b[:, 128 * tt:128 * (tt + 1)],
                                      pBC[:, 384:512]).then_inc(sDV.h, 1)
                    P.on("vector", s2e)
                    sDV.n += 1
                    P.mark(("s2e", ot), sDV)

                for cht in range(32):
                    ci = 32 * q + cht
                    eng = "vector"
                    sem = sDV
                    bufs = (hs3, hi3)
                    P.wait(eng, sDV, ("s2e", 4 * q + 3))

                    acc_n = sem.n

                    def hq(e, cht=cht, acc_n=acc_n, sem=sem, bufs=bufs):
                        hs_, hi_ = bufs
                        e.wait_ge(sem.h, acc_n)
                        e.tensor_tensor(hs_[:], h2q[:, cht, :], s2b[:],
                                        op=ALU.mult).then_inc(sem.h, 1)
                        e.wait_ge(sem.h, acc_n + 1)
                        e.tensor_scalar(out=hi_[:], in0=hs_[:], scalar1=-1.0, scalar2=None,
                                        op0=ALU.add).then_inc(sem.h, 1)
                        e.wait_ge(sem.h, acc_n + 2)
                        e.tensor_tensor(hs_[:], hs_[:], hi_[:], op=ALU.subtract).then_inc(sem.h, 1)
                        e.wait_ge(sem.h, acc_n + 3)
                        e.scalar_tensor_tensor(out=hi_[:], in0=hs_[:], scalar=1.5, in1=hi_[:],
                                               op0=ALU.is_gt, op1=ALU.add).then_inc(sem.h, 1)
                        e.wait_ge(sem.h, acc_n + 4)
                        e.scalar_tensor_tensor(out=h2q[:, cht, :], in0=hs_[:],
                                               scalar=0.5, in1=hi_[:],
                                               op0=ALU.is_ge, op1=ALU.add).then_inc(sem.h, 1)
                    P.on(eng, hq)
                    sem.n += 5
                    P.mark(("hq", ci), sem)

                for tt in range(4):
                    ot = 4 * q + tt
                    r0 = 1 + 512 * q + 128 * tt
                    if ot >= 2:
                        P.wait("sync", sDV, ("oev", ot - 2, 1))

                    def x1l(e, ot=ot, r0=r0):
                        e.dma_start(out=_sl3(x1rd, ot % 2, 0, E),
                                    in_=x1_d[r0:r0 + 128, :]).then_inc(XSL[ot % 2].h, 16)
                    P.on("sync", x1l)
                    XSL[ot % 2].n += 16
                    P.M[("x1l", ot)] = XSL[ot % 2].n

                    for eo in range(2):
                        gstep = 2 * ot + eo
                        P.wait("tensor", sDV, ("hq", 32 * q + 31))
                        P.wait("tensor", sSY, "w2l")
                        if gstep >= 2:
                            P.wait("tensor", sDV, ("oev", (gstep - 2) // 2, (gstep - 2) % 2))

                        def f2(e, q=q, tt=tt, eo=eo):
                            pp = PF2[eo]
                            for cht in range(32):
                                d = e.matmul(pp[:], h2q[:, cht, 128 * tt:128 * (tt + 1)],
                                             w2sb[:, cht, 512 * eo:512 * (eo + 1)],
                                             start=(cht == 0), stop=(cht == 31))
                            d.then_inc(sPE.h, 1)
                        P.on("tensor", f2)
                        sPE.n += 1
                        P.mark(("f2", ot, eo), sPE)

                        P.wait("vector", sPE, ("f2", ot, eo))
                        P.wait("vector", XSL[ot % 2], ("x1l", ot))
                        P.wait("vector", sSC, ("sc3", ot))
                        if ot >= 2 and eo == 0:
                            P.wait("vector", OSL[ot % 2], ("ost", ot - 2))

                        def oev(e, ot=ot, tt=tt, eo=eo):
                            pp = PF2[eo]
                            e.scalar_tensor_tensor(
                                out=obuf[:, ot % 2, 512 * eo:512 * (eo + 1)], in0=pp[:],
                                scalar=invc2s[:, tt:tt + 1],
                                in1=x1rd[:, ot % 2, 512 * eo:512 * (eo + 1)],
                                op0=ALU.mult, op1=ALU.add).then_inc(sDV.h, 1)
                        P.on("vector", oev)
                        sDV.n += 1
                        P.mark(("oev", ot, eo), sDV)

                    P.wait("sync", sDV, ("oev", ot, 1))

                    def ost(e, q=q, tt=tt, ot=ot):
                        e.dma_start(out=out[512 * q + 128 * tt:512 * q + 128 * (tt + 1), :],
                                    in_=_sl3(obuf, ot % 2, 0, E)).then_inc(OSL[ot % 2].h, 16)
                    P.on("sync", ost)
                    OSL[ot % 2].n += 16
                    P.M[("ost", ot)] = OSL[ot % 2].n

            P.wait("sync", sO0, ("ost", 14))
            P.wait("sync", sO1, ("ost", 15))

            blk3 = e3(nc.Block("blk3b"))
            P.emit(blk3)

    return nc


def kernel(L0_attn_x, L0_attn_Wq, L0_attn_Wk, L0_attn_Wv, L0_attn_Wg,
           L0_ffn_w1, L0_ffn_conv_w, L0_ffn_conv_b, L0_ffn_w2):
    xf = np.ascontiguousarray(np.asarray(L0_attn_x, dtype=np.float32).reshape(TOK, E))
    args = [np.ascontiguousarray(np.asarray(a, dtype=np.float32)) for a in
            (L0_attn_Wq, L0_attn_Wk, L0_attn_Wv, L0_attn_Wg, L0_ffn_w1,
             L0_ffn_conv_w, L0_ffn_conv_b, L0_ffn_w2)]
    in_maps = [_stage_inputs(c, xf, *args) for c in range(NC)]
    nc = build_program()
    res = run_bass_kernel_spmd(nc, in_maps, core_ids=list(range(NC)))
    if DEBUG_STAGE:
        return [r["dbg"] for r in res.results]
    return np.concatenate([res.results[c]["o"] for c in range(NC)], axis=0).reshape(B, S, E)


# version 20
# speedup vs baseline: 1.0514x; 1.0514x over previous
import heapq
import math
import numpy as np
import ml_dtypes
import concourse.bass as bass
import concourse.mybir as mybir
from concourse.bass_utils import run_bass_kernel_spmd
from concourse.aot_env import allow_ici_all_to_all
from contextlib import ExitStack

F32 = mybir.dt.float32
BF16 = mybir.dt.bfloat16
I32 = mybir.dt.int32
I16 = mybir.dt.int16
AF = mybir.ActivationFunctionType
ALU = mybir.AluOpType
AX = mybir.AxisListType.X

NC = 8
B, S, E, H, D = 4, 4096, 1024, 16, 64
TOK = B * S
OWN = TOK // NC
HR = OWN + 8
NTIL = 17
HRP = NTIL * 128
EPS_LN = 1e-5
EPS_RMS = 1e-6
SCALING = float(E) ** -0.5

DEBUG_STAGE = 0



def _decay(h):
    return math.log(1.0 - 2.0 ** (-5.0 - h))


def _host_tables(core):
    t = {}
    d0, d1 = _decay(2 * core), _decay(2 * core + 1)

    ang = 1.0 / 10000.0 ** np.linspace(0.0, 1.0, D // 2)
    angle = np.repeat(ang, 2)
    idx = np.arange(S, dtype=np.float64)
    sinT = np.sin(idx[:, None] * angle[None, :]).T
    cosT = np.cos(idx[:, None] * angle[None, :]).T

    kk = np.arange(S, dtype=np.float64)
    kloc = kk - 128.0 * np.floor(kk / 128.0)
    pat0 = np.exp(-d0 * kloc)
    pat1 = np.exp(-d1 * kloc)
    t["tk_cos"] = (np.concatenate([cosT * pat0, cosT * pat1], 0) * SCALING).astype(ml_dtypes.bfloat16)
    t["tk_sin"] = (np.concatenate([sinT * pat0, sinT * pat1], 0) * SCALING).astype(ml_dtypes.bfloat16)
    t["tq_cos"] = np.concatenate([cosT, cosT], 0).astype(ml_dtypes.bfloat16)
    t["tq_sin"] = np.concatenate([sinT, sinT], 0).astype(ml_dtypes.bfloat16)

    bt = np.zeros((2, 8, 32), dtype=np.float64)
    for hh, dd in ((0, d0), (1, d1)):
        for qt in range(8):
            for kt in range(4 * qt + 4):
                bt[hh, qt, kt] = np.exp(dd * (512.0 * qt - 128.0 * kt))
    t["btab"] = np.broadcast_to(bt.reshape(1, 512), (128, 512)).astype(np.float32).copy()

    q = np.arange(S, dtype=np.float64)
    qt0 = 512.0 * np.floor(q / 512.0)
    grow = np.zeros((1, 2 * S), dtype=np.float64)
    for hh, dd in ((0, d0), (1, d1)):
        r = np.exp(dd)
        rowsum = (1.0 - r ** (q + 1.0)) / (1.0 - r)
        val = (np.exp(dd * (q - qt0)) / np.sqrt(rowsum)).reshape(8, 512)
        for qt in range(8):
            grow[0, 1024 * qt + 512 * hh:1024 * qt + 512 * hh + 512] = val[qt]
    t["grow"] = grow.astype(np.float32)

    tril = np.zeros((128, 4, 512), dtype=np.float32)
    kp = np.arange(128)[:, None]
    qf = np.arange(512)[None, :]
    for ri, rel in enumerate((0, 128, 256, 384)):
        tril[:, ri, :] = (kp + rel <= qf).astype(np.float32)
    t["trilt"] = np.ascontiguousarray(tril.reshape(128, 2048))

    t["maskL"] = np.full((128, 1), 0.0 if core % 2 == 0 else 1.0, np.float32)
    t["maskR"] = np.full((128, 1), 0.0 if core % 2 == 1 else 1.0, np.float32)
    return t


def _stage_inputs(core, xf, Wq, Wk, Wv, Wg, w1, conv_w, conv_b, w2):
    m = dict(_host_tables(core))
    lo = OWN * core
    rows = np.clip(np.arange(-1, HRP - 1) + lo, 0, TOK - 1)
    m["xh"] = np.ascontiguousarray(xf[rows])
    cols = slice(128 * core, 128 * core + 128)
    m["wq"] = np.ascontiguousarray(Wq[:, cols])
    m["wk"] = np.ascontiguousarray(Wk[:, cols])
    m["wvg"] = np.ascontiguousarray(np.concatenate([Wv[:, cols], Wg[:, cols]], axis=1))
    m["w1f"] = w1
    m["w2f"] = w2
    m["convw"] = np.ascontiguousarray(conv_w.reshape(32, 128, 3).transpose(1, 0, 2).reshape(128, 96))
    m["convb"] = np.ascontiguousarray(conv_b.reshape(32, 128).T)
    return m



class Sem:
    def __init__(self, ctx, nc, name):
        self.h = ctx.enter_context(nc.semaphore(name))
        self.n = 0
        self.name = name


class Prog:
    ENGINES = ("sync", "gpsimd", "tensor", "vector", "scalar")

    def __init__(self, markers=None):
        self.steps = {e: [] for e in self.ENGINES}
        self.M = dict(markers or {})

    def on(self, eng, fn):
        self.steps[eng].append(fn)

    def wait(self, eng, sem, key_or_val):
        M = self.M
        if isinstance(key_or_val, int):
            v = key_or_val
            self.on(eng, lambda e: e.wait_ge(sem.h, v))
        else:
            k = key_or_val
            self.on(eng, lambda e: e.wait_ge(sem.h, M[k]))

    def mark(self, key, sem):
        self.M[key] = sem.n

    def emit(self, block):
        for eng in self.ENGINES:
            steps = self.steps[eng]
            if not steps:
                continue

            def body(e, steps=steps):
                for fn in steps:
                    fn(e)

            getattr(block, eng)(body)


def _sl3(t, k, a, b):
    ap_ = t[:, k, a:b]
    if len(ap_.shape) == 3:
        ap_ = ap_.squeeze(1)
    return ap_


def _tile_end2(qt):
    return 4 * qt * qt + 12 * qt + 7


def _tile_end(h, qt):
    n = 0
    for hh in range(2):
        for qq in range(8):
            cnt = 4 * qq + 4
            if hh == h and qq == qt:
                return n + cnt - 1
            n += cnt
    raise AssertionError


def build_program():
    import os as _osx
    SKIP_AG = bool(_osx.environ.get("SKIP_AG")) and DEBUG_STAGE == 19
    SKIP_WQ = bool(_osx.environ.get("SKIP_WQ")) and DEBUG_STAGE == 19
    LN_TILES = int(_osx.environ.get("LN_TILES", "16")) if DEBUG_STAGE == 19 else 16
    SKIP_TR = bool(_osx.environ.get("SKIP_TR")) and DEBUG_STAGE == 19
    nc = bass.Bass("TRN2", target_bir_lowering=False)

    ap = lambda name, shape, dt, kind: nc.dram_tensor(name, shape, dt, kind=kind).ap()
    xh = ap("xh", [HRP, E], F32, "ExternalInput")
    wq = ap("wq", [E, 128], F32, "ExternalInput")
    wk = ap("wk", [E, 128], F32, "ExternalInput")
    wvg = ap("wvg", [E, 256], F32, "ExternalInput")
    w1f = ap("w1f", [4096, E], F32, "ExternalInput")
    w2f = ap("w2f", [E, 4096], F32, "ExternalInput")
    convw = ap("convw", [128, 96], F32, "ExternalInput")
    convb = ap("convb", [128, 32], F32, "ExternalInput")
    tq_cos = ap("tq_cos", [128, S], BF16, "ExternalInput")
    tq_sin = ap("tq_sin", [128, S], BF16, "ExternalInput")
    tk_cos = ap("tk_cos", [128, S], BF16, "ExternalInput")
    tk_sin = ap("tk_sin", [128, S], BF16, "ExternalInput")
    btab = ap("btab", [128, 512], F32, "ExternalInput")
    grow = ap("grow", [1, 2 * S], F32, "ExternalInput")
    trilt = ap("trilt", [128, 2048], F32, "ExternalInput")
    maskL = ap("maskL", [128, 1], F32, "ExternalInput")
    maskR = ap("maskR", [128, 1], F32, "ExternalInput")
    out = ap("o", [OWN, E], F32, "ExternalOutput")
    dbg = None
    if DEBUG_STAGE == 1:
        dbg = ap("dbg", [NC * E, OWN], BF16, "ExternalOutput")
    elif DEBUG_STAGE == 2:
        dbg = ap("dbg", [NC * HR, 128], BF16, "ExternalOutput")
    elif DEBUG_STAGE in (19, 20, 21, 22, 23):
        dbg = ap("dbg", [128, 8192], BF16, "ExternalOutput")
    elif DEBUG_STAGE == 24:
        dbg = ap("dbg", [128, 8704], F32, "ExternalOutput")

    ag1in = nc.dram_tensor("ag1in", [E, OWN], BF16).ap()
    ag1out = nc.dram_tensor("ag1out", [NC * E, OWN], BF16, addr_space="Shared").ap()
    a2ain = nc.dram_tensor("a2ain", [NC * HR, 128], BF16).ap()
    a2aout = nc.dram_tensor("a2aout", [NC * HR, 128], BF16).ap()
    w1qT_d = nc.dram_tensor("w1qT_d", [E, 4096], BF16).ap()
    w2qT_d = nc.dram_tensor("w2qT_d", [4096, E], BF16).ap()
    x1_d = nc.dram_tensor("x1_d", [HRP, E], F32).ap()

    ident_bf = nc.inline_tensor(np.eye(128, dtype=ml_dtypes.bfloat16), "ident_bf").ap()
    ident_f = nc.inline_tensor(np.eye(128, dtype=np.float32), "ident_f").ap()
    r2m = np.zeros((128, 128), dtype=np.float64)
    for j in range(64):
        r2m[2 * j, 2 * j + 1] = -1.0
        r2m[2 * j + 1, 2 * j] = 1.0
    r2t = nc.inline_tensor(np.ascontiguousarray(r2m.T).astype(ml_dtypes.bfloat16), "r2t").ap()
    ones_bf = nc.inline_tensor(np.ones((128, 1), dtype=ml_dtypes.bfloat16), "ones_bf").ap()
    onesrow_f = nc.inline_tensor(np.ones((1, 128), dtype=np.float32), "onesrow_f").ap()
    allones_f = nc.inline_tensor(np.ones((128, 128), dtype=np.float32), "allones_f").ap()
    onescell_f = nc.inline_tensor(np.ones((1, 1), dtype=np.float32), "onescell_f").ap()
    CPOOL_VALS = (EPS_LN, -EPS_LN, EPS_RMS)
    cpool_f = nc.inline_tensor(
        np.broadcast_to(np.array(CPOOL_VALS, np.float32), (128, 3)).copy(), "cpool_f").ap()

    CTX = ExitStack()
    E_ = CTX.enter_context
    with CTX:
        sSY = Sem(CTX, nc, "sSY")
        sGP = Sem(CTX, nc, "sGP")
        sCC = Sem(CTX, nc, "sCC")
        sPE = Sem(CTX, nc, "sPE")
        sDV = Sem(CTX, nc, "sDV")
        sSC = Sem(CTX, nc, "sSC")
        sX0 = Sem(CTX, nc, "sX0")
        sX1 = Sem(CTX, nc, "sX1")
        sW0 = Sem(CTX, nc, "sW0")
        sW1 = Sem(CTX, nc, "sW1")
        sS0 = Sem(CTX, nc, "sS0")
        sS1 = Sem(CTX, nc, "sS1")
        sO0 = Sem(CTX, nc, "sO0")
        sO1 = Sem(CTX, nc, "sO1")
        sGC = Sem(CTX, nc, "sGC")
        sW2 = Sem(CTX, nc, "sW2")
        sW3 = Sem(CTX, nc, "sW3")
        WSL2 = (sW2, sW3)
        XSL = (sX0, sX1)
        WSL = (sW0, sW1)
        SSL = (sS0, sS1)
        OSL = (sO0, sO1)

        identb_t = E_(nc.sbuf_tensor("identb_t", [128, 128], BF16))
        identf_t = E_(nc.sbuf_tensor("identf_t", [128, 128], F32))
        r2t_t = E_(nc.sbuf_tensor("r2t_t", [128, 128], BF16))
        onesb_t = E_(nc.sbuf_tensor("onesb_t", [128, 1], BF16))
        onesr_t = E_(nc.sbuf_tensor("onesr_t", [1, 128], F32))
        allones_t = E_(nc.sbuf_tensor("allones_t", [128, 128], F32))
        onescell_t = E_(nc.sbuf_tensor("onescell_t", [1, 1], F32))
        cA = E_(nc.sbuf_tensor("cA", [128, 32], F32))
        cB = E_(nc.sbuf_tensor("cB", [128, 32], F32))
        m2w1c = E_(nc.sbuf_tensor("m2w1c", [128, 1], F32))
        m2w2c = E_(nc.sbuf_tensor("m2w2c", [128, 1], F32))
        convw_t = E_(nc.sbuf_tensor("convw_t", [128, 96], F32))
        convb_t = E_(nc.sbuf_tensor("convb_t", [128, 32], F32))
        tapsp = E_(nc.sbuf_tensor("tapsp", [128, 96], F32))
        maskL_t = E_(nc.sbuf_tensor("maskL_t", [128, 1], F32))
        maskR_t = E_(nc.sbuf_tensor("maskR_t", [128, 1], F32))
        cpool_t = E_(nc.sbuf_tensor("cpool_t", [128, 3], F32))
        for vi, vv in enumerate(CPOOL_VALS):
            nc.const_aps.register(F32, float(vv), cpool_t[:, vi:vi + 1])

        with ExitStack() as c1:
            e1 = c1.enter_context
            xa = e1(nc.sbuf_tensor("xa", [128, 2, E], F32))
            xc_t = e1(nc.sbuf_tensor("xc_t", [128, 2, E], F32))
            sqsc = e1(nc.sbuf_tensor("sqsc", [128, E], F32))
            xlnbf = e1(nc.sbuf_tensor("xlnbf", [128, 2, E], BF16))
            xlnTo = e1(nc.sbuf_tensor("xlnTo", [128, 8, OWN], BF16))
            wtile = e1(nc.sbuf_tensor("wtile", [128, 2, 4096], F32))
            wsq = e1(nc.sbuf_tensor("wsq", [128, 4096], F32))
            wint = e1(nc.sbuf_tensor("wint", [128, 4096], F32))
            wqb = e1(nc.sbuf_tensor("wqb", [128, 4096], BF16))
            wtT = e1(nc.sbuf_tensor("wtT", [128, 4096], BF16))
            wabs = e1(nc.sbuf_tensor("wabs", [128, 40], F32))
            ptrA = e1(nc.psum_tensor("ptrA", [128, 1024], BF16))
            ptrB = e1(nc.psum_tensor("ptrB", [128, 1024], BF16))
            pred = e1(nc.psum_tensor("pred", [128, 2], F32))
            PTR = (ptrA, ptrB)

            P = Prog()

            def cdma(dst, src):
                def f(e):
                    e.dma_start(out=dst, in_=src).then_inc(sSY.h, 16)
                return f

            for dst, src in ((identb_t[:], ident_bf[:]), (identf_t[:], ident_f[:]),
                             (r2t_t[:], r2t[:]), (onesb_t[:], ones_bf[:]),
                             (onesr_t[:], onesrow_f[:]), (allones_t[:], allones_f[:]),
                             (onescell_t[:], onescell_f[:]),
                             (convw_t[:], convw[:]), (convb_t[:], convb[:]),
                             (maskL_t[:], maskL[:]), (maskR_t[:], maskR[:]),
                             (cpool_t[:], cpool_f[:])):
                P.on("sync", cdma(dst, src))
                sSY.n += 16
            P.mark("const", sSY)

            for t in range(LN_TILES):
                sl = XSL[t % 2]
                if t >= 2:
                    P.wait("sync", sDV, ("lnxc", t - 2))

                def xload(e, t=t):
                    e.dma_start(out=_sl3(xa, t % 2, 0, E),
                                in_=xh[1 + 128 * t:129 + 128 * t, :]).then_inc(XSL[t % 2].h, 16)
                P.on("sync", xload)
                sl.n += 16

                P.wait("vector", sl, sl.n)
                co = 8 * (t % 2)

                def v1(e, t=t, co=co):
                    e.reduce_sum(cB[:, co:co + 1], _sl3(xa, t % 2, 0, E), axis=AX).then_inc(sDV.h, 1)
                P.on("vector", v1)
                sDV.n += 1
                P.mark(("lnsum", t), sDV)

                if t == 0:
                    P.wait("scalar", sSY, "const")
                P.wait("scalar", sDV, ("lnsum", t))

                def s1(e, co=co):
                    e.mul(out=cA[:, co:co + 1], in_=cB[:, co:co + 1], mul=1.0 / E).then_inc(sSC.h, 1)
                P.on("scalar", s1)
                sSC.n += 1
                P.mark(("lnm", t), sSC)

                P.wait("vector", sSC, ("lnm", t))
                if t >= 2:
                    P.wait("vector", sSC, ("lnout", t - 2))

                def v2(e, t=t, co=co):
                    e.tensor_scalar(out=_sl3(xc_t, t % 2, 0, E), in0=_sl3(xa, t % 2, 0, E),
                                    scalar1=cA[:, co:co + 1], scalar2=None,
                                    op0=ALU.subtract).then_inc(sDV.h, 1)
                P.on("vector", v2)
                sDV.n += 1
                P.mark(("lnxc", t), sDV)

                P.wait("scalar", sDV, ("lnxc", t))

                acc_n = sSC.n + 1

                def s2(e, t=t, co=co, acc_n=acc_n):
                    e.activation(out=sqsc[:], in_=_sl3(xc_t, t % 2, 0, E), func=AF.Square,
                                 accum_out=cA[:, co + 1:co + 2]).then_inc(sSC.h, 1)
                    e.wait_ge(sSC.h, acc_n)
                    e.activation(out=cA[:, co + 2:co + 3], in_=cA[:, co + 1:co + 2], func=AF.Sqrt,
                                 scale=1.0 / E, bias=EPS_LN).then_inc(sSC.h, 1)
                P.on("scalar", s2)
                sSC.n += 2
                P.mark(("lnsrt", t), sSC)

                P.wait("vector", sSC, ("lnsrt", t))

                def v3(e, co=co):
                    e.reciprocal(cB[:, co + 2:co + 3], cA[:, co + 2:co + 3]).then_inc(sDV.h, 1)
                P.on("vector", v3)
                sDV.n += 1
                P.mark(("lnrstd", t), sDV)

                P.wait("scalar", sDV, ("lnrstd", t))
                if t >= 2:
                    P.wait("scalar", sPE, ("lntr", t - 2))

                def s3(e, t=t, co=co):
                    e.activation(out=_sl3(xlnbf, t % 2, 0, E), in_=_sl3(xc_t, t % 2, 0, E),
                                 func=AF.Copy, scale=cB[:, co + 2:co + 3]).then_inc(sSC.h, 1)
                P.on("scalar", s3)
                sSC.n += 1
                P.mark(("lnout", t), sSC)

                if SKIP_TR:
                    P.M[("lntr", t)] = sPE.n
                    P.M[("lnband", t)] = sDV.n
                    continue
                P.wait("tensor", sSC, ("lnout", t))
                P.wait("tensor", sSY, "const")
                if t >= 2:
                    P.wait("tensor", sDV, ("lnband", t - 2))

                def te1(e, t=t):
                    pb = PTR[t % 2]
                    for ec in range(8):
                        d = e.transpose(pb[:, 128 * ec:128 * (ec + 1)],
                                        xlnbf[:, t % 2, 128 * ec:128 * (ec + 1)], identb_t[:])
                    d.then_inc(sPE.h, 1)
                P.on("tensor", te1)
                sPE.n += 1
                P.mark(("lntr", t), sPE)

                P.wait("vector", sPE, ("lntr", t))

                def v4(e, t=t):
                    pb = PTR[t % 2]
                    e.tensor_copy(xlnTo[:, :, 128 * t:128 * (t + 1)],
                                  pb[:].rearrange("p (ec c) -> p ec c", c=128)).then_inc(sDV.h, 1)
                P.on("vector", v4)
                sDV.n += 1
                P.mark(("lnband", t), sDV)

            if LN_TILES > 0:
                P.wait("sync", sDV, ("lnband", LN_TILES - 1))

            def agw(e):
                for ec in range(8):
                    e.dma_start(out=ag1in[128 * ec:128 * (ec + 1), :],
                                in_=xlnTo[:, ec, :]).then_inc(sSY.h, 16)
            if not SKIP_AG:
                P.on("sync", agw)
                sSY.n += 8 * 16
            P.mark("ag1w", sSY)

            if not SKIP_AG:
                P.wait("gpsimd", sSY, "ag1w")

            def ag1(e):
                e.collective_compute("AllGather", ALU.bypass,
                                     replica_groups=[list(range(NC))],
                                     ins=[ag1in[:].opt()], outs=[ag1out[:].opt()]).then_inc(sCC.h, 1)
            if not SKIP_AG:
                P.on("gpsimd", ag1)
                sCC.n += 1
            P.mark("ag1", sCC)

            for i in ([] if SKIP_WQ else range(40)):
                sl = WSL[i % 2]
                if i >= 2:
                    P.wait("sync", sDV, ("wabs", i - 2))
                wd = 1024 if i < 32 else 4096

                def wload(e, i=i, wd=wd):
                    src = w1f[128 * i:128 * (i + 1), :] if i < 32 else w2f[128 * (i - 32):128 * (i - 31), :]
                    e.dma_start(out=_sl3(wtile, i % 2, 0, wd), in_=src).then_inc(WSL[i % 2].h, 16)
                P.on("sync", wload)
                sl.n += 16
                P.wait("vector", sl, sl.n)

                def wabs_f(e, i=i, wd=wd):
                    e.reduce_sum(wabs[:, i:i + 1], _sl3(wtile, i % 2, 0, wd), axis=AX,
                                 apply_absolute_value=True).then_inc(sDV.h, 1)
                P.on("vector", wabs_f)
                sDV.n += 1
                P.mark(("wabs", i), sDV)

            if not SKIP_WQ:
                P.wait("vector", sDV, ("wabs", 39))

                def wsums(e):
                    e.reduce_sum(cB[:, 16:17], wabs[:, 0:32], axis=AX)
                    e.reduce_sum(cB[:, 17:18], wabs[:, 32:40], axis=AX).then_inc(sDV.h, 1)
                P.on("vector", wsums)
                sDV.n += 1
                P.mark("wsum", sDV)

                P.wait("tensor", sDV, "wsum")

                def wtot(e):
                    e.matmul(pred[:], allones_t[:], cB[:, 16:18], start=True, stop=True).then_inc(sPE.h, 1)
                P.on("tensor", wtot)
                sPE.n += 1
                P.mark("wtot", sPE)

                P.wait("scalar", sPE, "wtot")

                def wm1(e):
                    e.mul(out=cA[:, 16:17], in_=pred[:, 0:1], mul=1.0 / (4096.0 * 1024.0))
                    e.mul(out=cA[:, 17:18], in_=pred[:, 1:2], mul=1.0 / (1024.0 * 4096.0)).then_inc(sSC.h, 1)
                P.on("scalar", wm1)
                sSC.n += 1
                P.mark("wm1", sSC)

                P.wait("vector", sSC, "wm1")

                def wm2(e):
                    e.tensor_scalar_max(m2w1c[:], cA[:, 16:17], 1e-5)
                    e.tensor_scalar_max(m2w2c[:], cA[:, 17:18], 1e-5).then_inc(sDV.h, 1)
                P.on("vector", wm2)
                sDV.n += 1
                P.mark("wm2", sDV)

                P.wait("scalar", sDV, "wm2")

                def wm2s(e):
                    e.copy(out=cA[:, 18:19], in_=m2w1c[:])
                    e.copy(out=cA[:, 19:20], in_=m2w2c[:]).then_inc(sSC.h, 1)
                P.on("scalar", wm2s)
                sSC.n += 1
                P.mark("wm2s", sSC)

                P.wait("vector", sSC, "wm2s")

                def wrec(e):
                    e.reciprocal(cB[:, 18:19], cA[:, 18:19])
                    e.reciprocal(cB[:, 19:20], cA[:, 19:20]).then_inc(sDV.h, 1)
                P.on("vector", wrec)
                sDV.n += 1
                P.mark("wrec", sDV)

                P.wait("scalar", sDV, "wm2")
                P.wait("scalar", sSY, "const")

                def tap(e):
                    e.mul(out=tapsp[:], in_=convw_t[:], mul=m2w1c[:]).then_inc(sSC.h, 1)
                P.on("scalar", tap)
                sSC.n += 1
                P.mark("taps", sSC)

                band = 0
                for i in range(40):
                    sl2 = WSL2[i % 2]
                    wd = 1024 if i < 32 else 4096
                    nb = wd // 1024
                    if i < 2:
                        P.wait("scalar", sDV, ("wabs", 38 + i))

                    def wload2(e, i=i, wd=wd):
                        src = w1f[128 * i:128 * (i + 1), :] if i < 32 else w2f[128 * (i - 32):128 * (i - 31), :]
                        e.dma_start(out=_sl3(wtile, i % 2, 0, wd), in_=src).then_inc(WSL2[i % 2].h, 16)
                    P.on("scalar", wload2)
                    sl2.n += 16

                    P.wait("scalar", sl2, sl2.n)
                    P.wait("scalar", sDV, "wrec")
                    if i >= 1:
                        P.wait("scalar", sDV, ("wint", i - 1))

                    def wsf(e, i=i, wd=wd):
                        e.activation(out=wsq[:, 0:wd], in_=_sl3(wtile, i % 2, 0, wd), func=AF.Copy,
                                     scale=cB[:, 18:19] if i < 32 else cB[:, 19:20]).then_inc(sSC.h, 1)
                    P.on("scalar", wsf)
                    sSC.n += 1
                    P.mark(("wsf", i), sSC)

                    P.wait("vector", sSC, ("wsf", i))

                    acc_n = sDV.n + 1

                    def wcv(e, wd=wd, acc_n=acc_n):
                        e.tensor_scalar(out=wint[:, 0:wd], in0=wsq[:, 0:wd],
                                        scalar1=0.5, scalar2=1.0,
                                        op0=ALU.is_gt, op1=ALU.subtract).then_inc(sDV.h, 1)
                        e.wait_ge(sDV.h, acc_n)
                        e.scalar_tensor_tensor(out=wqb[:, 0:wd], in0=wsq[:, 0:wd],
                                               scalar=-0.5, in1=wint[:, 0:wd],
                                               op0=ALU.is_ge, op1=ALU.add).then_inc(sDV.h, 1)
                    P.on("vector", wcv)
                    sDV.n += 2
                    P.mark(("wint", i), sDV)

                    for bb in range(nb):
                        P.wait("tensor", sDV, ("wint", i))
                        if band < 2:
                            P.wait("tensor", sDV, ("lnband", 15))
                        else:
                            P.wait("tensor", sDV, ("wtb", band - 2))

                        def wtr(e, bb=bb, band=band):
                            pb = PTR[band % 2]
                            for k in range(8):
                                d = e.transpose(pb[:, 128 * k:128 * (k + 1)],
                                                wqb[:, 1024 * bb + 128 * k:1024 * bb + 128 * (k + 1)],
                                                identb_t[:])
                            d.then_inc(sPE.h, 1)
                        P.on("tensor", wtr)
                        sPE.n += 1
                        P.mark(("wtr", band), sPE)

                        P.wait("vector", sPE, ("wtr", band))
                        if i >= 1 and bb == 0:
                            P.wait("vector", sSY, ("wst", i - 1))

                        def wtb(e, bb=bb, band=band):
                            pb = PTR[band % 2]
                            e.tensor_copy(wtT[:, 1024 * bb:1024 * (bb + 1)],
                                          pb[:].rearrange("p (k c) -> p k c", c=128)).then_inc(sDV.h, 1)
                        P.on("vector", wtb)
                        sDV.n += 1
                        P.mark(("wtb", band), sDV)
                        band += 1

                    P.wait("sync", sDV, ("wtb", band - 1))

                    def wst(e, i=i):
                        if i < 32:
                            d = e.dma_start(
                                out=w1qT_d[:, 128 * i:128 * (i + 1)].rearrange("(ec p) c -> p ec c", p=128),
                                in_=wtT[:, 0:1024].rearrange("p (ec c) -> p ec c", c=128))
                        else:
                            jj = i - 32
                            d = e.dma_start(
                                out=w2qT_d[:, 128 * jj:128 * (jj + 1)].rearrange("(ch p) c -> p ch c", p=128),
                                in_=wtT[:, 0:4096].rearrange("p (ch c) -> p ch c", c=128))
                        d.then_inc(sSY.h, 16)
                    P.on("sync", wst)
                    sSY.n += 16
                    P.mark(("wst", i), sSY)

            if DEBUG_STAGE == 24:
                P.wait("sync", sDV, "wrec")
                P.wait("sync", sSC, ("wsf", 39))
                P.wait("sync", sDV, ("wint", 39))
                base24 = sSY.n

                def dump24(e, base24=base24):
                    e.dma_start(out=dbg[:, 0:40], in_=wabs[:]).then_inc(sSY.h, 16)
                    e.dma_start(out=dbg[:, 56:60], in_=cA[:, 16:20]).then_inc(sSY.h, 16)
                    e.dma_start(out=dbg[:, 88:92], in_=cB[:, 16:20]).then_inc(sSY.h, 16)
                    e.dma_start(out=dbg[:, 104:4200], in_=wsq[:]).then_inc(sSY.h, 16)
                    e.wait_ge(sSY.h, base24 + 64)
                P.on("sync", dump24)
                sSY.n += 64

                P.wait("gpsimd", sSC, ("wsf", 39))
                P.wait("gpsimd", sDV, ("wint", 39))
                base24g = sGP.n

                def dump24g(e, base24g=base24g):
                    e.dma_start(out=dbg[:, 4200:8296], in_=wqb[:]).then_inc(sGP.h, 16)
                    e.wait_ge(sGP.h, base24g + 16)
                P.on("gpsimd", dump24g)
                sGP.n += 16

            if DEBUG_STAGE == 1:
                P.wait("sync", sCC, "ag1")
                base0 = sSY.n

                def dump1(e, base0=base0):
                    n = base0
                    for r in range(NC):
                        e.dma_start(out=xlnTo[:],
                                    in_=ag1out[E * r:E * (r + 1), :].rearrange(
                                        "(ec p) c -> p ec c", p=128)).then_inc(sSY.h, 16)
                        n += 16
                        e.wait_ge(sSY.h, n)
                        e.dma_start(out=dbg[E * r:E * (r + 1), :].rearrange(
                                        "(ec p) c -> p ec c", p=128),
                                    in_=xlnTo[:]).then_inc(sSY.h, 16)
                        n += 16
                        e.wait_ge(sSY.h, n)
                P.on("sync", dump1)
                sSY.n += 2 * NC * 16

            for s_ in (sSY, sGP, sCC, sPE, sDV, sSC, sX0, sX1, sW0, sW1, sW2, sW3, sS0, sS1, sO0, sO1):
                P.mark(("b1", s_.name), s_)
            blk1 = e1(nc.Block("blk1"))
            P.emit(blk1)
            M1 = dict(P.M)

        if DEBUG_STAGE in (1, 24):
            return nc

        with ExitStack() as c2:
            e2 = c2.enter_context
            import os as _os2
            if DEBUG_STAGE == 19 and _os2.environ.get("SKIP_B2ALLOC"):
                P = Prog(M1)
                if sCC.n:
                    P.wait("sync", sCC, "ag1")

                n019 = sSY.n

                def dump19(e, n019=n019):
                    e.dma_start(out=dbg[:, 0:128], in_=identb_t[:]).then_inc(sSY.h, 16)
                    e.wait_ge(sSY.h, n019 + 16)
                P.on("sync", dump19)
                sSY.n += 16
                blk2y = e2(nc.Block("blk2"))
                P.emit(blk2y)
                return nc
            xlnT = e2(nc.sbuf_tensor("xlnT", [128, 8, S], BF16))
            tqc = e2(nc.sbuf_tensor("tqc", [128, S], BF16))
            tqs = e2(nc.sbuf_tensor("tqs", [128, S], BF16))
            tkc = e2(nc.sbuf_tensor("tkc", [128, S], BF16))
            tks = e2(nc.sbuf_tensor("tks", [128, S], BF16))
            btab_t = e2(nc.sbuf_tensor("btab_t", [128, 512], F32))
            grow_t = e2(nc.sbuf_tensor("grow_t", [1, 2 * S], BF16))
            tril_t = e2(nc.sbuf_tensor("tril_t", [128, 2048], BF16))
            wq_sb = e2(nc.sbuf_tensor("wq_sb", [128, 8, 128], BF16))
            wk_sb = e2(nc.sbuf_tensor("wk_sb", [128, 8, 128], BF16))
            wvg_sb = e2(nc.sbuf_tensor("wvg_sb", [128, 8, 256], BF16))
            qrT = e2(nc.sbuf_tensor("qrT", [128, S], BF16))
            krT = e2(nc.sbuf_tensor("krT", [128, S], BF16))
            vr = e2(nc.sbuf_tensor("vr", [128, 32, 128], BF16))
            gsil = e2(nc.sbuf_tensor("gsil", [128, 32, 128], BF16))
            tsinb = e2(nc.sbuf_tensor("tsinb", [128, 2, 512], BF16))
            tcosb = e2(nc.sbuf_tensor("tcosb", [128, 2, 512], BF16))
            ptb = e2(nc.sbuf_tensor("ptb", [128, 6, 512], BF16))
            pab = e2(nc.sbuf_tensor("pab", [128, 6, 512], BF16))
            pvsb0 = e2(nc.sbuf_tensor("pvsb0", [64, 512], BF16))
            pvsb1 = e2(nc.sbuf_tensor("pvsb1", [64, 512], BF16))
            densb = e2(nc.sbuf_tensor("densb", [1, 1024], F32))
            rsrow = e2(nc.sbuf_tensor("rsrow", [1, 1024], F32))
            o1b = e2(nc.sbuf_tensor("o1b", [128, 512], F32))
            sqep = e2(nc.sbuf_tensor("sqep", [128, 512], F32))
            o2b = e2(nc.sbuf_tensor("o2b", [128, 512], BF16))
            attno = e2(nc.sbuf_tensor("attno", [128, 32, 128], BF16))
            pQK0 = e2(nc.psum_tensor("pQK0", [128, 512], F32))
            pQK1 = e2(nc.psum_tensor("pQK1", [128, 512], F32))
            pPV0 = e2(nc.psum_tensor("pPV0", [64, 512], F32))
            pPV1 = e2(nc.psum_tensor("pPV1", [64, 512], F32))
            pDEN0 = e2(nc.psum_tensor("pDEN0", [1, 512], F32))
            pDEN1 = e2(nc.psum_tensor("pDEN1", [1, 512], F32))
            pROT = e2(nc.psum_tensor("pROT", [128, 512], F32))
            pEPI = e2(nc.psum_tensor("pEPI", [128, 512], BF16))
            PQK = (pQK0, pQK1)
            pPRJ0, pPRJ1 = pQK0, pQK1
            PPRJ = (pPRJ0, pPRJ1)
            PPVH = (pPV0, pPV1)
            PDENH = (pDEN0, pDEN1)

            P = Prog(M1)
            import os as _os
            if not _os.environ.get("SKIP_BARRIER"):
                for eng_ in Prog.ENGINES:
                    for s_ in (sSY, sGP, sPE, sDV, sSC, sX0, sX1, sW0, sW1, sW2, sW3, sS0, sS1, sO0, sO1):
                        P.wait(eng_, s_, ("b1", s_.name))

            def _attn_epilogue_stages(b, qt):
                e_ = 8 * b + qt
                tiles_g = ("pvden", 288 * b + _tile_end2(qt))

                def st_epev():
                    P.wait("scalar", sPE, tiles_g)
                    if e_ >= 1:
                        P.wait("scalar", sDV, ("eprs", e_ - 1))

                    def epev(e):
                        e.copy(out=pvsb0[:], in_=pPV0[:])
                        e.copy(out=pvsb1[:], in_=pPV1[:])
                        e.copy(out=densb[0:1, 0:512], in_=pDEN0[:])
                        e.copy(out=densb[0:1, 512:1024], in_=pDEN1[:]).then_inc(sSC.h, 1)
                    P.on("scalar", epev)
                    sSC.n += 1
                    P.mark(("epev", e_), sSC)

                def st_eprs():
                    P.wait("vector", sSC, ("epev", e_))
                    if e_ >= 1:
                        P.wait("vector", sPE, ("eptr", e_ - 1))

                    base_n = sDV.n

                    def eprs(e, qt=qt, base_n=base_n):
                        g0 = 1024 * qt
                        e.tensor_tensor(rsrow[:], densb[:], grow_t[0:1, g0:g0 + 1024],
                                        op=ALU.mult).then_inc(sDV.h, 1)
                        e.wait_ge(sDV.h, base_n + 1)
                        e.tensor_scalar(out=rsrow[:], in0=rsrow[:], scalar1=1.0, scalar2=50000.0,
                                        op0=ALU.max, op1=ALU.min).then_inc(sDV.h, 1)
                        e.wait_ge(sDV.h, base_n + 2)
                        e.reciprocal(rsrow[:], rsrow[:]).then_inc(sDV.h, 1)
                        e.wait_ge(sDV.h, base_n + 3)
                        e.tensor_tensor(rsrow[:], rsrow[:], grow_t[0:1, g0:g0 + 1024],
                                        op=ALU.mult).then_inc(sDV.h, 1)
                    P.on("vector", eprs)
                    sDV.n += 4
                    P.mark(("eprs", e_), sDV)

                def st_eptr():
                    P.wait("tensor", sDV, ("eprs", e_))
                    P.wait("tensor", sSC, ("epev", e_))
                    if e_ >= 1:
                        P.wait("tensor", sSC, ("epo1", e_ - 1))
                        P.wait("tensor", sDV, ("eprsc", e_ - 1))

                    def eptr(e):
                        for c in range(8):
                            qc, h = c // 2, c % 2
                            pvh = (pvsb0, pvsb1)[h]
                            e.transpose(pEPI[:, 64 * c:64 * (c + 1)],
                                        pvh[:, 128 * qc:128 * (qc + 1)],
                                        identb_t[0:64, 0:64])
                        for c in range(8):
                            qc, h = c // 2, c % 2
                            d = e.matmul(pROT[:, c:c + 1],
                                         rsrow[0:1, 512 * h + 128 * qc:512 * h + 128 * qc + 128],
                                         onescell_t[:], start=True, stop=True)
                        d.then_inc(sPE.h, 1)
                    P.on("tensor", eptr)
                    sPE.n += 1
                    P.mark(("eptr", e_), sPE)

                def st_eprsc():
                    P.wait("vector", sPE, ("eptr", e_))
                    if e_ >= 1:
                        P.wait("vector", sSC, ("epo1", e_ - 1))

                    def eprsc(e):
                        e.tensor_copy(cB[:, 16:24], pROT[:, 0:8]).then_inc(sDV.h, 1)
                    P.on("vector", eprsc)
                    sDV.n += 1
                    P.mark(("eprsc", e_), sDV)

                def st_epo1():
                    P.wait("scalar", sDV, ("eprsc", e_))
                    if e_ >= 1:
                        P.wait("scalar", sDV, ("eprms", e_ - 1))

                    acc_n = sSC.n

                    def epo1(e, acc_n=acc_n):
                        for c in range(8):
                            d = e.activation(out=o1b[:, 64 * c:64 * (c + 1)],
                                             in_=pEPI[:, 64 * c:64 * (c + 1)],
                                             func=AF.Copy, scale=cB[:, 16 + c:17 + c])
                        d.then_inc(sSC.h, 1)
                        e.wait_ge(sSC.h, acc_n + 1)
                        for c in range(8):
                            d = e.activation(out=sqep[:, 64 * c:64 * (c + 1)],
                                             in_=o1b[:, 64 * c:64 * (c + 1)], func=AF.Square,
                                             accum_out=cA[:, 16 + c:17 + c])
                        d.then_inc(sSC.h, 1)
                        e.wait_ge(sSC.h, acc_n + 2)
                        e.activation(out=cA[:, 24:32], in_=cA[:, 16:24], func=AF.Sqrt,
                                     scale=1.0 / 64.0, bias=EPS_RMS).then_inc(sSC.h, 1)
                    P.on("scalar", epo1)
                    sSC.n += 3
                    P.mark(("epo1", e_), sSC)

                def st_eprms():
                    P.wait("vector", sSC, ("epo1", e_))

                    def eprms(e):
                        e.reciprocal(cB[:, 24:32], cA[:, 24:32]).then_inc(sDV.h, 1)
                    P.on("vector", eprms)
                    sDV.n += 1
                    P.mark(("eprms", e_), sDV)

                def st_epo2():
                    P.wait("scalar", sDV, ("eprms", e_))
                    if e_ >= 1:
                        P.wait("scalar", sDV, ("epat", e_ - 1))

                    def epo2(e):
                        for c in range(8):
                            d = e.activation(out=o2b[:, 64 * c:64 * (c + 1)],
                                             in_=o1b[:, 64 * c:64 * (c + 1)],
                                             func=AF.Copy, scale=cB[:, 24 + c:25 + c])
                        d.then_inc(sSC.h, 1)
                    P.on("scalar", epo2)
                    sSC.n += 1
                    P.mark(("epo2", e_), sSC)

                def st_epat():
                    P.wait("vector", sSC, ("epo2", e_))
                    if b > 0 and e_ == 8 * b:
                        P.wait("vector", sGP, ("a2aw", b - 1))

                    def epat(e, qt=qt):
                        e.tensor_tensor(attno[:, 4 * qt:4 * qt + 4, :],
                                        o2b[:].rearrange("p (qc hd) -> p qc hd", hd=128),
                                        gsil[:, 4 * qt:4 * qt + 4, :],
                                        op=ALU.mult).then_inc(sDV.h, 1)
                    P.on("vector", epat)
                    sDV.n += 1
                    P.mark(("epat", e_), sDV)

                return [st_epev, st_eprs, st_eptr, st_eprsc,
                        st_epo1, st_eprms, st_epo2, st_epat]

            if DEBUG_STAGE == 19:
                if sCC.n:
                    P.wait("sync", sCC, "ag1")
                if not SKIP_WQ:
                    P.wait("sync", sSY, ("wst", 39))

                n019 = sSY.n

                def dump19(e, n019=n019):
                    e.dma_start(out=dbg[:, 0:128], in_=identb_t[:]).then_inc(sSY.h, 16)
                    e.wait_ge(sSY.h, n019 + 16)
                P.on("sync", dump19)
                sSY.n += 16
                for s_ in (sSY, sGP, sCC, sPE, sDV, sSC):
                    P.mark(("b2", s_.name), s_)
                blk2x = e2(nc.Block("blk2"))
                P.emit(blk2x)
                return nc

            def c2load(e):
                for dst, src in ((tqc[:], tq_cos[:]), (tqs[:], tq_sin[:]),
                                 (tkc[:], tk_cos[:]), (tks[:], tk_sin[:]),
                                 (btab_t[:], btab[:]), (grow_t[:], grow[:]),
                                 (tril_t[:], trilt[:])):
                    e.dma_start(out=dst, in_=src).then_inc(sGP.h, 16)
                e.dma_start(out=wq_sb[:], in_=wq[:].rearrange("(ec p) c -> p ec c", p=128)).then_inc(sGP.h, 16)
                e.dma_start(out=wk_sb[:], in_=wk[:].rearrange("(ec p) c -> p ec c", p=128)).then_inc(sGP.h, 16)
                e.dma_start(out=wvg_sb[:], in_=wvg[:].rearrange("(ec p) c -> p ec c", p=128)).then_inc(sGP.h, 16)
            P.on("gpsimd", c2load)
            sGP.n += 10 * 16
            P.mark("c2load", sGP)

            NB = 1 if DEBUG_STAGE in (19, 20, 21, 22, 23) else 4
            for b in range(NB):
                P.wait("sync", sCC, "ag1")
                if b > 0:
                    P.wait("sync", sPE, ("vgdone", b - 1))

                def xlnl(e, b=b):
                    for half in range(2):
                        for ec in range(8):
                            r0 = E * (2 * b + half) + 128 * ec
                            e.dma_start(out=xlnT[:, ec, OWN * half:OWN * (half + 1)],
                                        in_=ag1out[r0:r0 + 128, :]).then_inc(sSY.h, 16)
                P.on("sync", xlnl)
                sSY.n += 16 * 16
                P.mark(("xlnT", b), sSY)

                for part in (range(0) if DEBUG_STAGE == 20 else range(2)):
                    wsb = (wq_sb, wk_sb)[part]
                    tc = (tqc, tkc)[part]
                    tsn = (tqs, tks)[part]
                    dst = (qrT, krT)[part]
                    for ch in range(8):
                        gidx = 16 * b + 8 * part + ch
                        P.wait("tensor", sSY, ("xlnT", b))
                        P.wait("tensor", sGP, "c2load")
                        if gidx >= 2:
                            P.wait("tensor", sDV, ("rotcos", gidx - 2))
                        if b > 0 and gidx == 16 * b:
                            P.wait("tensor", sSC, ("gsl", 32 * b - 1))
                            P.wait("tensor", sSC, ("ptbm", 288 * b - 1))

                        def prj(e, ch=ch, wsb=wsb):
                            pp = PPRJ[ch % 2]
                            for ec in range(8):
                                d = e.matmul(pp[:], _sl3(wsb, ec, 0, 128),
                                             xlnT[:, ec, 512 * ch:512 * (ch + 1)],
                                             start=(ec == 0), stop=(ec == 7))
                            d.then_inc(sPE.h, 1)
                        P.on("tensor", prj)
                        sPE.n += 1
                        P.mark(("prj", gidx), sPE)

                        P.wait("vector", sPE, ("prj", gidx))
                        P.wait("vector", sGP, "c2load")
                        if gidx >= 2:
                            P.wait("vector", sPE, ("r2", gidx - 2))

                        def rot1(e, ch=ch, gidx=gidx, tc=tc, tsn=tsn):
                            pp = PPRJ[ch % 2]
                            e.tensor_tensor(_sl3(tsinb, gidx % 2, 0, 512), pp[:],
                                            tsn[:, 512 * ch:512 * (ch + 1)], op=ALU.mult)
                            e.tensor_tensor(_sl3(tcosb, gidx % 2, 0, 512), pp[:],
                                            tc[:, 512 * ch:512 * (ch + 1)],
                                            op=ALU.mult).then_inc(sDV.h, 1)
                        P.on("vector", rot1)
                        sDV.n += 1
                        P.mark(("rotcos", gidx), sDV)

                        P.wait("tensor", sDV, ("rotcos", gidx))
                        if gidx >= 1:
                            P.wait("tensor", sDV, ("rot", gidx - 1))

                        def r2mm(e, gidx=gidx):
                            e.matmul(pROT[:], r2t_t[:], _sl3(tsinb, gidx % 2, 0, 512),
                                     start=True, stop=True).then_inc(sPE.h, 1)
                        P.on("tensor", r2mm)
                        sPE.n += 1
                        P.mark(("r2", gidx), sPE)

                        P.wait("vector", sPE, ("r2", gidx))

                        def rot2(e, ch=ch, gidx=gidx, dst=dst):
                            e.tensor_tensor(dst[:, 512 * ch:512 * (ch + 1)], pROT[:],
                                            _sl3(tcosb, gidx % 2, 0, 512),
                                            op=ALU.add).then_inc(sDV.h, 1)
                        P.on("vector", rot2)
                        sDV.n += 1
                        P.mark(("rot", gidx), sDV)

                for tt in (range(0) if DEBUG_STAGE in (20, 21) else range(32)):
                    gv = 32 * b + tt
                    P.wait("tensor", sDV, ("rot", 16 * b + 15))
                    if tt >= 2:
                        P.wait("tensor", sDV, ("vrc", gv - 2))
                        P.wait("tensor", sSC, ("gsl", gv - 2))

                    def vgp(e, tt=tt):
                        pp = PPRJ[tt % 2]
                        for ec in range(8):
                            d = e.matmul(pp[:, 0:256], xlnT[:, ec, 128 * tt:128 * (tt + 1)],
                                         _sl3(wvg_sb, ec, 0, 256), start=(ec == 0), stop=(ec == 7))
                        d.then_inc(sPE.h, 1)
                    P.on("tensor", vgp)
                    sPE.n += 1
                    P.mark(("vg", gv), sPE)
                    if tt == 31:
                        P.mark(("vgdone", b), sPE)

                    P.wait("vector", sPE, ("vg", gv))

                    def vrc(e, tt=tt):
                        pp = PPRJ[tt % 2]
                        e.tensor_copy(vr[:, tt, :], pp[:, 0:128]).then_inc(sDV.h, 1)
                    P.on("vector", vrc)
                    sDV.n += 1
                    P.mark(("vrc", gv), sDV)

                    P.wait("scalar", sPE, ("vg", gv))
                    P.wait("scalar", sDV, ("vrc", gv))
                    if b > 0 and tt == 0:
                        P.wait("scalar", sDV, ("epat", 8 * b - 1))

                    def gsl(e, tt=tt):
                        pp = PPRJ[tt % 2]
                        e.activation(out=gsil[:, tt, :], in_=pp[:, 128:256],
                                     func=AF.Silu).then_inc(sSC.h, 1)
                    P.on("scalar", gsl)
                    sSC.n += 1
                    P.mark(("gsl", gv), sSC)

                if DEBUG_STAGE in (20, 21, 22):
                    continue
                tiles = []
                for qt in range(8):
                    for kt in range(4 * qt + 4):
                        for h in range(2):
                            tiles.append((h, qt, kt))
                G = len(tiles)
                assert G == 288

                def qk_step(e, h, qt, kt, par):
                    e.matmul(PQK[par][:], krT[64 * h:64 * (h + 1), 128 * kt:128 * (kt + 1)],
                             qrT[64 * h:64 * (h + 1), 512 * qt:512 * (qt + 1)],
                             start=True, stop=True).then_inc(sPE.h, 1)

                def pv_step(e, h, qt, kt, slot, first, last):
                    pt = _sl3(ptb, slot, 0, 512)
                    pa = _sl3(pab, slot, 0, 512)
                    e.matmul(PPVH[h][:], _sl3(vr, kt, 64 * h, 64 * h + 64), pt,
                             start=first, stop=last)
                    e.matmul(PDENH[h][:], onesb_t[:], pa,
                             start=first, stop=last).then_inc(sPE.h, 1)

                pend = []
                pend_seq = [0]

                def emit_pv(gi_prev, cur_gi, b=b):
                    hp, qp, kp_ = tiles[gi_prev]
                    g_prev = 288 * b + gi_prev
                    first = (kp_ == 0)
                    last = (kp_ == 4 * qp + 3)
                    P.wait("tensor", sDV, ("cast", g_prev))
                    if first and hp == 0:
                        egrp = 8 * b + qp
                        if egrp > 0:
                            P.wait("tensor", sSC, ("epev", egrp - 1))
                    P.on("tensor", lambda e, hp=hp, qp=qp, kp_=kp_, slot=g_prev % 6,
                         first=first, last=last: pv_step(e, hp, qp, kp_, slot, first, last))
                    sPE.n += 1
                    P.mark(("pvden", g_prev), sPE)
                    if last and hp == 1:
                        for k, st in enumerate(_attn_epilogue_stages(b, qp)):
                            heapq.heappush(pend, (cur_gi + k, pend_seq[0], st))
                            pend_seq[0] += 1

                def emit_qk(gi):
                    h, qt, kt = (tiles[gi][0], tiles[gi][1], tiles[gi][2])
                    g = 288 * b + gi
                    par = g % 2
                    P.wait("tensor", sDV, ("rot", 16 * b + 15))
                    if gi == 0:
                        P.wait("tensor", sDV, ("vrc", 32 * b + 31))
                        P.wait("tensor", sSC, ("gsl", 32 * b + 31))
                    if g >= 2:
                        P.wait("tensor", sSC, ("ptbm", g - 2))
                    P.on("tensor", lambda e, h=h, qt=qt, kt=kt, par=par: qk_step(e, h, qt, kt, par))
                    sPE.n += 1
                    P.mark(("qk", g), sPE)

                def emit_castchain(gi):
                    h, qt, kt = (tiles[gi][0], tiles[gi][1], tiles[gi][2])
                    g = 288 * b + gi
                    par = g % 2
                    slot = g % 6
                    diag = kt >= 4 * qt
                    if diag:
                        P.wait("vector", sPE, ("qk", g))

                        def dtril(e, qt=qt, kt=kt, par=par):
                            ri = kt - 4 * qt
                            e.tensor_tensor(PQK[par][:], PQK[par][:],
                                            tril_t[:, 512 * ri:512 * (ri + 1)],
                                            op=ALU.mult).then_inc(sDV.h, 1)
                        P.on("vector", dtril)
                        sDV.n += 1
                        P.mark(("diag", g), sDV)

                    P.wait("scalar", sPE, ("qk", g))
                    if diag:
                        P.wait("scalar", sDV, ("diag", g))
                    if g >= 6:
                        P.wait("scalar", sDV, ("cast", g - 6))
                    bi = 256 * h + 32 * qt + kt

                    def ptbm(e, par=par, slot=slot, bi=bi):
                        e.activation(out=_sl3(ptb, slot, 0, 512), in_=PQK[par][:],
                                     func=AF.Copy,
                                     scale=btab_t[:, bi:bi + 1]).then_inc(sSC.h, 1)
                    P.on("scalar", ptbm)
                    sSC.n += 1
                    P.mark(("ptbm", g), sSC)

                    P.wait("vector", sPE, ("qk", g))
                    P.wait("vector", sSC, ("ptbm", g))

                    def cast(e, slot=slot):
                        e.scalar_tensor_tensor(out=_sl3(pab, slot, 0, 512),
                                               in0=_sl3(ptb, slot, 0, 512), scalar=-1.0,
                                               in1=_sl3(ptb, slot, 0, 512),
                                               op0=ALU.mult, op1=ALU.max).then_inc(sDV.h, 1)
                    P.on("vector", cast)
                    sDV.n += 1
                    P.mark(("cast", g), sDV)

                for pi in range(G // 2):
                    emit_qk(2 * pi)
                    emit_qk(2 * pi + 1)
                    if pi > 0:
                        emit_pv(2 * pi - 2, 2 * pi + 1)
                        emit_pv(2 * pi - 1, 2 * pi + 1)
                    emit_castchain(2 * pi)
                    emit_castchain(2 * pi + 1)
                    gi = 2 * pi + 1
                    while pend and pend[0][0] <= gi:
                        heapq.heappop(pend)[2]()

                emit_pv(G - 2, G)
                emit_pv(G - 1, G)
                while pend:
                    heapq.heappop(pend)[2]()

                P.wait("gpsimd", sDV, ("epat", 8 * b + 7))

                def a2aw(e, b=b):
                    for jj in range(2):
                        j = 2 * b + jj
                        loc0 = jj * OWN
                        tt0 = loc0 // 128
                        e.dma_start(
                            out=a2ain[HR * j + 1:HR * j + 1 + OWN, :].rearrange("(tt p) c -> p tt c", p=128),
                            in_=attno[:, tt0:tt0 + 16, :]).then_inc(sGP.h, 16)
                        ta = max(loc0 - 1, 0)
                        e.dma_start(out=a2ain[HR * j:HR * j + 1, :],
                                    in_=attno[ta % 128:ta % 128 + 1, ta // 128, :]).then_inc(sGP.h, 16)
                        tb_ = min(loc0 + OWN, S - 1)
                        e.dma_start(out=a2ain[HR * j + 1 + OWN:HR * j + 2 + OWN, :],
                                    in_=attno[tb_ % 128:tb_ % 128 + 1, tb_ // 128, :]).then_inc(sGP.h, 16)
                        e.dma_start(out=a2ain[HR * j + 2 + OWN:HR * (j + 1), :],
                                    in_=attno[0:1, 0:6, :]).then_inc(sGP.h, 16)
                P.on("gpsimd", a2aw)
                sGP.n += 8 * 16
                P.mark(("a2aw", b), sGP)

            if DEBUG_STAGE == 20:
                P.wait("sync", sSY, ("xlnT", 0))
                P.wait("sync", sGP, "c2load")

                n0c = sSY.n

                def dump20(e, n0=n0c):
                    e.dma_start(out=dbg[:, 0:4096], in_=_sl3(xlnT, 0, 0, S)).then_inc(sSY.h, 16)
                    e.dma_start(out=dbg[:, 4096:8192], in_=_sl3(xlnT, 7, 0, S)).then_inc(sSY.h, 16)
                    e.wait_ge(sSY.h, n0 + 32)
                P.on("sync", dump20)
                sSY.n += 32
            if DEBUG_STAGE == 21:
                P.wait("sync", sDV, ("rot", 15))

                n0c = sSY.n

                def dump21(e, n0=n0c):
                    e.dma_start(out=dbg[:, 0:4096], in_=qrT[:]).then_inc(sSY.h, 16)
                    e.dma_start(out=dbg[:, 4096:8192], in_=krT[:]).then_inc(sSY.h, 16)
                    e.wait_ge(sSY.h, n0 + 32)
                P.on("sync", dump21)
                sSY.n += 32
            if DEBUG_STAGE == 22:
                P.wait("sync", sDV, ("vrc", 31))
                P.wait("sync", sSC, ("gsl", 31))

                n0c = sSY.n

                def dump22(e, n0=n0c):
                    e.dma_start(out=dbg[:, 0:4096], in_=vr[:].rearrange("p a c -> p (a c)")).then_inc(sSY.h, 16)
                    e.dma_start(out=dbg[:, 4096:8192], in_=gsil[:].rearrange("p a c -> p (a c)")).then_inc(sSY.h, 16)
                    e.wait_ge(sSY.h, n0 + 32)
                P.on("sync", dump22)
                sSY.n += 32
            if DEBUG_STAGE == 23:
                P.wait("sync", sDV, ("epat", 7))

                n0c = sSY.n

                def dump23(e, n0=n0c):
                    e.dma_start(out=dbg[:, 0:4096], in_=attno[:].rearrange("p a c -> p (a c)")).then_inc(sSY.h, 16)
                    e.wait_ge(sSY.h, n0 + 16)
                P.on("sync", dump23)
                sSY.n += 16
            if DEBUG_STAGE in (20, 21, 22, 23):
                for s_ in (sSY, sGP, sCC, sPE, sDV, sSC):
                    P.mark(("b2", s_.name), s_)
                blk2 = e2(nc.Block("blk2"))
                P.emit(blk2)
                return nc

            P.wait("gpsimd", sGP, ("a2aw", 3))

            def a2a(e):
                with allow_ici_all_to_all():
                    e.collective_compute("AllToAll", ALU.bypass,
                                         replica_groups=[list(range(NC))],
                                         ins=[a2ain[:].opt()], outs=[a2aout[:].opt()]).then_inc(sCC.h, 1)
            P.on("gpsimd", a2a)
            sCC.n += 1
            P.mark("a2a", sCC)

            if DEBUG_STAGE == 2:
                P.wait("sync", sCC, "a2a")
                base0 = sSY.n

                def dump2(e, base0=base0):
                    n = base0
                    bt_ = xlnT[:, 0, :].squeeze(1) if len(xlnT[:, 0, :].shape) == 3 else xlnT[:, 0, :]
                    bt16 = bt_[:, 0:2048].rearrange("p (tt c) -> p tt c", c=128)
                    bt8 = xlnT[0:8, 1, 0:128]
                    for r in range(NC):
                        e.dma_start(out=bt16,
                                    in_=a2aout[HR * r:HR * r + OWN, :].rearrange(
                                        "(tt p) c -> p tt c", p=128)).then_inc(sSY.h, 16)
                        e.dma_start(out=bt8,
                                    in_=a2aout[HR * r + OWN:HR * r + OWN + 8, :]).then_inc(sSY.h, 16)
                        n += 32
                        e.wait_ge(sSY.h, n)
                        e.dma_start(out=dbg[HR * r:HR * r + OWN, :].rearrange(
                                        "(tt p) c -> p tt c", p=128),
                                    in_=bt16).then_inc(sSY.h, 16)
                        e.dma_start(out=dbg[HR * r + OWN:HR * r + OWN + 8, :],
                                    in_=bt8).then_inc(sSY.h, 16)
                        n += 32
                        e.wait_ge(sSY.h, n)
                P.on("sync", dump2)
                sSY.n += 4 * NC * 16

            for s_ in (sSY, sGP, sCC, sPE, sDV, sSC, sX0, sX1, sW0, sW1, sW2, sW3, sS0, sS1, sO0, sO1):
                P.mark(("b2", s_.name), s_)
            blk2 = e2(nc.Block("blk2"))
            P.emit(blk2)
            M2 = dict(P.M)

        if DEBUG_STAGE == 2:
            return nc

        with ExitStack() as c3:
            e3 = c3.enter_context
            xrT = e3(nc.sbuf_tensor("xrT", [128, 8, HRP], BF16))
            invc = e3(nc.sbuf_tensor("invc", [128, 17], F32))
            invrow = e3(nc.sbuf_tensor("invrow", [1, HRP], F32))
            inv1b = e3(nc.sbuf_tensor("inv1b", [128, HRP], F32))
            c3a = ExitStack()
            e3a = c3a.enter_context
            attnf = e3a(nc.sbuf_tensor("attnf", [128, NTIL, E], BF16))
            xa3 = e3a(nc.sbuf_tensor("xa3", [128, 2, E], F32))
            x1b = e3a(nc.sbuf_tensor("x1b", [128, 2, E], F32))
            xc3 = e3a(nc.sbuf_tensor("xc3", [128, 2, E], F32))
            xln2 = e3a(nc.sbuf_tensor("xln2", [128, 2, E], F32))
            sq3 = e3a(nc.sbuf_tensor("sq3", [128, E], F32))
            xs3 = e3a(nc.sbuf_tensor("xs3", [128, E], F32))
            xi3 = e3a(nc.sbuf_tensor("xi3", [128, E], I32))
            xrb = e3a(nc.sbuf_tensor("xrb", [128, 2, E], BF16))
            pXT0 = e3a(nc.psum_tensor("pXT0", [128, 1024], BF16))
            pXT1 = e3a(nc.psum_tensor("pXT1", [128, 1024], BF16))
            pBCa = e3a(nc.psum_tensor("pBCa", [128, 512], F32))
            PXT = (pXT0, pXT1)

            P = Prog(M2)
            for eng_ in Prog.ENGINES:
                for s_ in (sSY, sGP, sPE, sDV, sSC, sX0, sX1, sW0, sW1, sW2, sW3, sS0, sS1, sO0, sO1):
                    P.wait(eng_, s_, ("b2", s_.name))

            def af16(e):
                e.memset(attnf[:, 16, :], 0.0).then_inc(sDV.h, 1)
            P.on("vector", af16)
            sDV.n += 1
            P.mark("af16", sDV)

            P.wait("gpsimd", sCC, "a2a")
            P.wait("gpsimd", sDV, "af16")

            def attl(e):
                for r in range(NC):
                    e.dma_start(
                        out=attnf[:, 0:16, 128 * r:128 * (r + 1)],
                        in_=a2aout[HR * r:HR * r + OWN, :].rearrange("(tt p) c -> p tt c", p=128)).then_inc(sGP.h, 16)
                    e.dma_start(out=attnf[0:8, 16, 128 * r:128 * (r + 1)],
                                in_=a2aout[HR * r + OWN:HR * r + OWN + 8, :]).then_inc(sGP.h, 16)
            P.on("gpsimd", attl)
            sGP.n += 16 * 16
            P.mark("attl", sGP)


            for t in range(NTIL):
                sl = XSL[t % 2]
                if t >= 2:
                    P.wait("sync", sDV, ("qmul", t - 2))

                def xl3(e, t=t):
                    e.dma_start(out=_sl3(xa3, t % 2, 0, E),
                                in_=xh[128 * t:128 * (t + 1), :]).then_inc(XSL[t % 2].h, 16)
                P.on("sync", xl3)
                sl.n += 16

                P.wait("vector", sl, sl.n)
                P.wait("vector", sGP, "attl")
                if t >= 2:
                    P.wait("vector", SSL[t % 2], ("x1st", t - 2))

                def x1a(e, t=t):
                    e.tensor_tensor(_sl3(x1b, t % 2, 0, E), _sl3(xa3, t % 2, 0, E),
                                    attnf[:, t, :], op=ALU.add).then_inc(sDV.h, 1)
                P.on("vector", x1a)
                sDV.n += 1
                P.mark(("x1", t), sDV)

                P.wait("sync", sDV, ("x1", t))

                def x1st(e, t=t):
                    e.dma_start(out=x1_d[128 * t:128 * (t + 1), :],
                                in_=_sl3(x1b, t % 2, 0, E)).then_inc(SSL[t % 2].h, 16)
                P.on("sync", x1st)
                SSL[t % 2].n += 16
                P.M[("x1st", t)] = SSL[t % 2].n

                co = 8 * (t % 2)
                P.wait("vector", sDV, ("x1", t))

                def l2a(e, t=t, co=co):
                    e.reduce_sum(cB[:, co:co + 1], _sl3(x1b, t % 2, 0, E), axis=AX).then_inc(sDV.h, 1)
                P.on("vector", l2a)
                sDV.n += 1
                P.mark(("l2sum", t), sDV)

                P.wait("scalar", sDV, ("l2sum", t))

                def l2b(e, co=co):
                    e.mul(out=cA[:, co:co + 1], in_=cB[:, co:co + 1], mul=1.0 / E).then_inc(sSC.h, 1)
                P.on("scalar", l2b)
                sSC.n += 1
                P.mark(("l2m", t), sSC)

                P.wait("vector", sSC, ("l2m", t))
                if t >= 2:
                    P.wait("vector", sSC, ("l2nrm", t - 2))

                def l2c(e, t=t, co=co):
                    e.tensor_scalar(out=_sl3(xc3, t % 2, 0, E), in0=_sl3(x1b, t % 2, 0, E),
                                    scalar1=cA[:, co:co + 1], scalar2=None,
                                    op0=ALU.subtract).then_inc(sDV.h, 1)
                P.on("vector", l2c)
                sDV.n += 1
                P.mark(("l2xc", t), sDV)

                P.wait("scalar", sDV, ("l2xc", t))

                acc_n = sSC.n + 1

                def l2d(e, t=t, co=co, acc_n=acc_n):
                    e.activation(out=sq3[:], in_=_sl3(xc3, t % 2, 0, E), func=AF.Square,
                                 accum_out=cA[:, co + 1:co + 2]).then_inc(sSC.h, 1)
                    e.wait_ge(sSC.h, acc_n)
                    e.activation(out=cA[:, co + 2:co + 3], in_=cA[:, co + 1:co + 2], func=AF.Sqrt,
                                 scale=1.0 / E, bias=EPS_LN).then_inc(sSC.h, 1)
                P.on("scalar", l2d)
                sSC.n += 2
                P.mark(("l2srt", t), sSC)

                P.wait("vector", sSC, ("l2srt", t))

                def l2e(e, co=co):
                    e.reciprocal(cB[:, co + 2:co + 3], cA[:, co + 2:co + 3]).then_inc(sDV.h, 1)
                P.on("vector", l2e)
                sDV.n += 1
                P.mark(("l2rstd", t), sDV)

                P.wait("scalar", sDV, ("l2rstd", t))
                if t >= 2:
                    P.wait("scalar", sDV, ("qmul", t - 2))

                def l2f(e, t=t, co=co):
                    e.activation(out=_sl3(xln2, t % 2, 0, E), in_=_sl3(xc3, t % 2, 0, E),
                                 func=AF.Copy, scale=cB[:, co + 2:co + 3]).then_inc(sSC.h, 1)
                P.on("scalar", l2f)
                sSC.n += 1
                P.mark(("l2nrm", t), sSC)

                P.wait("vector", sSC, ("l2nrm", t))

                def q1(e, t=t, co=co):
                    e.reduce_max(cB[:, co + 3:co + 4], _sl3(xln2, t % 2, 0, E), axis=AX,
                                 apply_absolute_value=True).then_inc(sDV.h, 1)
                P.on("vector", q1)
                sDV.n += 1
                P.mark(("qmx", t), sDV)

                P.wait("scalar", sDV, ("qmx", t))

                base_n = sSC.n

                def q2(e, t=t, co=co, base_n=base_n):
                    e.activation(out=cA[:, co + 3:co + 4], in_=cB[:, co + 3:co + 4], func=AF.Relu,
                                 bias=-1e-5).then_inc(sSC.h, 1)
                    e.wait_ge(sSC.h, base_n + 1)
                    e.add(out=cA[:, co + 3:co + 4], in_=cA[:, co + 3:co + 4],
                          add=1e-5).then_inc(sSC.h, 1)
                    e.wait_ge(sSC.h, base_n + 2)
                    e.mul(out=invc[:, t:t + 1], in_=cA[:, co + 3:co + 4],
                          mul=1.0 / 127.0).then_inc(sSC.h, 1)
                P.on("scalar", q2)
                sSC.n += 3
                P.mark(("qmc", t), sSC)

                P.wait("vector", sSC, ("qmc", t))

                def q3(e, co=co):
                    e.reciprocal(cB[:, co + 4:co + 5], cA[:, co + 3:co + 4]).then_inc(sDV.h, 1)
                P.on("vector", q3)
                sDV.n += 1
                P.mark(("qrec", t), sDV)

                P.wait("scalar", sDV, ("qrec", t))

                def q4(e, co=co):
                    e.mul(out=cA[:, co + 5:co + 6], in_=cB[:, co + 4:co + 5],
                          mul=127.0).then_inc(sSC.h, 1)
                P.on("scalar", q4)
                sSC.n += 1
                P.mark(("qs127", t), sSC)

                P.wait("vector", sSC, ("qs127", t))
                if t >= 2:
                    P.wait("vector", sPE, ("xtr", t - 2))

                acc_n = sDV.n

                def q5(e, t=t, co=co, acc_n=acc_n):
                    e.wait_ge(sDV.h, acc_n)
                    e.tensor_scalar(out=xs3[:], in0=_sl3(xln2, t % 2, 0, E),
                                    scalar1=cA[:, co + 5:co + 6], scalar2=None,
                                    op0=ALU.mult).then_inc(sDV.h, 1)
                    e.wait_ge(sDV.h, acc_n + 1)
                    e.tensor_scalar(out=xi3[:], in0=xs3[:], scalar1=-1.0, scalar2=None,
                                    op0=ALU.add).then_inc(sDV.h, 1)
                    e.wait_ge(sDV.h, acc_n + 2)
                    e.tensor_tensor(xs3[:], xs3[:], xi3[:], op=ALU.subtract).then_inc(sDV.h, 1)
                    e.wait_ge(sDV.h, acc_n + 3)
                    e.scalar_tensor_tensor(out=xi3[:], in0=xs3[:], scalar=1.5, in1=xi3[:],
                                           op0=ALU.is_gt, op1=ALU.add).then_inc(sDV.h, 1)
                    e.wait_ge(sDV.h, acc_n + 4)
                    e.scalar_tensor_tensor(out=_sl3(xrb, t % 2, 0, E), in0=xs3[:],
                                           scalar=0.5, in1=xi3[:],
                                           op0=ALU.is_ge, op1=ALU.add).then_inc(sDV.h, 1)
                P.on("vector", q5)
                sDV.n += 5
                P.mark(("qmul", t), sDV)

                P.wait("tensor", sDV, ("qmul", t))
                if t >= 2:
                    P.wait("tensor", sDV, ("xband", t - 2))

                def xtr(e, t=t):
                    pb = PXT[t % 2]
                    for ec in range(8):
                        d = e.transpose(pb[:, 128 * ec:128 * (ec + 1)],
                                        xrb[:, t % 2, 128 * ec:128 * (ec + 1)], identb_t[:])
                    d.then_inc(sPE.h, 1)
                P.on("tensor", xtr)
                sPE.n += 1
                P.mark(("xtr", t), sPE)

                P.wait("vector", sPE, ("xtr", t))

                def xband(e, t=t):
                    pb = PXT[t % 2]
                    e.tensor_copy(xrT[:, :, 128 * t:128 * (t + 1)],
                                  pb[:].rearrange("p (ec c) -> p ec c", c=128)).then_inc(sDV.h, 1)
                P.on("vector", xband)
                sDV.n += 1
                P.mark(("xband", t), sDV)

            grp = [(0, 4), (4, 8), (8, 12), (12, 16), (16, 17)]
            for gix, (a0, b0) in enumerate(grp):
                P.wait("tensor", sSC, ("qmc", b0 - 1))
                if gix >= 1:
                    P.wait("tensor", sDV, ("inve", gix - 1))

                def invtr(e, a0=a0, b0=b0):
                    for t in range(a0, b0):
                        d = e.transpose(pBCa[0:1, 128 * (t - a0):128 * (t - a0) + 128],
                                        invc[:, t:t + 1], identf_t[:])
                    d.then_inc(sPE.h, 1)
                P.on("tensor", invtr)
                sPE.n += 1
                P.mark(("invtr", gix), sPE)

                P.wait("vector", sPE, ("invtr", gix))

                def inve(e, a0=a0, b0=b0):
                    for t in range(a0, b0):
                        d = e.tensor_copy(invrow[0:1, 128 * t:128 * (t + 1)],
                                          pBCa[0:1, 128 * (t - a0):128 * (t - a0) + 128])
                    d.then_inc(sDV.h, 1)
                P.on("vector", inve)
                sDV.n += 1
                P.mark(("inve", gix), sDV)

            nch = (HRP + 511) // 512
            for cc in range(nch):
                w = min(512, HRP - 512 * cc)
                P.wait("tensor", sDV, ("inve", len(grp) - 1))
                if cc >= 1:
                    P.wait("tensor", sDV, ("bce", cc - 1))

                def bcm(e, cc=cc, w=w):
                    e.matmul(pBCa[:, 0:w], onesr_t[:], invrow[0:1, 512 * cc:512 * cc + w],
                             start=True, stop=True).then_inc(sPE.h, 1)
                P.on("tensor", bcm)
                sPE.n += 1
                P.mark(("bcm", cc), sPE)

                P.wait("vector", sPE, ("bcm", cc))

                def bce(e, cc=cc, w=w):
                    e.tensor_copy(inv1b[:, 512 * cc:512 * cc + w], pBCa[:, 0:w]).then_inc(sDV.h, 1)
                P.on("vector", bce)
                sDV.n += 1
                P.mark(("bce", cc), sDV)

            for s_ in (sSY, sGP, sCC, sPE, sDV, sSC, sX0, sX1, sW0, sW1, sW2, sW3, sS0, sS1, sO0, sO1):
                P.mark(("b3a", s_.name), s_)
            blk3a = e3a(nc.Block("blk3a"))
            P.emit(blk3a)
            M3a = dict(P.M)
            c3a.close()

            w1t = e3(nc.sbuf_tensor("w1t", [128, 2, 8, 128], BF16))
            hbuf = e3(nc.sbuf_tensor("hbuf", [128, 2, 514], F32))
            cv1 = e3(nc.sbuf_tensor("cv1", [128, 512], F32))
            cv2 = e3(nc.sbuf_tensor("cv2", [128, 512], F32))
            h2q = e3(nc.sbuf_tensor("h2q", [128, 32, 512], BF16))
            macc = e3(nc.sbuf_tensor("macc", [128, 512], F32))
            s2b = e3(nc.sbuf_tensor("s2b", [128, 512], F32))
            hs3 = e3(nc.sbuf_tensor("hs3", [128, 512], F32))
            hi3 = e3(nc.sbuf_tensor("hi3", [128, 512], I32))
            hs3g = e3(nc.sbuf_tensor("hs3g", [128, 512], F32))
            hi3g = e3(nc.sbuf_tensor("hi3g", [128, 512], I32))
            invc2s = e3(nc.sbuf_tensor("invc2s", [128, 4], F32))
            w2sb = e3(nc.sbuf_tensor("w2sb", [128, 32, E], BF16))
            x1rd = e3(nc.sbuf_tensor("x1rd", [128, 2, E], F32))
            obuf = e3(nc.sbuf_tensor("obuf", [128, 2, E], F32))
            pF10 = e3(nc.psum_tensor("pF10", [128, 512], F32))
            pF11 = e3(nc.psum_tensor("pF11", [128, 512], F32))
            pF20 = e3(nc.psum_tensor("pF20", [128, 512], F32))
            pF21 = e3(nc.psum_tensor("pF21", [128, 512], F32))
            pBC = e3(nc.psum_tensor("pBC", [128, 512], F32))
            PF1 = (pF10, pF11)
            PF2 = (pF20, pF21)

            P = Prog(M3a)
            for eng_ in Prog.ENGINES:
                for s_ in (sSY, sGP, sPE, sDV, sSC, sX0, sX1, sW0, sW1, sW2, sW3, sS0, sS1, sO0, sO1):
                    P.wait(eng_, s_, ("b3a", s_.name))

            def w2l(e):
                e.dma_start(out=w2sb[:],
                            in_=w2qT_d[:].rearrange("(ch p) c -> p ch c", p=128)).then_inc(sSY.h, 16)
            P.on("sync", w2l)
            sSY.n += 16
            P.mark("w2l", sSY)

            for q in range(4):
                hlo = 512 * q

                if q > 0:
                    P.wait("vector", sPE, ("mtr", 4 * q - 1))

                def mz(e):
                    e.memset(macc[:], 0.0).then_inc(sDV.h, 1)
                P.on("vector", mz)
                sDV.n += 1
                P.mark(("mz", q), sDV)

                for cht in range(32):
                    ci = 32 * q + cht
                    sl = WSL[ci % 2]
                    if ci >= 2:
                        P.wait("sync", sPE, ("f1", ci - 2))

                    def w1l(e, ci=ci, cht=cht):
                        e.dma_start(out=w1t[:, ci % 2, :, :],
                                    in_=w1qT_d[:, 128 * cht:128 * (cht + 1)].rearrange(
                                        "(ec p) c -> p ec c", p=128)).then_inc(WSL[ci % 2].h, 16)
                    P.on("sync", w1l)
                    sl.n += 16

                    P.wait("tensor", sl, sl.n)
                    P.wait("tensor", sDV, ("xband", NTIL - 1))
                    if ci == 0:
                        P.wait("tensor", sDV, ("bce", nch - 1))
                    elif cht == 0:
                        P.wait("tensor", sDV, ("s2e", 4 * q - 1))
                    if ci >= 2:
                        P.wait("tensor", sDV, ("f1e", ci - 2))
                    if ci >= 1:
                        P.wait("tensor", sDV, ("f1e", ci - 1))

                    def f1(e, ci=ci, hlo=hlo):
                        pp = PF1[ci % 2]
                        for ec in range(8):
                            d = e.matmul(pp[:], _sl3(w1t[:, ci % 2], ec, 0, 128),
                                         xrT[:, ec, hlo:hlo + 512],
                                         start=(ec == 0), stop=(ec == 7))
                        for ec in range(8):
                            d = e.matmul(pBC[:, 0:2], _sl3(w1t[:, ci % 2], ec, 0, 128),
                                         xrT[:, ec, hlo + 512:hlo + 514],
                                         start=(ec == 0), stop=(ec == 7))
                        d.then_inc(sPE.h, 1)
                    P.on("tensor", f1)
                    sPE.n += 1
                    P.mark(("f1", ci), sPE)

                    P.wait("vector", sPE, ("f1", ci))
                    P.wait("vector", sDV, ("mz", q))
                    if ci >= 2:
                        P.wait("vector", sSC, ("silu", ci - 2))

                    base_n = sDV.n
                    nmask = (1 if q == 0 else 0) + (1 if q == 3 else 0)

                    def f1e(e, q=q, ci=ci, hlo=hlo, base_n=base_n):
                        n = base_n
                        pp = PF1[ci % 2]
                        e.tensor_tensor(_sl3(hbuf, ci % 2, 0, 512), pp[:],
                                        inv1b[:, hlo:hlo + 512], op=ALU.mult).then_inc(sDV.h, 1)
                        e.tensor_tensor(_sl3(hbuf, ci % 2, 512, 514), pBC[:, 0:2],
                                        inv1b[:, hlo + 512:hlo + 514], op=ALU.mult).then_inc(sDV.h, 1)
                        n += 2
                        e.wait_ge(sDV.h, n)
                        if q == 0:
                            e.tensor_scalar(out=_sl3(hbuf, ci % 2, 0, 1),
                                            in0=_sl3(hbuf, ci % 2, 0, 1),
                                            scalar1=maskL_t[:], scalar2=None,
                                            op0=ALU.mult).then_inc(sDV.h, 1)
                            n += 1
                            e.wait_ge(sDV.h, n)
                        if q == 3:
                            e.tensor_scalar(out=_sl3(hbuf, ci % 2, 513, 514),
                                            in0=_sl3(hbuf, ci % 2, 513, 514),
                                            scalar1=maskR_t[:], scalar2=None,
                                            op0=ALU.mult).then_inc(sDV.h, 1)
                            n += 1
                            e.wait_ge(sDV.h, n)
                        ch = ci % 32
                        w0 = tapsp[:, 3 * ch:3 * ch + 1]
                        w1_ = tapsp[:, 3 * ch + 1:3 * ch + 2]
                        w2_ = tapsp[:, 3 * ch + 2:3 * ch + 3]
                        bcol = convb_t[:, ch:ch + 1]
                        e.scalar_tensor_tensor(out=cv1[:], in0=hbuf[:, ci % 2, 2:514], scalar=w2_,
                                               in1=bcol.broadcast_to([128, 512]),
                                               op0=ALU.mult, op1=ALU.add).then_inc(sDV.h, 1)
                        n += 1
                        e.wait_ge(sDV.h, n)
                        e.scalar_tensor_tensor(out=cv2[:], in0=hbuf[:, ci % 2, 1:513], scalar=w1_,
                                               in1=cv1[:], op0=ALU.mult, op1=ALU.add).then_inc(sDV.h, 1)
                        n += 1
                        e.wait_ge(sDV.h, n)
                        e.scalar_tensor_tensor(out=cv1[:], in0=hbuf[:, ci % 2, 0:512], scalar=w0,
                                               in1=cv2[:], op0=ALU.mult, op1=ALU.add).then_inc(sDV.h, 1)
                    P.on("vector", f1e)
                    sDV.n += 5 + nmask
                    P.mark(("f1e", ci), sDV)

                    P.wait("scalar", sDV, ("f1e", ci))
                    if q > 0 and cht == 0:
                        P.wait("scalar", sPE, ("f2", 4 * q - 1, 1))

                    def silu(e, cht=cht):
                        e.activation(out=h2q[:, cht, :], in_=cv1[:], func=AF.Silu).then_inc(sSC.h, 1)
                    P.on("scalar", silu)
                    sSC.n += 1
                    P.mark(("silu", ci), sSC)

                    P.wait("vector", sSC, ("silu", ci))

                    base_n = sDV.n

                    def mupd(e, cht=cht, base_n=base_n):
                        e.scalar_tensor_tensor(out=macc[:], in0=h2q[:, cht, :], scalar=1.0,
                                               in1=macc[:], op0=ALU.mult,
                                               op1=ALU.max).then_inc(sDV.h, 1)
                        e.wait_ge(sDV.h, base_n + 1)
                        e.scalar_tensor_tensor(out=macc[:], in0=h2q[:, cht, :], scalar=-1.0,
                                               in1=macc[:], op0=ALU.mult,
                                               op1=ALU.max).then_inc(sDV.h, 1)
                    P.on("vector", mupd)
                    sDV.n += 2
                    P.mark(("mupd", ci), sDV)

                for tt in range(4):
                    ot = 4 * q + tt
                    P.wait("tensor", sDV, ("mupd", 32 * q + 31))
                    P.wait("tensor", sDV, ("f1e", 32 * q + 31))
                    if ot >= 1:
                        P.wait("tensor", sDV, ("s2e", ot - 1))

                    def mtr(e, tt=tt):
                        e.transpose(pBC[:, 128:256], macc[:, 128 * tt:128 * (tt + 1)],
                                    identf_t[:]).then_inc(sPE.h, 1)
                    P.on("tensor", mtr)
                    sPE.n += 1
                    P.mark(("mtr", ot), sPE)

                    P.wait("vector", sPE, ("mtr", ot))

                    def mred(e):
                        e.reduce_max(cB[:, 20:21], pBC[:, 128:256], axis=AX).then_inc(sDV.h, 1)
                    P.on("vector", mred)
                    sDV.n += 1
                    P.mark(("mred", ot), sDV)

                    P.wait("scalar", sDV, ("mred", ot))
                    if q > 0 and tt == 0:
                        P.wait("scalar", sDV, ("oev", 4 * q - 1, 1))

                    base_n = sSC.n

                    def sc2(e, base_n=base_n):
                        e.activation(out=cA[:, 20:21], in_=cB[:, 20:21], func=AF.Relu,
                                     bias=-1e-5).then_inc(sSC.h, 1)
                        e.wait_ge(sSC.h, base_n + 1)
                        e.add(out=cA[:, 20:21], in_=cA[:, 20:21], add=1e-5).then_inc(sSC.h, 1)
                    P.on("scalar", sc2)
                    sSC.n += 2
                    P.mark(("mxc2", ot), sSC)

                    P.wait("vector", sSC, ("mxc2", ot))

                    def v2c(e):
                        e.tensor_scalar_mul(cB[:, 21:22], cA[:, 20:21], 1.0 / 127.0)
                        e.reciprocal(cB[:, 22:23], cA[:, 20:21]).then_inc(sDV.h, 1)
                    P.on("vector", v2c)
                    sDV.n += 1
                    P.mark(("v2c", ot), sDV)

                    P.wait("scalar", sDV, ("v2c", ot))
                    P.wait("scalar", sDV, "wm2")

                    def sc3(e, tt=tt):
                        e.activation(out=invc2s[:, tt:tt + 1], in_=m2w2c[:], func=AF.Copy,
                                     scale=cB[:, 21:22])
                        e.mul(out=cA[:, 21:22], in_=cB[:, 22:23], mul=127.0).then_inc(sSC.h, 1)
                    P.on("scalar", sc3)
                    sSC.n += 1
                    P.mark(("sc3", ot), sSC)

                    P.wait("tensor", sSC, ("sc3", ot))

                    def s2tr(e):
                        e.transpose(pBC[0:1, 256:384], cA[:, 21:22], identf_t[:]).then_inc(sPE.h, 1)
                    P.on("tensor", s2tr)
                    sPE.n += 1
                    P.mark(("s2tr", ot), sPE)

                    P.wait("vector", sPE, ("s2tr", ot))

                    def s2re(e):
                        e.tensor_copy(invrow[0:1, 0:128], pBC[0:1, 256:384]).then_inc(sDV.h, 1)
                    P.on("vector", s2re)
                    sDV.n += 1
                    P.mark(("s2r", ot), sDV)

                    P.wait("tensor", sDV, ("s2r", ot))

                    def s2bc(e):
                        e.matmul(pBC[:, 384:512], onesr_t[:], invrow[0:1, 0:128],
                                 start=True, stop=True).then_inc(sPE.h, 1)
                    P.on("tensor", s2bc)
                    sPE.n += 1
                    P.mark(("s2bc", ot), sPE)

                    P.wait("vector", sPE, ("s2bc", ot))

                    def s2e(e, tt=tt):
                        e.tensor_copy(s2b[:, 128 * tt:128 * (tt + 1)],
                                      pBC[:, 384:512]).then_inc(sDV.h, 1)
                    P.on("vector", s2e)
                    sDV.n += 1
                    P.mark(("s2e", ot), sDV)

                for cht in range(32):
                    ci = 32 * q + cht
                    eng = "vector"
                    sem = sDV
                    bufs = (hs3, hi3)
                    P.wait(eng, sDV, ("s2e", 4 * q + 3))

                    acc_n = sem.n

                    def hq(e, cht=cht, acc_n=acc_n, sem=sem, bufs=bufs):
                        hs_, hi_ = bufs
                        e.wait_ge(sem.h, acc_n)
                        e.tensor_tensor(hs_[:], h2q[:, cht, :], s2b[:],
                                        op=ALU.mult).then_inc(sem.h, 1)
                        e.wait_ge(sem.h, acc_n + 1)
                        e.tensor_scalar(out=hi_[:], in0=hs_[:], scalar1=-1.0, scalar2=None,
                                        op0=ALU.add).then_inc(sem.h, 1)
                        e.wait_ge(sem.h, acc_n + 2)
                        e.tensor_tensor(hs_[:], hs_[:], hi_[:], op=ALU.subtract).then_inc(sem.h, 1)
                        e.wait_ge(sem.h, acc_n + 3)
                        e.scalar_tensor_tensor(out=hi_[:], in0=hs_[:], scalar=1.5, in1=hi_[:],
                                               op0=ALU.is_gt, op1=ALU.add).then_inc(sem.h, 1)
                        e.wait_ge(sem.h, acc_n + 4)
                        e.scalar_tensor_tensor(out=h2q[:, cht, :], in0=hs_[:],
                                               scalar=0.5, in1=hi_[:],
                                               op0=ALU.is_ge, op1=ALU.add).then_inc(sem.h, 1)
                    P.on(eng, hq)
                    sem.n += 5
                    P.mark(("hq", ci), sem)

                for tt in range(4):
                    ot = 4 * q + tt
                    r0 = 1 + 512 * q + 128 * tt
                    if ot >= 2:
                        P.wait("sync", sDV, ("oev", ot - 2, 1))

                    def x1l(e, ot=ot, r0=r0):
                        e.dma_start(out=_sl3(x1rd, ot % 2, 0, E),
                                    in_=x1_d[r0:r0 + 128, :]).then_inc(XSL[ot % 2].h, 16)
                    P.on("sync", x1l)
                    XSL[ot % 2].n += 16
                    P.M[("x1l", ot)] = XSL[ot % 2].n

                    for eo in range(2):
                        gstep = 2 * ot + eo
                        P.wait("tensor", sDV, ("hq", 32 * q + 31))
                        P.wait("tensor", sSY, "w2l")
                        if gstep >= 2:
                            P.wait("tensor", sDV, ("oev", (gstep - 2) // 2, (gstep - 2) % 2))

                        def f2(e, q=q, tt=tt, eo=eo):
                            pp = PF2[eo]
                            for cht in range(32):
                                d = e.matmul(pp[:], h2q[:, cht, 128 * tt:128 * (tt + 1)],
                                             w2sb[:, cht, 512 * eo:512 * (eo + 1)],
                                             start=(cht == 0), stop=(cht == 31))
                            d.then_inc(sPE.h, 1)
                        P.on("tensor", f2)
                        sPE.n += 1
                        P.mark(("f2", ot, eo), sPE)

                        P.wait("vector", sPE, ("f2", ot, eo))
                        P.wait("vector", XSL[ot % 2], ("x1l", ot))
                        P.wait("vector", sSC, ("sc3", ot))
                        if ot >= 2 and eo == 0:
                            P.wait("vector", OSL[ot % 2], ("ost", ot - 2))

                        def oev(e, ot=ot, tt=tt, eo=eo):
                            pp = PF2[eo]
                            e.scalar_tensor_tensor(
                                out=obuf[:, ot % 2, 512 * eo:512 * (eo + 1)], in0=pp[:],
                                scalar=invc2s[:, tt:tt + 1],
                                in1=x1rd[:, ot % 2, 512 * eo:512 * (eo + 1)],
                                op0=ALU.mult, op1=ALU.add).then_inc(sDV.h, 1)
                        P.on("vector", oev)
                        sDV.n += 1
                        P.mark(("oev", ot, eo), sDV)

                    P.wait("sync", sDV, ("oev", ot, 1))

                    def ost(e, q=q, tt=tt, ot=ot):
                        e.dma_start(out=out[512 * q + 128 * tt:512 * q + 128 * (tt + 1), :],
                                    in_=_sl3(obuf, ot % 2, 0, E)).then_inc(OSL[ot % 2].h, 16)
                    P.on("sync", ost)
                    OSL[ot % 2].n += 16
                    P.M[("ost", ot)] = OSL[ot % 2].n

            P.wait("sync", sO0, ("ost", 14))
            P.wait("sync", sO1, ("ost", 15))

            blk3 = e3(nc.Block("blk3b"))
            P.emit(blk3)

    return nc


def kernel(L0_attn_x, L0_attn_Wq, L0_attn_Wk, L0_attn_Wv, L0_attn_Wg,
           L0_ffn_w1, L0_ffn_conv_w, L0_ffn_conv_b, L0_ffn_w2):
    xf = np.ascontiguousarray(np.asarray(L0_attn_x, dtype=np.float32).reshape(TOK, E))
    args = [np.ascontiguousarray(np.asarray(a, dtype=np.float32)) for a in
            (L0_attn_Wq, L0_attn_Wk, L0_attn_Wv, L0_attn_Wg, L0_ffn_w1,
             L0_ffn_conv_w, L0_ffn_conv_b, L0_ffn_w2)]
    in_maps = [_stage_inputs(c, xf, *args) for c in range(NC)]
    nc = build_program()
    res = run_bass_kernel_spmd(nc, in_maps, core_ids=list(range(NC)))
    if DEBUG_STAGE:
        return [r["dbg"] for r in res.results]
    return np.concatenate([res.results[c]["o"] for c in range(NC)], axis=0).reshape(B, S, E)


# version 21 (final)
# speedup vs baseline: 1.0579x; 1.0062x over previous
import heapq
import math
import numpy as np
import ml_dtypes
import concourse.bass as bass
import concourse.mybir as mybir
from concourse.bass_utils import run_bass_kernel_spmd
from concourse.aot_env import allow_ici_all_to_all
from contextlib import ExitStack

F32 = mybir.dt.float32
BF16 = mybir.dt.bfloat16
I32 = mybir.dt.int32
I16 = mybir.dt.int16
AF = mybir.ActivationFunctionType
ALU = mybir.AluOpType
AX = mybir.AxisListType.X

NC = 8
B, S, E, H, D = 4, 4096, 1024, 16, 64
TOK = B * S
OWN = TOK // NC
HR = OWN + 8
NTIL = 17
HRP = NTIL * 128
EPS_LN = 1e-5
EPS_RMS = 1e-6
SCALING = float(E) ** -0.5

DEBUG_STAGE = 0



def _decay(h):
    return math.log(1.0 - 2.0 ** (-5.0 - h))


def _host_tables(core):
    t = {}
    d0, d1 = _decay(2 * core), _decay(2 * core + 1)

    ang = 1.0 / 10000.0 ** np.linspace(0.0, 1.0, D // 2)
    angle = np.repeat(ang, 2)
    idx = np.arange(S, dtype=np.float64)
    sinT = np.sin(idx[:, None] * angle[None, :]).T
    cosT = np.cos(idx[:, None] * angle[None, :]).T

    kk = np.arange(S, dtype=np.float64)
    kloc = kk - 128.0 * np.floor(kk / 128.0)
    pat0 = np.exp(-d0 * kloc)
    pat1 = np.exp(-d1 * kloc)
    t["tk_cos"] = (np.concatenate([cosT * pat0, cosT * pat1], 0) * SCALING).astype(ml_dtypes.bfloat16)
    t["tk_sin"] = (np.concatenate([sinT * pat0, sinT * pat1], 0) * SCALING).astype(ml_dtypes.bfloat16)
    t["tq_cos"] = np.concatenate([cosT, cosT], 0).astype(ml_dtypes.bfloat16)
    t["tq_sin"] = np.concatenate([sinT, sinT], 0).astype(ml_dtypes.bfloat16)

    bt = np.zeros((2, 8, 32), dtype=np.float64)
    for hh, dd in ((0, d0), (1, d1)):
        for qt in range(8):
            for kt in range(4 * qt + 4):
                bt[hh, qt, kt] = np.exp(dd * (512.0 * qt - 128.0 * kt))
    t["btab"] = np.broadcast_to(bt.reshape(1, 512), (128, 512)).astype(np.float32).copy()

    q = np.arange(S, dtype=np.float64)
    qt0 = 512.0 * np.floor(q / 512.0)
    grow = np.zeros((1, 2 * S), dtype=np.float64)
    for hh, dd in ((0, d0), (1, d1)):
        r = np.exp(dd)
        rowsum = (1.0 - r ** (q + 1.0)) / (1.0 - r)
        val = (np.exp(dd * (q - qt0)) / np.sqrt(rowsum)).reshape(8, 512)
        for qt in range(8):
            grow[0, 1024 * qt + 512 * hh:1024 * qt + 512 * hh + 512] = val[qt]
    t["grow"] = grow.astype(np.float32)

    tril = np.zeros((128, 4, 512), dtype=np.float32)
    kp = np.arange(128)[:, None]
    qf = np.arange(512)[None, :]
    for ri, rel in enumerate((0, 128, 256, 384)):
        tril[:, ri, :] = (kp + rel <= qf).astype(np.float32)
    t["trilt"] = np.ascontiguousarray(tril.reshape(128, 2048))

    t["maskL"] = np.full((128, 1), 0.0 if core % 2 == 0 else 1.0, np.float32)
    t["maskR"] = np.full((128, 1), 0.0 if core % 2 == 1 else 1.0, np.float32)
    return t


def _stage_inputs(core, xf, Wq, Wk, Wv, Wg, w1, conv_w, conv_b, w2):
    m = dict(_host_tables(core))
    lo = OWN * core
    rows = np.clip(np.arange(-1, HRP - 1) + lo, 0, TOK - 1)
    m["xh"] = np.ascontiguousarray(xf[rows])
    cols = slice(128 * core, 128 * core + 128)
    m["wq"] = np.ascontiguousarray(Wq[:, cols])
    m["wk"] = np.ascontiguousarray(Wk[:, cols])
    m["wvg"] = np.ascontiguousarray(np.concatenate([Wv[:, cols], Wg[:, cols]], axis=1))
    m["w1f"] = w1
    m["w2f"] = w2
    m["convw"] = np.ascontiguousarray(conv_w.reshape(32, 128, 3).transpose(1, 0, 2).reshape(128, 96))
    m["convb"] = np.ascontiguousarray(conv_b.reshape(32, 128).T)
    return m



class Sem:
    def __init__(self, ctx, nc, name):
        self.h = ctx.enter_context(nc.semaphore(name))
        self.n = 0
        self.name = name


class Prog:
    ENGINES = ("sync", "gpsimd", "tensor", "vector", "scalar")

    def __init__(self, markers=None):
        self.steps = {e: [] for e in self.ENGINES}
        self.M = dict(markers or {})

    def on(self, eng, fn):
        self.steps[eng].append(fn)

    def wait(self, eng, sem, key_or_val):
        M = self.M
        if isinstance(key_or_val, int):
            v = key_or_val
            self.on(eng, lambda e: e.wait_ge(sem.h, v))
        else:
            k = key_or_val
            self.on(eng, lambda e: e.wait_ge(sem.h, M[k]))

    def mark(self, key, sem):
        self.M[key] = sem.n

    def emit(self, block):
        for eng in self.ENGINES:
            steps = self.steps[eng]
            if not steps:
                continue

            def body(e, steps=steps):
                for fn in steps:
                    fn(e)

            getattr(block, eng)(body)


def _sl3(t, k, a, b):
    ap_ = t[:, k, a:b]
    if len(ap_.shape) == 3:
        ap_ = ap_.squeeze(1)
    return ap_


def _tile_end2(qt):
    return 4 * qt * qt + 12 * qt + 7


def _tile_end(h, qt):
    n = 0
    for hh in range(2):
        for qq in range(8):
            cnt = 4 * qq + 4
            if hh == h and qq == qt:
                return n + cnt - 1
            n += cnt
    raise AssertionError


def build_program():
    import os as _osx
    SKIP_AG = bool(_osx.environ.get("SKIP_AG")) and DEBUG_STAGE == 19
    SKIP_WQ = bool(_osx.environ.get("SKIP_WQ")) and DEBUG_STAGE == 19
    LN_TILES = int(_osx.environ.get("LN_TILES", "16")) if DEBUG_STAGE == 19 else 16
    SKIP_TR = bool(_osx.environ.get("SKIP_TR")) and DEBUG_STAGE == 19
    nc = bass.Bass("TRN2", target_bir_lowering=False)

    ap = lambda name, shape, dt, kind: nc.dram_tensor(name, shape, dt, kind=kind).ap()
    xh = ap("xh", [HRP, E], F32, "ExternalInput")
    wq = ap("wq", [E, 128], F32, "ExternalInput")
    wk = ap("wk", [E, 128], F32, "ExternalInput")
    wvg = ap("wvg", [E, 256], F32, "ExternalInput")
    w1f = ap("w1f", [4096, E], F32, "ExternalInput")
    w2f = ap("w2f", [E, 4096], F32, "ExternalInput")
    convw = ap("convw", [128, 96], F32, "ExternalInput")
    convb = ap("convb", [128, 32], F32, "ExternalInput")
    tq_cos = ap("tq_cos", [128, S], BF16, "ExternalInput")
    tq_sin = ap("tq_sin", [128, S], BF16, "ExternalInput")
    tk_cos = ap("tk_cos", [128, S], BF16, "ExternalInput")
    tk_sin = ap("tk_sin", [128, S], BF16, "ExternalInput")
    btab = ap("btab", [128, 512], F32, "ExternalInput")
    grow = ap("grow", [1, 2 * S], F32, "ExternalInput")
    trilt = ap("trilt", [128, 2048], F32, "ExternalInput")
    maskL = ap("maskL", [128, 1], F32, "ExternalInput")
    maskR = ap("maskR", [128, 1], F32, "ExternalInput")
    out = ap("o", [OWN, E], F32, "ExternalOutput")
    dbg = None
    if DEBUG_STAGE == 1:
        dbg = ap("dbg", [NC * E, OWN], BF16, "ExternalOutput")
    elif DEBUG_STAGE == 2:
        dbg = ap("dbg", [NC * HR, 128], BF16, "ExternalOutput")
    elif DEBUG_STAGE in (19, 20, 21, 22, 23):
        dbg = ap("dbg", [128, 8192], BF16, "ExternalOutput")
    elif DEBUG_STAGE == 24:
        dbg = ap("dbg", [128, 8704], F32, "ExternalOutput")

    ag1in = nc.dram_tensor("ag1in", [E, OWN], BF16).ap()
    ag1out = nc.dram_tensor("ag1out", [NC * E, OWN], BF16, addr_space="Shared").ap()
    a2ain = nc.dram_tensor("a2ain", [NC * HR, 128], BF16).ap()
    a2aout = nc.dram_tensor("a2aout", [NC * HR, 128], BF16).ap()
    w1qT_d = nc.dram_tensor("w1qT_d", [E, 4096], BF16).ap()
    w2qT_d = nc.dram_tensor("w2qT_d", [4096, E], BF16).ap()
    x1_d = nc.dram_tensor("x1_d", [HRP, E], F32).ap()

    ident_bf = nc.inline_tensor(np.eye(128, dtype=ml_dtypes.bfloat16), "ident_bf").ap()
    ident_f = nc.inline_tensor(np.eye(128, dtype=np.float32), "ident_f").ap()
    r2m = np.zeros((128, 128), dtype=np.float64)
    for j in range(64):
        r2m[2 * j, 2 * j + 1] = -1.0
        r2m[2 * j + 1, 2 * j] = 1.0
    r2t = nc.inline_tensor(np.ascontiguousarray(r2m.T).astype(ml_dtypes.bfloat16), "r2t").ap()
    ones_bf = nc.inline_tensor(np.ones((128, 1), dtype=ml_dtypes.bfloat16), "ones_bf").ap()
    onesrow_f = nc.inline_tensor(np.ones((1, 128), dtype=np.float32), "onesrow_f").ap()
    allones_f = nc.inline_tensor(np.ones((128, 128), dtype=np.float32), "allones_f").ap()
    onescell_f = nc.inline_tensor(np.ones((1, 1), dtype=np.float32), "onescell_f").ap()
    CPOOL_VALS = (EPS_LN, -EPS_LN, EPS_RMS)
    cpool_f = nc.inline_tensor(
        np.broadcast_to(np.array(CPOOL_VALS, np.float32), (128, 3)).copy(), "cpool_f").ap()

    CTX = ExitStack()
    E_ = CTX.enter_context
    with CTX:
        sSY = Sem(CTX, nc, "sSY")
        sGP = Sem(CTX, nc, "sGP")
        sCC = Sem(CTX, nc, "sCC")
        sPE = Sem(CTX, nc, "sPE")
        sDV = Sem(CTX, nc, "sDV")
        sSC = Sem(CTX, nc, "sSC")
        sX0 = Sem(CTX, nc, "sX0")
        sX1 = Sem(CTX, nc, "sX1")
        sW0 = Sem(CTX, nc, "sW0")
        sW1 = Sem(CTX, nc, "sW1")
        sS0 = Sem(CTX, nc, "sS0")
        sS1 = Sem(CTX, nc, "sS1")
        sO0 = Sem(CTX, nc, "sO0")
        sO1 = Sem(CTX, nc, "sO1")
        sGC = Sem(CTX, nc, "sGC")
        sW2 = Sem(CTX, nc, "sW2")
        sW3 = Sem(CTX, nc, "sW3")
        WSL2 = (sW2, sW3)
        XSL = (sX0, sX1)
        WSL = (sW0, sW1)
        SSL = (sS0, sS1)
        OSL = (sO0, sO1)

        identb_t = E_(nc.sbuf_tensor("identb_t", [128, 128], BF16))
        identf_t = E_(nc.sbuf_tensor("identf_t", [128, 128], F32))
        r2t_t = E_(nc.sbuf_tensor("r2t_t", [128, 128], BF16))
        onesb_t = E_(nc.sbuf_tensor("onesb_t", [128, 1], BF16))
        onesr_t = E_(nc.sbuf_tensor("onesr_t", [1, 128], F32))
        allones_t = E_(nc.sbuf_tensor("allones_t", [128, 128], F32))
        onescell_t = E_(nc.sbuf_tensor("onescell_t", [1, 1], F32))
        cA = E_(nc.sbuf_tensor("cA", [128, 32], F32))
        cB = E_(nc.sbuf_tensor("cB", [128, 32], F32))
        m2w1c = E_(nc.sbuf_tensor("m2w1c", [128, 1], F32))
        m2w2c = E_(nc.sbuf_tensor("m2w2c", [128, 1], F32))
        convw_t = E_(nc.sbuf_tensor("convw_t", [128, 96], F32))
        convb_t = E_(nc.sbuf_tensor("convb_t", [128, 32], F32))
        tapsp = E_(nc.sbuf_tensor("tapsp", [128, 96], F32))
        maskL_t = E_(nc.sbuf_tensor("maskL_t", [128, 1], F32))
        maskR_t = E_(nc.sbuf_tensor("maskR_t", [128, 1], F32))
        cpool_t = E_(nc.sbuf_tensor("cpool_t", [128, 3], F32))
        for vi, vv in enumerate(CPOOL_VALS):
            nc.const_aps.register(F32, float(vv), cpool_t[:, vi:vi + 1])

        with ExitStack() as c1:
            e1 = c1.enter_context
            xa = e1(nc.sbuf_tensor("xa", [128, 2, E], F32))
            xc_t = e1(nc.sbuf_tensor("xc_t", [128, 2, E], F32))
            sqsc = e1(nc.sbuf_tensor("sqsc", [128, E], F32))
            xlnbf = e1(nc.sbuf_tensor("xlnbf", [128, 2, E], BF16))
            xlnTo = e1(nc.sbuf_tensor("xlnTo", [128, 8, OWN], BF16))
            wtile = e1(nc.sbuf_tensor("wtile", [128, 2, 4096], F32))
            wsq = e1(nc.sbuf_tensor("wsq", [128, 4096], F32))
            wint = e1(nc.sbuf_tensor("wint", [128, 4096], F32))
            wqb = e1(nc.sbuf_tensor("wqb", [128, 4096], BF16))
            wtT = e1(nc.sbuf_tensor("wtT", [128, 4096], BF16))
            wabs = e1(nc.sbuf_tensor("wabs", [128, 40], F32))
            ptrA = e1(nc.psum_tensor("ptrA", [128, 1024], BF16))
            ptrB = e1(nc.psum_tensor("ptrB", [128, 1024], BF16))
            pred = e1(nc.psum_tensor("pred", [128, 2], F32))
            PTR = (ptrA, ptrB)

            P = Prog()

            def cdma(dst, src):
                def f(e):
                    e.dma_start(out=dst, in_=src).then_inc(sSY.h, 16)
                return f

            for dst, src in ((identb_t[:], ident_bf[:]), (identf_t[:], ident_f[:]),
                             (r2t_t[:], r2t[:]), (onesb_t[:], ones_bf[:]),
                             (onesr_t[:], onesrow_f[:]), (allones_t[:], allones_f[:]),
                             (onescell_t[:], onescell_f[:]),
                             (convw_t[:], convw[:]), (convb_t[:], convb[:]),
                             (maskL_t[:], maskL[:]), (maskR_t[:], maskR[:]),
                             (cpool_t[:], cpool_f[:])):
                P.on("sync", cdma(dst, src))
                sSY.n += 16
            P.mark("const", sSY)

            for t in range(LN_TILES):
                sl = XSL[t % 2]
                if t >= 2:
                    P.wait("sync", sDV, ("lnxc", t - 2))

                def xload(e, t=t):
                    e.dma_start(out=_sl3(xa, t % 2, 0, E),
                                in_=xh[1 + 128 * t:129 + 128 * t, :]).then_inc(XSL[t % 2].h, 16)
                P.on("sync", xload)
                sl.n += 16

                P.wait("vector", sl, sl.n)
                co = 8 * (t % 2)

                def v1(e, t=t, co=co):
                    e.reduce_sum(cB[:, co:co + 1], _sl3(xa, t % 2, 0, E), axis=AX).then_inc(sDV.h, 1)
                P.on("vector", v1)
                sDV.n += 1
                P.mark(("lnsum", t), sDV)

                if t == 0:
                    P.wait("scalar", sSY, "const")
                P.wait("scalar", sDV, ("lnsum", t))

                def s1(e, co=co):
                    e.mul(out=cA[:, co:co + 1], in_=cB[:, co:co + 1], mul=1.0 / E).then_inc(sSC.h, 1)
                P.on("scalar", s1)
                sSC.n += 1
                P.mark(("lnm", t), sSC)

                P.wait("vector", sSC, ("lnm", t))
                if t >= 2:
                    P.wait("vector", sSC, ("lnout", t - 2))

                def v2(e, t=t, co=co):
                    e.tensor_scalar(out=_sl3(xc_t, t % 2, 0, E), in0=_sl3(xa, t % 2, 0, E),
                                    scalar1=cA[:, co:co + 1], scalar2=None,
                                    op0=ALU.subtract).then_inc(sDV.h, 1)
                P.on("vector", v2)
                sDV.n += 1
                P.mark(("lnxc", t), sDV)

                P.wait("scalar", sDV, ("lnxc", t))

                acc_n = sSC.n + 1

                def s2(e, t=t, co=co, acc_n=acc_n):
                    e.activation(out=sqsc[:], in_=_sl3(xc_t, t % 2, 0, E), func=AF.Square,
                                 accum_out=cA[:, co + 1:co + 2]).then_inc(sSC.h, 1)
                    e.wait_ge(sSC.h, acc_n)
                    e.activation(out=cA[:, co + 2:co + 3], in_=cA[:, co + 1:co + 2], func=AF.Sqrt,
                                 scale=1.0 / E, bias=EPS_LN).then_inc(sSC.h, 1)
                P.on("scalar", s2)
                sSC.n += 2
                P.mark(("lnsrt", t), sSC)

                P.wait("vector", sSC, ("lnsrt", t))

                def v3(e, co=co):
                    e.reciprocal(cB[:, co + 2:co + 3], cA[:, co + 2:co + 3]).then_inc(sDV.h, 1)
                P.on("vector", v3)
                sDV.n += 1
                P.mark(("lnrstd", t), sDV)

                P.wait("scalar", sDV, ("lnrstd", t))
                if t >= 2:
                    P.wait("scalar", sPE, ("lntr", t - 2))

                def s3(e, t=t, co=co):
                    e.activation(out=_sl3(xlnbf, t % 2, 0, E), in_=_sl3(xc_t, t % 2, 0, E),
                                 func=AF.Copy, scale=cB[:, co + 2:co + 3]).then_inc(sSC.h, 1)
                P.on("scalar", s3)
                sSC.n += 1
                P.mark(("lnout", t), sSC)

                if SKIP_TR:
                    P.M[("lntr", t)] = sPE.n
                    P.M[("lnband", t)] = sDV.n
                    continue
                P.wait("tensor", sSC, ("lnout", t))
                P.wait("tensor", sSY, "const")
                if t >= 2:
                    P.wait("tensor", sDV, ("lnband", t - 2))

                def te1(e, t=t):
                    pb = PTR[t % 2]
                    for ec in range(8):
                        d = e.transpose(pb[:, 128 * ec:128 * (ec + 1)],
                                        xlnbf[:, t % 2, 128 * ec:128 * (ec + 1)], identb_t[:])
                    d.then_inc(sPE.h, 1)
                P.on("tensor", te1)
                sPE.n += 1
                P.mark(("lntr", t), sPE)

                P.wait("vector", sPE, ("lntr", t))

                def v4(e, t=t):
                    pb = PTR[t % 2]
                    e.tensor_copy(xlnTo[:, :, 128 * t:128 * (t + 1)],
                                  pb[:].rearrange("p (ec c) -> p ec c", c=128)).then_inc(sDV.h, 1)
                P.on("vector", v4)
                sDV.n += 1
                P.mark(("lnband", t), sDV)

            if LN_TILES > 0:
                P.wait("sync", sDV, ("lnband", LN_TILES - 1))

            def agw(e):
                for ec in range(8):
                    e.dma_start(out=ag1in[128 * ec:128 * (ec + 1), :],
                                in_=xlnTo[:, ec, :]).then_inc(sSY.h, 16)
            if not SKIP_AG:
                P.on("sync", agw)
                sSY.n += 8 * 16
            P.mark("ag1w", sSY)

            if not SKIP_AG:
                P.wait("gpsimd", sSY, "ag1w")

            def ag1(e):
                e.collective_compute("AllGather", ALU.bypass,
                                     replica_groups=[list(range(NC))],
                                     ins=[ag1in[:].opt()], outs=[ag1out[:].opt()]).then_inc(sCC.h, 1)
            if not SKIP_AG:
                P.on("gpsimd", ag1)
                sCC.n += 1
            P.mark("ag1", sCC)

            for i in ([] if SKIP_WQ else range(40)):
                sl = WSL[i % 2]
                ldq = "sync" if i % 2 == 0 else "scalar"
                if i >= 2:
                    P.wait(ldq, sDV, ("wabs", i - 2))
                wd = 1024 if i < 32 else 4096

                def wload(e, i=i, wd=wd):
                    src = w1f[128 * i:128 * (i + 1), :] if i < 32 else w2f[128 * (i - 32):128 * (i - 31), :]
                    e.dma_start(out=_sl3(wtile, i % 2, 0, wd), in_=src).then_inc(WSL[i % 2].h, 16)
                P.on(ldq, wload)
                sl.n += 16
                P.wait("vector", sl, sl.n)

                def wabs_f(e, i=i, wd=wd):
                    e.reduce_sum(wabs[:, i:i + 1], _sl3(wtile, i % 2, 0, wd), axis=AX,
                                 apply_absolute_value=True).then_inc(sDV.h, 1)
                P.on("vector", wabs_f)
                sDV.n += 1
                P.mark(("wabs", i), sDV)

            if not SKIP_WQ:
                P.wait("vector", sDV, ("wabs", 39))

                def wsums(e):
                    e.reduce_sum(cB[:, 16:17], wabs[:, 0:32], axis=AX)
                    e.reduce_sum(cB[:, 17:18], wabs[:, 32:40], axis=AX).then_inc(sDV.h, 1)
                P.on("vector", wsums)
                sDV.n += 1
                P.mark("wsum", sDV)

                P.wait("tensor", sDV, "wsum")

                def wtot(e):
                    e.matmul(pred[:], allones_t[:], cB[:, 16:18], start=True, stop=True).then_inc(sPE.h, 1)
                P.on("tensor", wtot)
                sPE.n += 1
                P.mark("wtot", sPE)

                P.wait("scalar", sPE, "wtot")

                def wm1(e):
                    e.mul(out=cA[:, 16:17], in_=pred[:, 0:1], mul=1.0 / (4096.0 * 1024.0))
                    e.mul(out=cA[:, 17:18], in_=pred[:, 1:2], mul=1.0 / (1024.0 * 4096.0)).then_inc(sSC.h, 1)
                P.on("scalar", wm1)
                sSC.n += 1
                P.mark("wm1", sSC)

                P.wait("vector", sSC, "wm1")

                def wm2(e):
                    e.tensor_scalar_max(m2w1c[:], cA[:, 16:17], 1e-5)
                    e.tensor_scalar_max(m2w2c[:], cA[:, 17:18], 1e-5).then_inc(sDV.h, 1)
                P.on("vector", wm2)
                sDV.n += 1
                P.mark("wm2", sDV)

                P.wait("scalar", sDV, "wm2")

                def wm2s(e):
                    e.copy(out=cA[:, 18:19], in_=m2w1c[:])
                    e.copy(out=cA[:, 19:20], in_=m2w2c[:]).then_inc(sSC.h, 1)
                P.on("scalar", wm2s)
                sSC.n += 1
                P.mark("wm2s", sSC)

                P.wait("vector", sSC, "wm2s")

                def wrec(e):
                    e.reciprocal(cB[:, 18:19], cA[:, 18:19])
                    e.reciprocal(cB[:, 19:20], cA[:, 19:20]).then_inc(sDV.h, 1)
                P.on("vector", wrec)
                sDV.n += 1
                P.mark("wrec", sDV)

                P.wait("scalar", sDV, "wm2")
                P.wait("scalar", sSY, "const")

                def tap(e):
                    e.mul(out=tapsp[:], in_=convw_t[:], mul=m2w1c[:]).then_inc(sSC.h, 1)
                P.on("scalar", tap)
                sSC.n += 1
                P.mark("taps", sSC)

                band = 0
                for i in range(40):
                    sl2 = WSL2[i % 2]
                    wd = 1024 if i < 32 else 4096
                    nb = wd // 1024
                    if i < 2:
                        P.wait("scalar", sDV, ("wabs", 38 + i))

                    def wload2(e, i=i, wd=wd):
                        src = w1f[128 * i:128 * (i + 1), :] if i < 32 else w2f[128 * (i - 32):128 * (i - 31), :]
                        e.dma_start(out=_sl3(wtile, i % 2, 0, wd), in_=src).then_inc(WSL2[i % 2].h, 16)
                    P.on("scalar", wload2)
                    sl2.n += 16

                    P.wait("scalar", sl2, sl2.n)
                    P.wait("scalar", sDV, "wrec")
                    if i >= 1:
                        P.wait("scalar", sDV, ("wint", i - 1))

                    def wsf(e, i=i, wd=wd):
                        e.activation(out=wsq[:, 0:wd], in_=_sl3(wtile, i % 2, 0, wd), func=AF.Copy,
                                     scale=cB[:, 18:19] if i < 32 else cB[:, 19:20]).then_inc(sSC.h, 1)
                    P.on("scalar", wsf)
                    sSC.n += 1
                    P.mark(("wsf", i), sSC)

                    P.wait("vector", sSC, ("wsf", i))

                    acc_n = sDV.n + 1

                    def wcv(e, wd=wd, acc_n=acc_n):
                        e.tensor_scalar(out=wint[:, 0:wd], in0=wsq[:, 0:wd],
                                        scalar1=0.5, scalar2=1.0,
                                        op0=ALU.is_gt, op1=ALU.subtract).then_inc(sDV.h, 1)
                        e.wait_ge(sDV.h, acc_n)
                        e.scalar_tensor_tensor(out=wqb[:, 0:wd], in0=wsq[:, 0:wd],
                                               scalar=-0.5, in1=wint[:, 0:wd],
                                               op0=ALU.is_ge, op1=ALU.add).then_inc(sDV.h, 1)
                    P.on("vector", wcv)
                    sDV.n += 2
                    P.mark(("wint", i), sDV)

                    for bb in range(nb):
                        P.wait("tensor", sDV, ("wint", i))
                        if band < 2:
                            P.wait("tensor", sDV, ("lnband", 15))
                        else:
                            P.wait("tensor", sDV, ("wtb", band - 2))

                        def wtr(e, bb=bb, band=band):
                            pb = PTR[band % 2]
                            for k in range(8):
                                d = e.transpose(pb[:, 128 * k:128 * (k + 1)],
                                                wqb[:, 1024 * bb + 128 * k:1024 * bb + 128 * (k + 1)],
                                                identb_t[:])
                            d.then_inc(sPE.h, 1)
                        P.on("tensor", wtr)
                        sPE.n += 1
                        P.mark(("wtr", band), sPE)

                        P.wait("vector", sPE, ("wtr", band))
                        if i >= 1 and bb == 0:
                            P.wait("vector", sSY, ("wst", i - 1))

                        def wtb(e, bb=bb, band=band):
                            pb = PTR[band % 2]
                            e.tensor_copy(wtT[:, 1024 * bb:1024 * (bb + 1)],
                                          pb[:].rearrange("p (k c) -> p k c", c=128)).then_inc(sDV.h, 1)
                        P.on("vector", wtb)
                        sDV.n += 1
                        P.mark(("wtb", band), sDV)
                        band += 1

                    P.wait("sync", sDV, ("wtb", band - 1))

                    def wst(e, i=i):
                        if i < 32:
                            d = e.dma_start(
                                out=w1qT_d[:, 128 * i:128 * (i + 1)].rearrange("(ec p) c -> p ec c", p=128),
                                in_=wtT[:, 0:1024].rearrange("p (ec c) -> p ec c", c=128))
                        else:
                            jj = i - 32
                            d = e.dma_start(
                                out=w2qT_d[:, 128 * jj:128 * (jj + 1)].rearrange("(ch p) c -> p ch c", p=128),
                                in_=wtT[:, 0:4096].rearrange("p (ch c) -> p ch c", c=128))
                        d.then_inc(sSY.h, 16)
                    P.on("sync", wst)
                    sSY.n += 16
                    P.mark(("wst", i), sSY)

            if DEBUG_STAGE == 24:
                P.wait("sync", sDV, "wrec")
                P.wait("sync", sSC, ("wsf", 39))
                P.wait("sync", sDV, ("wint", 39))
                base24 = sSY.n

                def dump24(e, base24=base24):
                    e.dma_start(out=dbg[:, 0:40], in_=wabs[:]).then_inc(sSY.h, 16)
                    e.dma_start(out=dbg[:, 56:60], in_=cA[:, 16:20]).then_inc(sSY.h, 16)
                    e.dma_start(out=dbg[:, 88:92], in_=cB[:, 16:20]).then_inc(sSY.h, 16)
                    e.dma_start(out=dbg[:, 104:4200], in_=wsq[:]).then_inc(sSY.h, 16)
                    e.wait_ge(sSY.h, base24 + 64)
                P.on("sync", dump24)
                sSY.n += 64

                P.wait("gpsimd", sSC, ("wsf", 39))
                P.wait("gpsimd", sDV, ("wint", 39))
                base24g = sGP.n

                def dump24g(e, base24g=base24g):
                    e.dma_start(out=dbg[:, 4200:8296], in_=wqb[:]).then_inc(sGP.h, 16)
                    e.wait_ge(sGP.h, base24g + 16)
                P.on("gpsimd", dump24g)
                sGP.n += 16

            if DEBUG_STAGE == 1:
                P.wait("sync", sCC, "ag1")
                base0 = sSY.n

                def dump1(e, base0=base0):
                    n = base0
                    for r in range(NC):
                        e.dma_start(out=xlnTo[:],
                                    in_=ag1out[E * r:E * (r + 1), :].rearrange(
                                        "(ec p) c -> p ec c", p=128)).then_inc(sSY.h, 16)
                        n += 16
                        e.wait_ge(sSY.h, n)
                        e.dma_start(out=dbg[E * r:E * (r + 1), :].rearrange(
                                        "(ec p) c -> p ec c", p=128),
                                    in_=xlnTo[:]).then_inc(sSY.h, 16)
                        n += 16
                        e.wait_ge(sSY.h, n)
                P.on("sync", dump1)
                sSY.n += 2 * NC * 16

            for s_ in (sSY, sGP, sCC, sPE, sDV, sSC, sX0, sX1, sW0, sW1, sW2, sW3, sS0, sS1, sO0, sO1):
                P.mark(("b1", s_.name), s_)
            blk1 = e1(nc.Block("blk1"))
            P.emit(blk1)
            M1 = dict(P.M)

        if DEBUG_STAGE in (1, 24):
            return nc

        with ExitStack() as c2:
            e2 = c2.enter_context
            import os as _os2
            if DEBUG_STAGE == 19 and _os2.environ.get("SKIP_B2ALLOC"):
                P = Prog(M1)
                if sCC.n:
                    P.wait("sync", sCC, "ag1")

                n019 = sSY.n

                def dump19(e, n019=n019):
                    e.dma_start(out=dbg[:, 0:128], in_=identb_t[:]).then_inc(sSY.h, 16)
                    e.wait_ge(sSY.h, n019 + 16)
                P.on("sync", dump19)
                sSY.n += 16
                blk2y = e2(nc.Block("blk2"))
                P.emit(blk2y)
                return nc
            xlnT = e2(nc.sbuf_tensor("xlnT", [128, 8, S], BF16))
            tqc = e2(nc.sbuf_tensor("tqc", [128, S], BF16))
            tqs = e2(nc.sbuf_tensor("tqs", [128, S], BF16))
            tkc = e2(nc.sbuf_tensor("tkc", [128, S], BF16))
            tks = e2(nc.sbuf_tensor("tks", [128, S], BF16))
            btab_t = e2(nc.sbuf_tensor("btab_t", [128, 512], F32))
            grow_t = e2(nc.sbuf_tensor("grow_t", [1, 2 * S], BF16))
            tril_t = e2(nc.sbuf_tensor("tril_t", [128, 2048], BF16))
            wq_sb = e2(nc.sbuf_tensor("wq_sb", [128, 8, 128], BF16))
            wk_sb = e2(nc.sbuf_tensor("wk_sb", [128, 8, 128], BF16))
            wvg_sb = e2(nc.sbuf_tensor("wvg_sb", [128, 8, 256], BF16))
            qrT = e2(nc.sbuf_tensor("qrT", [128, S], BF16))
            krT = e2(nc.sbuf_tensor("krT", [128, S], BF16))
            vr = e2(nc.sbuf_tensor("vr", [128, 32, 128], BF16))
            gsil = e2(nc.sbuf_tensor("gsil", [128, 32, 128], BF16))
            tsinb = e2(nc.sbuf_tensor("tsinb", [128, 2, 512], BF16))
            tcosb = e2(nc.sbuf_tensor("tcosb", [128, 2, 512], BF16))
            ptb = e2(nc.sbuf_tensor("ptb", [128, 6, 512], BF16))
            pab = e2(nc.sbuf_tensor("pab", [128, 6, 512], BF16))
            pvsb0 = e2(nc.sbuf_tensor("pvsb0", [64, 512], BF16))
            pvsb1 = e2(nc.sbuf_tensor("pvsb1", [64, 512], BF16))
            densb = e2(nc.sbuf_tensor("densb", [1, 1024], F32))
            rsrow = e2(nc.sbuf_tensor("rsrow", [1, 1024], F32))
            o1b = e2(nc.sbuf_tensor("o1b", [128, 512], F32))
            sqep = e2(nc.sbuf_tensor("sqep", [128, 512], F32))
            o2b = e2(nc.sbuf_tensor("o2b", [128, 512], BF16))
            attno = e2(nc.sbuf_tensor("attno", [128, 32, 128], BF16))
            pQK0 = e2(nc.psum_tensor("pQK0", [128, 512], F32))
            pQK1 = e2(nc.psum_tensor("pQK1", [128, 512], F32))
            pPV0 = e2(nc.psum_tensor("pPV0", [64, 512], F32))
            pPV1 = e2(nc.psum_tensor("pPV1", [64, 512], F32))
            pDEN0 = e2(nc.psum_tensor("pDEN0", [1, 512], F32))
            pDEN1 = e2(nc.psum_tensor("pDEN1", [1, 512], F32))
            pROT = e2(nc.psum_tensor("pROT", [128, 512], F32))
            pEPI = e2(nc.psum_tensor("pEPI", [128, 512], BF16))
            PQK = (pQK0, pQK1)
            pPRJ0, pPRJ1 = pQK0, pQK1
            PPRJ = (pPRJ0, pPRJ1)
            PPVH = (pPV0, pPV1)
            PDENH = (pDEN0, pDEN1)

            P = Prog(M1)
            import os as _os
            if not _os.environ.get("SKIP_BARRIER"):
                for eng_ in Prog.ENGINES:
                    for s_ in (sSY, sGP, sPE, sDV, sSC, sX0, sX1, sW0, sW1, sW2, sW3, sS0, sS1, sO0, sO1):
                        P.wait(eng_, s_, ("b1", s_.name))

            def _attn_epilogue_stages(b, qt):
                e_ = 8 * b + qt
                tiles_g = ("pvden", 288 * b + _tile_end2(qt))

                def st_epev():
                    P.wait("scalar", sPE, tiles_g)
                    if e_ >= 1:
                        P.wait("scalar", sDV, ("eprs", e_ - 1))

                    def epev(e):
                        e.copy(out=pvsb0[:], in_=pPV0[:])
                        e.copy(out=pvsb1[:], in_=pPV1[:])
                        e.copy(out=densb[0:1, 0:512], in_=pDEN0[:])
                        e.copy(out=densb[0:1, 512:1024], in_=pDEN1[:]).then_inc(sSC.h, 1)
                    P.on("scalar", epev)
                    sSC.n += 1
                    P.mark(("epev", e_), sSC)

                def st_eprs():
                    P.wait("vector", sSC, ("epev", e_))
                    if e_ >= 1:
                        P.wait("vector", sPE, ("eptr", e_ - 1))

                    base_n = sDV.n

                    def eprs(e, qt=qt, base_n=base_n):
                        g0 = 1024 * qt
                        e.tensor_tensor(rsrow[:], densb[:], grow_t[0:1, g0:g0 + 1024],
                                        op=ALU.mult).then_inc(sDV.h, 1)
                        e.wait_ge(sDV.h, base_n + 1)
                        e.tensor_scalar(out=rsrow[:], in0=rsrow[:], scalar1=1.0, scalar2=50000.0,
                                        op0=ALU.max, op1=ALU.min).then_inc(sDV.h, 1)
                        e.wait_ge(sDV.h, base_n + 2)
                        e.reciprocal(rsrow[:], rsrow[:]).then_inc(sDV.h, 1)
                        e.wait_ge(sDV.h, base_n + 3)
                        e.tensor_tensor(rsrow[:], rsrow[:], grow_t[0:1, g0:g0 + 1024],
                                        op=ALU.mult).then_inc(sDV.h, 1)
                    P.on("vector", eprs)
                    sDV.n += 4
                    P.mark(("eprs", e_), sDV)

                def st_eptr():
                    P.wait("tensor", sDV, ("eprs", e_))
                    P.wait("tensor", sSC, ("epev", e_))
                    if e_ >= 1:
                        P.wait("tensor", sSC, ("epo1", e_ - 1))
                        P.wait("tensor", sDV, ("eprsc", e_ - 1))

                    def eptr(e):
                        for c in range(8):
                            qc, h = c // 2, c % 2
                            pvh = (pvsb0, pvsb1)[h]
                            e.transpose(pEPI[:, 64 * c:64 * (c + 1)],
                                        pvh[:, 128 * qc:128 * (qc + 1)],
                                        identb_t[0:64, 0:64])
                        for c in range(8):
                            qc, h = c // 2, c % 2
                            d = e.matmul(pROT[:, c:c + 1],
                                         rsrow[0:1, 512 * h + 128 * qc:512 * h + 128 * qc + 128],
                                         onescell_t[:], start=True, stop=True)
                        d.then_inc(sPE.h, 1)
                    P.on("tensor", eptr)
                    sPE.n += 1
                    P.mark(("eptr", e_), sPE)

                def st_eprsc():
                    P.wait("vector", sPE, ("eptr", e_))
                    if e_ >= 1:
                        P.wait("vector", sSC, ("epo1", e_ - 1))

                    def eprsc(e):
                        e.tensor_copy(cB[:, 16:24], pROT[:, 0:8]).then_inc(sDV.h, 1)
                    P.on("vector", eprsc)
                    sDV.n += 1
                    P.mark(("eprsc", e_), sDV)

                def st_epo1():
                    P.wait("scalar", sDV, ("eprsc", e_))
                    if e_ >= 1:
                        P.wait("scalar", sDV, ("eprms", e_ - 1))

                    acc_n = sSC.n

                    def epo1(e, acc_n=acc_n):
                        for c in range(8):
                            d = e.activation(out=o1b[:, 64 * c:64 * (c + 1)],
                                             in_=pEPI[:, 64 * c:64 * (c + 1)],
                                             func=AF.Copy, scale=cB[:, 16 + c:17 + c])
                        d.then_inc(sSC.h, 1)
                        e.wait_ge(sSC.h, acc_n + 1)
                        for c in range(8):
                            d = e.activation(out=sqep[:, 64 * c:64 * (c + 1)],
                                             in_=o1b[:, 64 * c:64 * (c + 1)], func=AF.Square,
                                             accum_out=cA[:, 16 + c:17 + c])
                        d.then_inc(sSC.h, 1)
                        e.wait_ge(sSC.h, acc_n + 2)
                        e.activation(out=cA[:, 24:32], in_=cA[:, 16:24], func=AF.Sqrt,
                                     scale=1.0 / 64.0, bias=EPS_RMS).then_inc(sSC.h, 1)
                    P.on("scalar", epo1)
                    sSC.n += 3
                    P.mark(("epo1", e_), sSC)

                def st_eprms():
                    P.wait("vector", sSC, ("epo1", e_))

                    def eprms(e):
                        e.reciprocal(cB[:, 24:32], cA[:, 24:32]).then_inc(sDV.h, 1)
                    P.on("vector", eprms)
                    sDV.n += 1
                    P.mark(("eprms", e_), sDV)

                def st_epo2():
                    P.wait("scalar", sDV, ("eprms", e_))
                    if e_ >= 1:
                        P.wait("scalar", sDV, ("epat", e_ - 1))

                    def epo2(e):
                        for c in range(8):
                            d = e.activation(out=o2b[:, 64 * c:64 * (c + 1)],
                                             in_=o1b[:, 64 * c:64 * (c + 1)],
                                             func=AF.Copy, scale=cB[:, 24 + c:25 + c])
                        d.then_inc(sSC.h, 1)
                    P.on("scalar", epo2)
                    sSC.n += 1
                    P.mark(("epo2", e_), sSC)

                def st_epat():
                    P.wait("vector", sSC, ("epo2", e_))
                    if b > 0 and e_ == 8 * b:
                        P.wait("vector", sGP, ("a2aw", b - 1))

                    def epat(e, qt=qt):
                        e.tensor_tensor(attno[:, 4 * qt:4 * qt + 4, :],
                                        o2b[:].rearrange("p (qc hd) -> p qc hd", hd=128),
                                        gsil[:, 4 * qt:4 * qt + 4, :],
                                        op=ALU.mult).then_inc(sDV.h, 1)
                    P.on("vector", epat)
                    sDV.n += 1
                    P.mark(("epat", e_), sDV)

                return [st_epev, st_eprs, st_eptr, st_eprsc,
                        st_epo1, st_eprms, st_epo2, st_epat]

            if DEBUG_STAGE == 19:
                if sCC.n:
                    P.wait("sync", sCC, "ag1")
                if not SKIP_WQ:
                    P.wait("sync", sSY, ("wst", 39))

                n019 = sSY.n

                def dump19(e, n019=n019):
                    e.dma_start(out=dbg[:, 0:128], in_=identb_t[:]).then_inc(sSY.h, 16)
                    e.wait_ge(sSY.h, n019 + 16)
                P.on("sync", dump19)
                sSY.n += 16
                for s_ in (sSY, sGP, sCC, sPE, sDV, sSC):
                    P.mark(("b2", s_.name), s_)
                blk2x = e2(nc.Block("blk2"))
                P.emit(blk2x)
                return nc

            def c2load(e):
                for dst, src in ((tqc[:], tq_cos[:]), (tqs[:], tq_sin[:]),
                                 (tkc[:], tk_cos[:]), (tks[:], tk_sin[:]),
                                 (btab_t[:], btab[:]), (grow_t[:], grow[:]),
                                 (tril_t[:], trilt[:])):
                    e.dma_start(out=dst, in_=src).then_inc(sGP.h, 16)
                e.dma_start(out=wq_sb[:], in_=wq[:].rearrange("(ec p) c -> p ec c", p=128)).then_inc(sGP.h, 16)
                e.dma_start(out=wk_sb[:], in_=wk[:].rearrange("(ec p) c -> p ec c", p=128)).then_inc(sGP.h, 16)
                e.dma_start(out=wvg_sb[:], in_=wvg[:].rearrange("(ec p) c -> p ec c", p=128)).then_inc(sGP.h, 16)
            P.on("gpsimd", c2load)
            sGP.n += 10 * 16
            P.mark("c2load", sGP)

            NB = 1 if DEBUG_STAGE in (19, 20, 21, 22, 23) else 4
            for b in range(NB):
                P.wait("sync", sCC, "ag1")
                if b > 0:
                    P.wait("sync", sPE, ("vgdone", b - 1))

                def xlnl(e, b=b):
                    for half in range(2):
                        for ec in range(8):
                            r0 = E * (2 * b + half) + 128 * ec
                            e.dma_start(out=xlnT[:, ec, OWN * half:OWN * (half + 1)],
                                        in_=ag1out[r0:r0 + 128, :]).then_inc(sSY.h, 16)
                P.on("sync", xlnl)
                sSY.n += 16 * 16
                P.mark(("xlnT", b), sSY)

                for part in (range(0) if DEBUG_STAGE == 20 else range(2)):
                    wsb = (wq_sb, wk_sb)[part]
                    tc = (tqc, tkc)[part]
                    tsn = (tqs, tks)[part]
                    dst = (qrT, krT)[part]
                    for ch in range(8):
                        gidx = 16 * b + 8 * part + ch
                        P.wait("tensor", sSY, ("xlnT", b))
                        P.wait("tensor", sGP, "c2load")
                        if gidx >= 2:
                            P.wait("tensor", sDV, ("rotcos", gidx - 2))
                        if b > 0 and gidx == 16 * b:
                            P.wait("tensor", sSC, ("gsl", 32 * b - 1))
                            P.wait("tensor", sSC, ("ptbm", 288 * b - 1))

                        def prj(e, ch=ch, wsb=wsb):
                            pp = PPRJ[ch % 2]
                            for ec in range(8):
                                d = e.matmul(pp[:], _sl3(wsb, ec, 0, 128),
                                             xlnT[:, ec, 512 * ch:512 * (ch + 1)],
                                             start=(ec == 0), stop=(ec == 7))
                            d.then_inc(sPE.h, 1)
                        P.on("tensor", prj)
                        sPE.n += 1
                        P.mark(("prj", gidx), sPE)

                        P.wait("vector", sPE, ("prj", gidx))
                        P.wait("vector", sGP, "c2load")
                        if gidx >= 2:
                            P.wait("vector", sPE, ("r2", gidx - 2))

                        def rot1(e, ch=ch, gidx=gidx, tc=tc, tsn=tsn):
                            pp = PPRJ[ch % 2]
                            e.tensor_tensor(_sl3(tsinb, gidx % 2, 0, 512), pp[:],
                                            tsn[:, 512 * ch:512 * (ch + 1)], op=ALU.mult)
                            e.tensor_tensor(_sl3(tcosb, gidx % 2, 0, 512), pp[:],
                                            tc[:, 512 * ch:512 * (ch + 1)],
                                            op=ALU.mult).then_inc(sDV.h, 1)
                        P.on("vector", rot1)
                        sDV.n += 1
                        P.mark(("rotcos", gidx), sDV)

                        P.wait("tensor", sDV, ("rotcos", gidx))
                        if gidx >= 1:
                            P.wait("tensor", sDV, ("rot", gidx - 1))

                        def r2mm(e, gidx=gidx):
                            e.matmul(pROT[:], r2t_t[:], _sl3(tsinb, gidx % 2, 0, 512),
                                     start=True, stop=True).then_inc(sPE.h, 1)
                        P.on("tensor", r2mm)
                        sPE.n += 1
                        P.mark(("r2", gidx), sPE)

                        P.wait("vector", sPE, ("r2", gidx))

                        def rot2(e, ch=ch, gidx=gidx, dst=dst):
                            e.tensor_tensor(dst[:, 512 * ch:512 * (ch + 1)], pROT[:],
                                            _sl3(tcosb, gidx % 2, 0, 512),
                                            op=ALU.add).then_inc(sDV.h, 1)
                        P.on("vector", rot2)
                        sDV.n += 1
                        P.mark(("rot", gidx), sDV)

                for tt in (range(0) if DEBUG_STAGE in (20, 21) else range(32)):
                    gv = 32 * b + tt
                    P.wait("tensor", sDV, ("rot", 16 * b + 15))
                    if tt >= 2:
                        P.wait("tensor", sDV, ("vrc", gv - 2))
                        P.wait("tensor", sSC, ("gsl", gv - 2))

                    def vgp(e, tt=tt):
                        pp = PPRJ[tt % 2]
                        for ec in range(8):
                            d = e.matmul(pp[:, 0:256], xlnT[:, ec, 128 * tt:128 * (tt + 1)],
                                         _sl3(wvg_sb, ec, 0, 256), start=(ec == 0), stop=(ec == 7))
                        d.then_inc(sPE.h, 1)
                    P.on("tensor", vgp)
                    sPE.n += 1
                    P.mark(("vg", gv), sPE)
                    if tt == 31:
                        P.mark(("vgdone", b), sPE)

                    P.wait("vector", sPE, ("vg", gv))

                    def vrc(e, tt=tt):
                        pp = PPRJ[tt % 2]
                        e.tensor_copy(vr[:, tt, :], pp[:, 0:128]).then_inc(sDV.h, 1)
                    P.on("vector", vrc)
                    sDV.n += 1
                    P.mark(("vrc", gv), sDV)

                    P.wait("scalar", sPE, ("vg", gv))
                    P.wait("scalar", sDV, ("vrc", gv))
                    if b > 0 and tt == 0:
                        P.wait("scalar", sDV, ("epat", 8 * b - 1))

                    def gsl(e, tt=tt):
                        pp = PPRJ[tt % 2]
                        e.activation(out=gsil[:, tt, :], in_=pp[:, 128:256],
                                     func=AF.Silu).then_inc(sSC.h, 1)
                    P.on("scalar", gsl)
                    sSC.n += 1
                    P.mark(("gsl", gv), sSC)

                if DEBUG_STAGE in (20, 21, 22):
                    continue
                tiles = []
                for qt in range(8):
                    for kt in range(4 * qt + 4):
                        for h in range(2):
                            tiles.append((h, qt, kt))
                G = len(tiles)
                assert G == 288

                def qk_step(e, h, qt, kt, par):
                    e.matmul(PQK[par][:], krT[64 * h:64 * (h + 1), 128 * kt:128 * (kt + 1)],
                             qrT[64 * h:64 * (h + 1), 512 * qt:512 * (qt + 1)],
                             start=True, stop=True).then_inc(sPE.h, 1)

                def pv_step(e, h, qt, kt, slot, first, last):
                    pt = _sl3(ptb, slot, 0, 512)
                    pa = _sl3(pab, slot, 0, 512)
                    e.matmul(PPVH[h][:], _sl3(vr, kt, 64 * h, 64 * h + 64), pt,
                             start=first, stop=last)
                    e.matmul(PDENH[h][:], onesb_t[:], pa,
                             start=first, stop=last).then_inc(sPE.h, 1)

                pend = []
                pend_seq = [0]

                def emit_pv(gi_prev, cur_gi, b=b):
                    hp, qp, kp_ = tiles[gi_prev]
                    g_prev = 288 * b + gi_prev
                    first = (kp_ == 0)
                    last = (kp_ == 4 * qp + 3)
                    P.wait("tensor", sDV, ("cast", g_prev))
                    if first and hp == 0:
                        egrp = 8 * b + qp
                        if egrp > 0:
                            P.wait("tensor", sSC, ("epev", egrp - 1))
                    P.on("tensor", lambda e, hp=hp, qp=qp, kp_=kp_, slot=g_prev % 6,
                         first=first, last=last: pv_step(e, hp, qp, kp_, slot, first, last))
                    sPE.n += 1
                    P.mark(("pvden", g_prev), sPE)
                    if last and hp == 1:
                        for k, st in enumerate(_attn_epilogue_stages(b, qp)):
                            heapq.heappush(pend, (cur_gi + k, pend_seq[0], st))
                            pend_seq[0] += 1

                def emit_qk(gi):
                    h, qt, kt = (tiles[gi][0], tiles[gi][1], tiles[gi][2])
                    g = 288 * b + gi
                    par = g % 2
                    P.wait("tensor", sDV, ("rot", 16 * b + 15))
                    if gi == 0:
                        P.wait("tensor", sDV, ("vrc", 32 * b + 31))
                        P.wait("tensor", sSC, ("gsl", 32 * b + 31))
                    if g >= 2:
                        P.wait("tensor", sSC, ("ptbm", g - 2))
                    P.on("tensor", lambda e, h=h, qt=qt, kt=kt, par=par: qk_step(e, h, qt, kt, par))
                    sPE.n += 1
                    P.mark(("qk", g), sPE)

                def emit_castchain(gi):
                    h, qt, kt = (tiles[gi][0], tiles[gi][1], tiles[gi][2])
                    g = 288 * b + gi
                    par = g % 2
                    slot = g % 6
                    diag = kt >= 4 * qt
                    if diag:
                        P.wait("vector", sPE, ("qk", g))

                        def dtril(e, qt=qt, kt=kt, par=par):
                            ri = kt - 4 * qt
                            e.tensor_tensor(PQK[par][:], PQK[par][:],
                                            tril_t[:, 512 * ri:512 * (ri + 1)],
                                            op=ALU.mult).then_inc(sDV.h, 1)
                        P.on("vector", dtril)
                        sDV.n += 1
                        P.mark(("diag", g), sDV)

                    P.wait("scalar", sPE, ("qk", g))
                    if diag:
                        P.wait("scalar", sDV, ("diag", g))
                    if g >= 6:
                        P.wait("scalar", sDV, ("cast", g - 6))
                    bi = 256 * h + 32 * qt + kt

                    def ptbm(e, par=par, slot=slot, bi=bi):
                        e.activation(out=_sl3(ptb, slot, 0, 512), in_=PQK[par][:],
                                     func=AF.Copy,
                                     scale=btab_t[:, bi:bi + 1]).then_inc(sSC.h, 1)
                    P.on("scalar", ptbm)
                    sSC.n += 1
                    P.mark(("ptbm", g), sSC)

                    P.wait("vector", sPE, ("qk", g))
                    P.wait("vector", sSC, ("ptbm", g))

                    def cast(e, slot=slot):
                        e.scalar_tensor_tensor(out=_sl3(pab, slot, 0, 512),
                                               in0=_sl3(ptb, slot, 0, 512), scalar=-1.0,
                                               in1=_sl3(ptb, slot, 0, 512),
                                               op0=ALU.mult, op1=ALU.max).then_inc(sDV.h, 1)
                    P.on("vector", cast)
                    sDV.n += 1
                    P.mark(("cast", g), sDV)

                for pi in range(G // 2):
                    emit_qk(2 * pi)
                    emit_qk(2 * pi + 1)
                    if pi > 0:
                        emit_pv(2 * pi - 2, 2 * pi + 1)
                        emit_pv(2 * pi - 1, 2 * pi + 1)
                    emit_castchain(2 * pi)
                    emit_castchain(2 * pi + 1)
                    gi = 2 * pi + 1
                    while pend and pend[0][0] <= gi:
                        heapq.heappop(pend)[2]()

                emit_pv(G - 2, G)
                emit_pv(G - 1, G)
                while pend:
                    heapq.heappop(pend)[2]()

                P.wait("gpsimd", sDV, ("epat", 8 * b + 7))

                def a2aw(e, b=b):
                    for jj in range(2):
                        j = 2 * b + jj
                        loc0 = jj * OWN
                        tt0 = loc0 // 128
                        e.dma_start(
                            out=a2ain[HR * j + 1:HR * j + 1 + OWN, :].rearrange("(tt p) c -> p tt c", p=128),
                            in_=attno[:, tt0:tt0 + 16, :]).then_inc(sGP.h, 16)
                        ta = max(loc0 - 1, 0)
                        e.dma_start(out=a2ain[HR * j:HR * j + 1, :],
                                    in_=attno[ta % 128:ta % 128 + 1, ta // 128, :]).then_inc(sGP.h, 16)
                        tb_ = min(loc0 + OWN, S - 1)
                        e.dma_start(out=a2ain[HR * j + 1 + OWN:HR * j + 2 + OWN, :],
                                    in_=attno[tb_ % 128:tb_ % 128 + 1, tb_ // 128, :]).then_inc(sGP.h, 16)
                        e.dma_start(out=a2ain[HR * j + 2 + OWN:HR * (j + 1), :],
                                    in_=attno[0:1, 0:6, :]).then_inc(sGP.h, 16)
                P.on("gpsimd", a2aw)
                sGP.n += 8 * 16
                P.mark(("a2aw", b), sGP)

            if DEBUG_STAGE == 20:
                P.wait("sync", sSY, ("xlnT", 0))
                P.wait("sync", sGP, "c2load")

                n0c = sSY.n

                def dump20(e, n0=n0c):
                    e.dma_start(out=dbg[:, 0:4096], in_=_sl3(xlnT, 0, 0, S)).then_inc(sSY.h, 16)
                    e.dma_start(out=dbg[:, 4096:8192], in_=_sl3(xlnT, 7, 0, S)).then_inc(sSY.h, 16)
                    e.wait_ge(sSY.h, n0 + 32)
                P.on("sync", dump20)
                sSY.n += 32
            if DEBUG_STAGE == 21:
                P.wait("sync", sDV, ("rot", 15))

                n0c = sSY.n

                def dump21(e, n0=n0c):
                    e.dma_start(out=dbg[:, 0:4096], in_=qrT[:]).then_inc(sSY.h, 16)
                    e.dma_start(out=dbg[:, 4096:8192], in_=krT[:]).then_inc(sSY.h, 16)
                    e.wait_ge(sSY.h, n0 + 32)
                P.on("sync", dump21)
                sSY.n += 32
            if DEBUG_STAGE == 22:
                P.wait("sync", sDV, ("vrc", 31))
                P.wait("sync", sSC, ("gsl", 31))

                n0c = sSY.n

                def dump22(e, n0=n0c):
                    e.dma_start(out=dbg[:, 0:4096], in_=vr[:].rearrange("p a c -> p (a c)")).then_inc(sSY.h, 16)
                    e.dma_start(out=dbg[:, 4096:8192], in_=gsil[:].rearrange("p a c -> p (a c)")).then_inc(sSY.h, 16)
                    e.wait_ge(sSY.h, n0 + 32)
                P.on("sync", dump22)
                sSY.n += 32
            if DEBUG_STAGE == 23:
                P.wait("sync", sDV, ("epat", 7))

                n0c = sSY.n

                def dump23(e, n0=n0c):
                    e.dma_start(out=dbg[:, 0:4096], in_=attno[:].rearrange("p a c -> p (a c)")).then_inc(sSY.h, 16)
                    e.wait_ge(sSY.h, n0 + 16)
                P.on("sync", dump23)
                sSY.n += 16
            if DEBUG_STAGE in (20, 21, 22, 23):
                for s_ in (sSY, sGP, sCC, sPE, sDV, sSC):
                    P.mark(("b2", s_.name), s_)
                blk2 = e2(nc.Block("blk2"))
                P.emit(blk2)
                return nc

            P.wait("gpsimd", sGP, ("a2aw", 3))

            def a2a(e):
                with allow_ici_all_to_all():
                    e.collective_compute("AllToAll", ALU.bypass,
                                         replica_groups=[list(range(NC))],
                                         ins=[a2ain[:].opt()], outs=[a2aout[:].opt()]).then_inc(sCC.h, 1)
            P.on("gpsimd", a2a)
            sCC.n += 1
            P.mark("a2a", sCC)

            if DEBUG_STAGE == 2:
                P.wait("sync", sCC, "a2a")
                base0 = sSY.n

                def dump2(e, base0=base0):
                    n = base0
                    bt_ = xlnT[:, 0, :].squeeze(1) if len(xlnT[:, 0, :].shape) == 3 else xlnT[:, 0, :]
                    bt16 = bt_[:, 0:2048].rearrange("p (tt c) -> p tt c", c=128)
                    bt8 = xlnT[0:8, 1, 0:128]
                    for r in range(NC):
                        e.dma_start(out=bt16,
                                    in_=a2aout[HR * r:HR * r + OWN, :].rearrange(
                                        "(tt p) c -> p tt c", p=128)).then_inc(sSY.h, 16)
                        e.dma_start(out=bt8,
                                    in_=a2aout[HR * r + OWN:HR * r + OWN + 8, :]).then_inc(sSY.h, 16)
                        n += 32
                        e.wait_ge(sSY.h, n)
                        e.dma_start(out=dbg[HR * r:HR * r + OWN, :].rearrange(
                                        "(tt p) c -> p tt c", p=128),
                                    in_=bt16).then_inc(sSY.h, 16)
                        e.dma_start(out=dbg[HR * r + OWN:HR * r + OWN + 8, :],
                                    in_=bt8).then_inc(sSY.h, 16)
                        n += 32
                        e.wait_ge(sSY.h, n)
                P.on("sync", dump2)
                sSY.n += 4 * NC * 16

            for s_ in (sSY, sGP, sCC, sPE, sDV, sSC, sX0, sX1, sW0, sW1, sW2, sW3, sS0, sS1, sO0, sO1):
                P.mark(("b2", s_.name), s_)
            blk2 = e2(nc.Block("blk2"))
            P.emit(blk2)
            M2 = dict(P.M)

        if DEBUG_STAGE == 2:
            return nc

        with ExitStack() as c3:
            e3 = c3.enter_context
            xrT = e3(nc.sbuf_tensor("xrT", [128, 8, HRP], BF16))
            invc = e3(nc.sbuf_tensor("invc", [128, 17], F32))
            invrow = e3(nc.sbuf_tensor("invrow", [1, HRP], F32))
            inv1b = e3(nc.sbuf_tensor("inv1b", [128, HRP], F32))
            c3a = ExitStack()
            e3a = c3a.enter_context
            attnf = e3a(nc.sbuf_tensor("attnf", [128, NTIL, E], BF16))
            xa3 = e3a(nc.sbuf_tensor("xa3", [128, 2, E], F32))
            x1b = e3a(nc.sbuf_tensor("x1b", [128, 2, E], F32))
            xc3 = e3a(nc.sbuf_tensor("xc3", [128, 2, E], F32))
            xln2 = e3a(nc.sbuf_tensor("xln2", [128, 2, E], F32))
            sq3 = e3a(nc.sbuf_tensor("sq3", [128, E], F32))
            xs3 = e3a(nc.sbuf_tensor("xs3", [128, E], F32))
            xi3 = e3a(nc.sbuf_tensor("xi3", [128, E], I32))
            xrb = e3a(nc.sbuf_tensor("xrb", [128, 2, E], BF16))
            pXT0 = e3a(nc.psum_tensor("pXT0", [128, 1024], BF16))
            pXT1 = e3a(nc.psum_tensor("pXT1", [128, 1024], BF16))
            pBCa = e3a(nc.psum_tensor("pBCa", [128, 512], F32))
            PXT = (pXT0, pXT1)

            P = Prog(M2)
            for eng_ in Prog.ENGINES:
                for s_ in (sSY, sGP, sPE, sDV, sSC, sX0, sX1, sW0, sW1, sW2, sW3, sS0, sS1, sO0, sO1):
                    P.wait(eng_, s_, ("b2", s_.name))

            def af16(e):
                e.memset(attnf[:, 16, :], 0.0).then_inc(sDV.h, 1)
            P.on("vector", af16)
            sDV.n += 1
            P.mark("af16", sDV)

            P.wait("gpsimd", sCC, "a2a")
            P.wait("gpsimd", sDV, "af16")

            def attl(e):
                for r in range(NC):
                    e.dma_start(
                        out=attnf[:, 0:16, 128 * r:128 * (r + 1)],
                        in_=a2aout[HR * r:HR * r + OWN, :].rearrange("(tt p) c -> p tt c", p=128)).then_inc(sGP.h, 16)
                    e.dma_start(out=attnf[0:8, 16, 128 * r:128 * (r + 1)],
                                in_=a2aout[HR * r + OWN:HR * r + OWN + 8, :]).then_inc(sGP.h, 16)
            P.on("gpsimd", attl)
            sGP.n += 16 * 16
            P.mark("attl", sGP)


            for t in range(NTIL):
                sl = XSL[t % 2]
                if t >= 2:
                    P.wait("sync", sDV, ("qmul", t - 2))

                def xl3(e, t=t):
                    e.dma_start(out=_sl3(xa3, t % 2, 0, E),
                                in_=xh[128 * t:128 * (t + 1), :]).then_inc(XSL[t % 2].h, 16)
                P.on("sync", xl3)
                sl.n += 16

                P.wait("vector", sl, sl.n)
                P.wait("vector", sGP, "attl")
                if t >= 2:
                    P.wait("vector", SSL[t % 2], ("x1st", t - 2))

                def x1a(e, t=t):
                    e.tensor_tensor(_sl3(x1b, t % 2, 0, E), _sl3(xa3, t % 2, 0, E),
                                    attnf[:, t, :], op=ALU.add).then_inc(sDV.h, 1)
                P.on("vector", x1a)
                sDV.n += 1
                P.mark(("x1", t), sDV)

                P.wait("sync", sDV, ("x1", t))

                def x1st(e, t=t):
                    e.dma_start(out=x1_d[128 * t:128 * (t + 1), :],
                                in_=_sl3(x1b, t % 2, 0, E)).then_inc(SSL[t % 2].h, 16)
                P.on("sync", x1st)
                SSL[t % 2].n += 16
                P.M[("x1st", t)] = SSL[t % 2].n

                co = 8 * (t % 2)
                P.wait("vector", sDV, ("x1", t))

                def l2a(e, t=t, co=co):
                    e.reduce_sum(cB[:, co:co + 1], _sl3(x1b, t % 2, 0, E), axis=AX).then_inc(sDV.h, 1)
                P.on("vector", l2a)
                sDV.n += 1
                P.mark(("l2sum", t), sDV)

                P.wait("scalar", sDV, ("l2sum", t))

                def l2b(e, co=co):
                    e.mul(out=cA[:, co:co + 1], in_=cB[:, co:co + 1], mul=1.0 / E).then_inc(sSC.h, 1)
                P.on("scalar", l2b)
                sSC.n += 1
                P.mark(("l2m", t), sSC)

                P.wait("vector", sSC, ("l2m", t))
                if t >= 2:
                    P.wait("vector", sSC, ("l2nrm", t - 2))

                def l2c(e, t=t, co=co):
                    e.tensor_scalar(out=_sl3(xc3, t % 2, 0, E), in0=_sl3(x1b, t % 2, 0, E),
                                    scalar1=cA[:, co:co + 1], scalar2=None,
                                    op0=ALU.subtract).then_inc(sDV.h, 1)
                P.on("vector", l2c)
                sDV.n += 1
                P.mark(("l2xc", t), sDV)

                P.wait("scalar", sDV, ("l2xc", t))

                acc_n = sSC.n + 1

                def l2d(e, t=t, co=co, acc_n=acc_n):
                    e.activation(out=sq3[:], in_=_sl3(xc3, t % 2, 0, E), func=AF.Square,
                                 accum_out=cA[:, co + 1:co + 2]).then_inc(sSC.h, 1)
                    e.wait_ge(sSC.h, acc_n)
                    e.activation(out=cA[:, co + 2:co + 3], in_=cA[:, co + 1:co + 2], func=AF.Sqrt,
                                 scale=1.0 / E, bias=EPS_LN).then_inc(sSC.h, 1)
                P.on("scalar", l2d)
                sSC.n += 2
                P.mark(("l2srt", t), sSC)

                P.wait("vector", sSC, ("l2srt", t))

                def l2e(e, co=co):
                    e.reciprocal(cB[:, co + 2:co + 3], cA[:, co + 2:co + 3]).then_inc(sDV.h, 1)
                P.on("vector", l2e)
                sDV.n += 1
                P.mark(("l2rstd", t), sDV)

                P.wait("scalar", sDV, ("l2rstd", t))
                if t >= 2:
                    P.wait("scalar", sDV, ("qmul", t - 2))

                def l2f(e, t=t, co=co):
                    e.activation(out=_sl3(xln2, t % 2, 0, E), in_=_sl3(xc3, t % 2, 0, E),
                                 func=AF.Copy, scale=cB[:, co + 2:co + 3]).then_inc(sSC.h, 1)
                P.on("scalar", l2f)
                sSC.n += 1
                P.mark(("l2nrm", t), sSC)

                P.wait("vector", sSC, ("l2nrm", t))

                def q1(e, t=t, co=co):
                    e.reduce_max(cB[:, co + 3:co + 4], _sl3(xln2, t % 2, 0, E), axis=AX,
                                 apply_absolute_value=True).then_inc(sDV.h, 1)
                P.on("vector", q1)
                sDV.n += 1
                P.mark(("qmx", t), sDV)

                P.wait("scalar", sDV, ("qmx", t))

                base_n = sSC.n

                def q2(e, t=t, co=co, base_n=base_n):
                    e.activation(out=cA[:, co + 3:co + 4], in_=cB[:, co + 3:co + 4], func=AF.Relu,
                                 bias=-1e-5).then_inc(sSC.h, 1)
                    e.wait_ge(sSC.h, base_n + 1)
                    e.add(out=cA[:, co + 3:co + 4], in_=cA[:, co + 3:co + 4],
                          add=1e-5).then_inc(sSC.h, 1)
                    e.wait_ge(sSC.h, base_n + 2)
                    e.mul(out=invc[:, t:t + 1], in_=cA[:, co + 3:co + 4],
                          mul=1.0 / 127.0).then_inc(sSC.h, 1)
                P.on("scalar", q2)
                sSC.n += 3
                P.mark(("qmc", t), sSC)

                P.wait("vector", sSC, ("qmc", t))

                def q3(e, co=co):
                    e.reciprocal(cB[:, co + 4:co + 5], cA[:, co + 3:co + 4]).then_inc(sDV.h, 1)
                P.on("vector", q3)
                sDV.n += 1
                P.mark(("qrec", t), sDV)

                P.wait("scalar", sDV, ("qrec", t))

                def q4(e, co=co):
                    e.mul(out=cA[:, co + 5:co + 6], in_=cB[:, co + 4:co + 5],
                          mul=127.0).then_inc(sSC.h, 1)
                P.on("scalar", q4)
                sSC.n += 1
                P.mark(("qs127", t), sSC)

                P.wait("vector", sSC, ("qs127", t))
                if t >= 2:
                    P.wait("vector", sPE, ("xtr", t - 2))

                acc_n = sDV.n

                def q5(e, t=t, co=co, acc_n=acc_n):
                    e.wait_ge(sDV.h, acc_n)
                    e.tensor_scalar(out=xs3[:], in0=_sl3(xln2, t % 2, 0, E),
                                    scalar1=cA[:, co + 5:co + 6], scalar2=None,
                                    op0=ALU.mult).then_inc(sDV.h, 1)
                    e.wait_ge(sDV.h, acc_n + 1)
                    e.tensor_scalar(out=xi3[:], in0=xs3[:], scalar1=-1.0, scalar2=None,
                                    op0=ALU.add).then_inc(sDV.h, 1)
                    e.wait_ge(sDV.h, acc_n + 2)
                    e.tensor_tensor(xs3[:], xs3[:], xi3[:], op=ALU.subtract).then_inc(sDV.h, 1)
                    e.wait_ge(sDV.h, acc_n + 3)
                    e.scalar_tensor_tensor(out=xi3[:], in0=xs3[:], scalar=1.5, in1=xi3[:],
                                           op0=ALU.is_gt, op1=ALU.add).then_inc(sDV.h, 1)
                    e.wait_ge(sDV.h, acc_n + 4)
                    e.scalar_tensor_tensor(out=_sl3(xrb, t % 2, 0, E), in0=xs3[:],
                                           scalar=0.5, in1=xi3[:],
                                           op0=ALU.is_ge, op1=ALU.add).then_inc(sDV.h, 1)
                P.on("vector", q5)
                sDV.n += 5
                P.mark(("qmul", t), sDV)

                P.wait("tensor", sDV, ("qmul", t))
                if t >= 2:
                    P.wait("tensor", sDV, ("xband", t - 2))

                def xtr(e, t=t):
                    pb = PXT[t % 2]
                    for ec in range(8):
                        d = e.transpose(pb[:, 128 * ec:128 * (ec + 1)],
                                        xrb[:, t % 2, 128 * ec:128 * (ec + 1)], identb_t[:])
                    d.then_inc(sPE.h, 1)
                P.on("tensor", xtr)
                sPE.n += 1
                P.mark(("xtr", t), sPE)

                P.wait("vector", sPE, ("xtr", t))

                def xband(e, t=t):
                    pb = PXT[t % 2]
                    e.tensor_copy(xrT[:, :, 128 * t:128 * (t + 1)],
                                  pb[:].rearrange("p (ec c) -> p ec c", c=128)).then_inc(sDV.h, 1)
                P.on("vector", xband)
                sDV.n += 1
                P.mark(("xband", t), sDV)

            grp = [(0, 4), (4, 8), (8, 12), (12, 16), (16, 17)]
            for gix, (a0, b0) in enumerate(grp):
                P.wait("tensor", sSC, ("qmc", b0 - 1))
                if gix >= 1:
                    P.wait("tensor", sDV, ("inve", gix - 1))

                def invtr(e, a0=a0, b0=b0):
                    for t in range(a0, b0):
                        d = e.transpose(pBCa[0:1, 128 * (t - a0):128 * (t - a0) + 128],
                                        invc[:, t:t + 1], identf_t[:])
                    d.then_inc(sPE.h, 1)
                P.on("tensor", invtr)
                sPE.n += 1
                P.mark(("invtr", gix), sPE)

                P.wait("vector", sPE, ("invtr", gix))

                def inve(e, a0=a0, b0=b0):
                    for t in range(a0, b0):
                        d = e.tensor_copy(invrow[0:1, 128 * t:128 * (t + 1)],
                                          pBCa[0:1, 128 * (t - a0):128 * (t - a0) + 128])
                    d.then_inc(sDV.h, 1)
                P.on("vector", inve)
                sDV.n += 1
                P.mark(("inve", gix), sDV)

            nch = (HRP + 511) // 512
            for cc in range(nch):
                w = min(512, HRP - 512 * cc)
                P.wait("tensor", sDV, ("inve", len(grp) - 1))
                if cc >= 1:
                    P.wait("tensor", sDV, ("bce", cc - 1))

                def bcm(e, cc=cc, w=w):
                    e.matmul(pBCa[:, 0:w], onesr_t[:], invrow[0:1, 512 * cc:512 * cc + w],
                             start=True, stop=True).then_inc(sPE.h, 1)
                P.on("tensor", bcm)
                sPE.n += 1
                P.mark(("bcm", cc), sPE)

                P.wait("vector", sPE, ("bcm", cc))

                def bce(e, cc=cc, w=w):
                    e.tensor_copy(inv1b[:, 512 * cc:512 * cc + w], pBCa[:, 0:w]).then_inc(sDV.h, 1)
                P.on("vector", bce)
                sDV.n += 1
                P.mark(("bce", cc), sDV)

            for s_ in (sSY, sGP, sCC, sPE, sDV, sSC, sX0, sX1, sW0, sW1, sW2, sW3, sS0, sS1, sO0, sO1):
                P.mark(("b3a", s_.name), s_)
            blk3a = e3a(nc.Block("blk3a"))
            P.emit(blk3a)
            M3a = dict(P.M)
            c3a.close()

            w1t = e3(nc.sbuf_tensor("w1t", [128, 2, 8, 128], BF16))
            hbuf = e3(nc.sbuf_tensor("hbuf", [128, 2, 514], F32))
            cv1 = e3(nc.sbuf_tensor("cv1", [128, 512], F32))
            cv2 = e3(nc.sbuf_tensor("cv2", [128, 512], F32))
            h2q = e3(nc.sbuf_tensor("h2q", [128, 32, 512], BF16))
            macc = e3(nc.sbuf_tensor("macc", [128, 512], F32))
            s2b = e3(nc.sbuf_tensor("s2b", [128, 512], F32))
            hs3 = e3(nc.sbuf_tensor("hs3", [128, 512], F32))
            hi3 = e3(nc.sbuf_tensor("hi3", [128, 512], I32))
            hs3g = e3(nc.sbuf_tensor("hs3g", [128, 512], F32))
            hi3g = e3(nc.sbuf_tensor("hi3g", [128, 512], I32))
            invc2s = e3(nc.sbuf_tensor("invc2s", [128, 4], F32))
            w2sb = e3(nc.sbuf_tensor("w2sb", [128, 32, E], BF16))
            x1rd = e3(nc.sbuf_tensor("x1rd", [128, 2, E], F32))
            obuf = e3(nc.sbuf_tensor("obuf", [128, 2, E], F32))
            pF10 = e3(nc.psum_tensor("pF10", [128, 512], F32))
            pF11 = e3(nc.psum_tensor("pF11", [128, 512], F32))
            pF20 = e3(nc.psum_tensor("pF20", [128, 512], F32))
            pF21 = e3(nc.psum_tensor("pF21", [128, 512], F32))
            pBC = e3(nc.psum_tensor("pBC", [128, 512], F32))
            PF1 = (pF10, pF11)
            PF2 = (pF20, pF21)

            P = Prog(M3a)
            for eng_ in Prog.ENGINES:
                for s_ in (sSY, sGP, sPE, sDV, sSC, sX0, sX1, sW0, sW1, sW2, sW3, sS0, sS1, sO0, sO1):
                    P.wait(eng_, s_, ("b3a", s_.name))

            def w2l(e):
                e.dma_start(out=w2sb[:],
                            in_=w2qT_d[:].rearrange("(ch p) c -> p ch c", p=128)).then_inc(sSY.h, 16)
            P.on("sync", w2l)
            sSY.n += 16
            P.mark("w2l", sSY)

            for q in range(4):
                hlo = 512 * q

                if q > 0:
                    P.wait("vector", sPE, ("mtr", 4 * q - 1))

                def mz(e):
                    e.memset(macc[:], 0.0).then_inc(sDV.h, 1)
                P.on("vector", mz)
                sDV.n += 1
                P.mark(("mz", q), sDV)

                for cht in range(32):
                    ci = 32 * q + cht
                    sl = WSL[ci % 2]
                    if ci >= 2:
                        P.wait("sync", sPE, ("f1", ci - 2))

                    def w1l(e, ci=ci, cht=cht):
                        e.dma_start(out=w1t[:, ci % 2, :, :],
                                    in_=w1qT_d[:, 128 * cht:128 * (cht + 1)].rearrange(
                                        "(ec p) c -> p ec c", p=128)).then_inc(WSL[ci % 2].h, 16)
                    P.on("sync", w1l)
                    sl.n += 16

                    P.wait("tensor", sl, sl.n)
                    P.wait("tensor", sDV, ("xband", NTIL - 1))
                    if ci == 0:
                        P.wait("tensor", sDV, ("bce", nch - 1))
                    elif cht == 0:
                        P.wait("tensor", sDV, ("s2e", 4 * q - 1))
                    if ci >= 2:
                        P.wait("tensor", sDV, ("f1e", ci - 2))
                    if ci >= 1:
                        P.wait("tensor", sDV, ("f1e", ci - 1))

                    def f1(e, ci=ci, hlo=hlo):
                        pp = PF1[ci % 2]
                        for ec in range(8):
                            d = e.matmul(pp[:], _sl3(w1t[:, ci % 2], ec, 0, 128),
                                         xrT[:, ec, hlo:hlo + 512],
                                         start=(ec == 0), stop=(ec == 7))
                        for ec in range(8):
                            d = e.matmul(pBC[:, 0:2], _sl3(w1t[:, ci % 2], ec, 0, 128),
                                         xrT[:, ec, hlo + 512:hlo + 514],
                                         start=(ec == 0), stop=(ec == 7))
                        d.then_inc(sPE.h, 1)
                    P.on("tensor", f1)
                    sPE.n += 1
                    P.mark(("f1", ci), sPE)

                    P.wait("vector", sPE, ("f1", ci))
                    P.wait("vector", sDV, ("mz", q))
                    if ci >= 2:
                        P.wait("vector", sSC, ("silu", ci - 2))

                    base_n = sDV.n
                    nmask = (1 if q == 0 else 0) + (1 if q == 3 else 0)

                    def f1e(e, q=q, ci=ci, hlo=hlo, base_n=base_n):
                        n = base_n
                        pp = PF1[ci % 2]
                        e.tensor_tensor(_sl3(hbuf, ci % 2, 0, 512), pp[:],
                                        inv1b[:, hlo:hlo + 512], op=ALU.mult).then_inc(sDV.h, 1)
                        e.tensor_tensor(_sl3(hbuf, ci % 2, 512, 514), pBC[:, 0:2],
                                        inv1b[:, hlo + 512:hlo + 514], op=ALU.mult).then_inc(sDV.h, 1)
                        n += 2
                        e.wait_ge(sDV.h, n)
                        if q == 0:
                            e.tensor_scalar(out=_sl3(hbuf, ci % 2, 0, 1),
                                            in0=_sl3(hbuf, ci % 2, 0, 1),
                                            scalar1=maskL_t[:], scalar2=None,
                                            op0=ALU.mult).then_inc(sDV.h, 1)
                            n += 1
                            e.wait_ge(sDV.h, n)
                        if q == 3:
                            e.tensor_scalar(out=_sl3(hbuf, ci % 2, 513, 514),
                                            in0=_sl3(hbuf, ci % 2, 513, 514),
                                            scalar1=maskR_t[:], scalar2=None,
                                            op0=ALU.mult).then_inc(sDV.h, 1)
                            n += 1
                            e.wait_ge(sDV.h, n)
                        ch = ci % 32
                        w0 = tapsp[:, 3 * ch:3 * ch + 1]
                        w1_ = tapsp[:, 3 * ch + 1:3 * ch + 2]
                        w2_ = tapsp[:, 3 * ch + 2:3 * ch + 3]
                        bcol = convb_t[:, ch:ch + 1]
                        e.scalar_tensor_tensor(out=cv1[:], in0=hbuf[:, ci % 2, 2:514], scalar=w2_,
                                               in1=bcol.broadcast_to([128, 512]),
                                               op0=ALU.mult, op1=ALU.add).then_inc(sDV.h, 1)
                        n += 1
                        e.wait_ge(sDV.h, n)
                        e.scalar_tensor_tensor(out=cv2[:], in0=hbuf[:, ci % 2, 1:513], scalar=w1_,
                                               in1=cv1[:], op0=ALU.mult, op1=ALU.add).then_inc(sDV.h, 1)
                        n += 1
                        e.wait_ge(sDV.h, n)
                        e.scalar_tensor_tensor(out=cv1[:], in0=hbuf[:, ci % 2, 0:512], scalar=w0,
                                               in1=cv2[:], op0=ALU.mult, op1=ALU.add).then_inc(sDV.h, 1)
                    P.on("vector", f1e)
                    sDV.n += 5 + nmask
                    P.mark(("f1e", ci), sDV)

                    P.wait("scalar", sDV, ("f1e", ci))
                    if q > 0 and cht == 0:
                        P.wait("scalar", sPE, ("f2", 4 * q - 1, 1))

                    def silu(e, cht=cht):
                        e.activation(out=h2q[:, cht, :], in_=cv1[:], func=AF.Silu).then_inc(sSC.h, 1)
                    P.on("scalar", silu)
                    sSC.n += 1
                    P.mark(("silu", ci), sSC)

                    P.wait("vector", sSC, ("silu", ci))

                    base_n = sDV.n

                    def mupd(e, cht=cht, base_n=base_n):
                        e.scalar_tensor_tensor(out=macc[:], in0=h2q[:, cht, :], scalar=1.0,
                                               in1=macc[:], op0=ALU.mult,
                                               op1=ALU.max).then_inc(sDV.h, 1)
                        e.wait_ge(sDV.h, base_n + 1)
                        e.scalar_tensor_tensor(out=macc[:], in0=h2q[:, cht, :], scalar=-1.0,
                                               in1=macc[:], op0=ALU.mult,
                                               op1=ALU.max).then_inc(sDV.h, 1)
                    P.on("vector", mupd)
                    sDV.n += 2
                    P.mark(("mupd", ci), sDV)

                for tt in range(4):
                    ot = 4 * q + tt
                    P.wait("tensor", sDV, ("mupd", 32 * q + 31))
                    P.wait("tensor", sDV, ("f1e", 32 * q + 31))
                    if ot >= 1:
                        P.wait("tensor", sDV, ("s2e", ot - 1))

                    def mtr(e, tt=tt):
                        e.transpose(pBC[:, 128:256], macc[:, 128 * tt:128 * (tt + 1)],
                                    identf_t[:]).then_inc(sPE.h, 1)
                    P.on("tensor", mtr)
                    sPE.n += 1
                    P.mark(("mtr", ot), sPE)

                    P.wait("vector", sPE, ("mtr", ot))

                    def mred(e):
                        e.reduce_max(cB[:, 20:21], pBC[:, 128:256], axis=AX).then_inc(sDV.h, 1)
                    P.on("vector", mred)
                    sDV.n += 1
                    P.mark(("mred", ot), sDV)

                    P.wait("scalar", sDV, ("mred", ot))
                    if q > 0 and tt == 0:
                        P.wait("scalar", sDV, ("oev", 4 * q - 1, 1))

                    base_n = sSC.n

                    def sc2(e, base_n=base_n):
                        e.activation(out=cA[:, 20:21], in_=cB[:, 20:21], func=AF.Relu,
                                     bias=-1e-5).then_inc(sSC.h, 1)
                        e.wait_ge(sSC.h, base_n + 1)
                        e.add(out=cA[:, 20:21], in_=cA[:, 20:21], add=1e-5).then_inc(sSC.h, 1)
                    P.on("scalar", sc2)
                    sSC.n += 2
                    P.mark(("mxc2", ot), sSC)

                    P.wait("vector", sSC, ("mxc2", ot))

                    def v2c(e):
                        e.tensor_scalar_mul(cB[:, 21:22], cA[:, 20:21], 1.0 / 127.0)
                        e.reciprocal(cB[:, 22:23], cA[:, 20:21]).then_inc(sDV.h, 1)
                    P.on("vector", v2c)
                    sDV.n += 1
                    P.mark(("v2c", ot), sDV)

                    P.wait("scalar", sDV, ("v2c", ot))
                    P.wait("scalar", sDV, "wm2")

                    def sc3(e, tt=tt):
                        e.activation(out=invc2s[:, tt:tt + 1], in_=m2w2c[:], func=AF.Copy,
                                     scale=cB[:, 21:22])
                        e.mul(out=cA[:, 21:22], in_=cB[:, 22:23], mul=127.0).then_inc(sSC.h, 1)
                    P.on("scalar", sc3)
                    sSC.n += 1
                    P.mark(("sc3", ot), sSC)

                    P.wait("tensor", sSC, ("sc3", ot))

                    def s2tr(e):
                        e.transpose(pBC[0:1, 256:384], cA[:, 21:22], identf_t[:]).then_inc(sPE.h, 1)
                    P.on("tensor", s2tr)
                    sPE.n += 1
                    P.mark(("s2tr", ot), sPE)

                    P.wait("vector", sPE, ("s2tr", ot))

                    def s2re(e):
                        e.tensor_copy(invrow[0:1, 0:128], pBC[0:1, 256:384]).then_inc(sDV.h, 1)
                    P.on("vector", s2re)
                    sDV.n += 1
                    P.mark(("s2r", ot), sDV)

                    P.wait("tensor", sDV, ("s2r", ot))

                    def s2bc(e):
                        e.matmul(pBC[:, 384:512], onesr_t[:], invrow[0:1, 0:128],
                                 start=True, stop=True).then_inc(sPE.h, 1)
                    P.on("tensor", s2bc)
                    sPE.n += 1
                    P.mark(("s2bc", ot), sPE)

                    P.wait("vector", sPE, ("s2bc", ot))

                    def s2e(e, tt=tt):
                        e.tensor_copy(s2b[:, 128 * tt:128 * (tt + 1)],
                                      pBC[:, 384:512]).then_inc(sDV.h, 1)
                    P.on("vector", s2e)
                    sDV.n += 1
                    P.mark(("s2e", ot), sDV)

                for cht in range(32):
                    ci = 32 * q + cht
                    eng = "vector"
                    sem = sDV
                    bufs = (hs3, hi3)
                    P.wait(eng, sDV, ("s2e", 4 * q + 3))

                    acc_n = sem.n

                    def hq(e, cht=cht, acc_n=acc_n, sem=sem, bufs=bufs):
                        hs_, hi_ = bufs
                        e.wait_ge(sem.h, acc_n)
                        e.tensor_tensor(hs_[:], h2q[:, cht, :], s2b[:],
                                        op=ALU.mult).then_inc(sem.h, 1)
                        e.wait_ge(sem.h, acc_n + 1)
                        e.tensor_scalar(out=hi_[:], in0=hs_[:], scalar1=-1.0, scalar2=None,
                                        op0=ALU.add).then_inc(sem.h, 1)
                        e.wait_ge(sem.h, acc_n + 2)
                        e.tensor_tensor(hs_[:], hs_[:], hi_[:], op=ALU.subtract).then_inc(sem.h, 1)
                        e.wait_ge(sem.h, acc_n + 3)
                        e.scalar_tensor_tensor(out=hi_[:], in0=hs_[:], scalar=1.5, in1=hi_[:],
                                               op0=ALU.is_gt, op1=ALU.add).then_inc(sem.h, 1)
                        e.wait_ge(sem.h, acc_n + 4)
                        e.scalar_tensor_tensor(out=h2q[:, cht, :], in0=hs_[:],
                                               scalar=0.5, in1=hi_[:],
                                               op0=ALU.is_ge, op1=ALU.add).then_inc(sem.h, 1)
                    P.on(eng, hq)
                    sem.n += 5
                    P.mark(("hq", ci), sem)

                for tt in range(4):
                    ot = 4 * q + tt
                    r0 = 1 + 512 * q + 128 * tt
                    if ot >= 2:
                        P.wait("sync", sDV, ("oev", ot - 2, 1))

                    def x1l(e, ot=ot, r0=r0):
                        e.dma_start(out=_sl3(x1rd, ot % 2, 0, E),
                                    in_=x1_d[r0:r0 + 128, :]).then_inc(XSL[ot % 2].h, 16)
                    P.on("sync", x1l)
                    XSL[ot % 2].n += 16
                    P.M[("x1l", ot)] = XSL[ot % 2].n

                    for eo in range(2):
                        gstep = 2 * ot + eo
                        P.wait("tensor", sDV, ("hq", 32 * q + 31))
                        P.wait("tensor", sSY, "w2l")
                        if gstep >= 2:
                            P.wait("tensor", sDV, ("oev", (gstep - 2) // 2, (gstep - 2) % 2))

                        def f2(e, q=q, tt=tt, eo=eo):
                            pp = PF2[eo]
                            for cht in range(32):
                                d = e.matmul(pp[:], h2q[:, cht, 128 * tt:128 * (tt + 1)],
                                             w2sb[:, cht, 512 * eo:512 * (eo + 1)],
                                             start=(cht == 0), stop=(cht == 31))
                            d.then_inc(sPE.h, 1)
                        P.on("tensor", f2)
                        sPE.n += 1
                        P.mark(("f2", ot, eo), sPE)

                        P.wait("vector", sPE, ("f2", ot, eo))
                        P.wait("vector", XSL[ot % 2], ("x1l", ot))
                        P.wait("vector", sSC, ("sc3", ot))
                        if ot >= 2 and eo == 0:
                            P.wait("vector", OSL[ot % 2], ("ost", ot - 2))

                        def oev(e, ot=ot, tt=tt, eo=eo):
                            pp = PF2[eo]
                            e.scalar_tensor_tensor(
                                out=obuf[:, ot % 2, 512 * eo:512 * (eo + 1)], in0=pp[:],
                                scalar=invc2s[:, tt:tt + 1],
                                in1=x1rd[:, ot % 2, 512 * eo:512 * (eo + 1)],
                                op0=ALU.mult, op1=ALU.add).then_inc(sDV.h, 1)
                        P.on("vector", oev)
                        sDV.n += 1
                        P.mark(("oev", ot, eo), sDV)

                    P.wait("sync", sDV, ("oev", ot, 1))

                    def ost(e, q=q, tt=tt, ot=ot):
                        e.dma_start(out=out[512 * q + 128 * tt:512 * q + 128 * (tt + 1), :],
                                    in_=_sl3(obuf, ot % 2, 0, E)).then_inc(OSL[ot % 2].h, 16)
                    P.on("sync", ost)
                    OSL[ot % 2].n += 16
                    P.M[("ost", ot)] = OSL[ot % 2].n

            P.wait("sync", sO0, ("ost", 14))
            P.wait("sync", sO1, ("ost", 15))

            blk3 = e3(nc.Block("blk3b"))
            P.emit(blk3)

    return nc


def kernel(L0_attn_x, L0_attn_Wq, L0_attn_Wk, L0_attn_Wv, L0_attn_Wg,
           L0_ffn_w1, L0_ffn_conv_w, L0_ffn_conv_b, L0_ffn_w2):
    xf = np.ascontiguousarray(np.asarray(L0_attn_x, dtype=np.float32).reshape(TOK, E))
    args = [np.ascontiguousarray(np.asarray(a, dtype=np.float32)) for a in
            (L0_attn_Wq, L0_attn_Wk, L0_attn_Wv, L0_attn_Wg, L0_ffn_w1,
             L0_ffn_conv_w, L0_ffn_conv_b, L0_ffn_w2)]
    in_maps = [_stage_inputs(c, xf, *args) for c in range(NC)]
    nc = build_program()
    res = run_bass_kernel_spmd(nc, in_maps, core_ids=list(range(NC)))
    if DEBUG_STAGE:
        return [r["dbg"] for r in res.results]
    return np.concatenate([res.results[c]["o"] for c in range(NC)], axis=0).reshape(B, S, E)
